# version 11
# baseline (speedup 1.0000x reference)
import numpy as np
import ml_dtypes
import concourse.bass as bass
import concourse.mybir as mybir
from concourse.bass_utils import run_bass_kernel_spmd

F32 = mybir.dt.float32
BF16 = mybir.dt.bfloat16
AF = mybir.ActivationFunctionType
OP = mybir.AluOpType
NPBF16 = ml_dtypes.bfloat16

H = 16
DH = 128
DV = 128
RR = 4
HID = 2048
NIC = HID // 128
EPS = 1e-5
CT = 16
INNER = 4
CMICRO = 4 * CT
NI = CT // INNER
WARM = 96
HALO = WARM + 3
SLABCH = 4
NEG87 = np.float32(-87.0)


def bfnp(x):
    return np.asarray(x, np.float32).astype(NPBF16)



ENGINES = ("tensor", "vector", "scalar", "gpsimd", "sync")
DMAQ = {"qsync": "sync", "qact": "scalar", "qpool": "gpsimd"}
NLANES = 12


class Op:
    __slots__ = ("idx", "engine", "emit", "deps", "dma_queue", "pos", "tag", "qpos", "waits")

    def __init__(self, idx, engine, emit, deps, dma_queue, tag):
        self.idx = idx
        self.engine = engine
        self.emit = emit
        self.deps = deps
        self.dma_queue = dma_queue
        self.pos = None
        self.tag = tag


class Tile:
    __slots__ = ("name", "wr", "readers")

    def __init__(self, name):
        self.name = name
        self.wr = None
        self.readers = []


class Plan:
    def __init__(self, nc):
        self.nc = nc
        self.ops = []

    def add(self, engine, emit, reads=(), writes=(), hard=(), dma_queue=None,
            tag=""):
        deps = {}
        idx = len(self.ops)
        for t in reads:
            if t.wr is not None:
                deps[t.wr] = max(deps.get(t.wr, 0), 0)
            t.readers.append(idx)
        for t in hard:
            if t.wr is not None:
                deps[t.wr] = 1
            t.readers.append(idx)
        for t in writes:
            if t.wr is not None:
                deps[t.wr] = max(deps.get(t.wr, 0), 0)
            for r in t.readers:
                if r != idx:
                    deps[r] = max(deps.get(r, 0), 0)
            t.wr = idx
            t.readers = []
        self.ops.append(Op(idx, engine, emit, deps, dma_queue, tag))
        return idx

    def finalize(self):
        streams = {e: [] for e in ENGINES}
        qpos = {q: 0 for q in DMAQ}
        cpos = {e: 0 for e in ENGINES}
        for op in self.ops:
            streams[op.engine].append(op)
            if op.dma_queue:
                op.qpos = qpos[op.dma_queue]
                qpos[op.dma_queue] += 1
                op.pos = None
            else:
                op.pos = cpos[op.engine]
                cpos[op.engine] += 1
        for op in self.ops:
            waits = {}
            for d, hardflag in op.deps.items():
                dop = self.ops[d]
                if dop.dma_queue:
                    key = ("q", dop.dma_queue, dop.qpos % NLANES)
                    val = 16 * (dop.qpos // NLANES + 1)
                elif (dop.engine == op.engine and not op.dma_queue
                      and not hardflag and op.engine == "tensor"):
                    continue
                else:
                    key = ("e", dop.engine)
                    val = dop.pos + 1
                if waits.get(key, 0) < val:
                    waits[key] = val
            if op.dma_queue:
                key = ("q", op.dma_queue, op.qpos % NLANES)
                val = 16 * (op.qpos // NLANES)
                if val > 0 and waits.get(key, 0) < val:
                    waits[key] = val
            op.waits = waits
        self.streams = streams



def host_consts():
    time = np.zeros(CMICRO, np.int64)
    tok = np.zeros(CMICRO, np.int64)
    for sp in range(CMICRO):
        I, r, p = sp // 16, (sp % 16) // 4, sp % 4
        time[sp] = 4 * (4 * I + p) + r
        tok[sp] = 4 * I + p
    iblk = tok // INNER
    smask = (time[:, None] > time[None, :]).astype(np.float32)
    imask = (time[:, None] >= time[None, :]).astype(np.float32)
    gapM = np.full(CMICRO, -1, np.int64)
    gapA = np.full(CMICRO, -1, np.int64)
    for I in range(NI):
        rows = [s for s in range(CMICRO) if iblk[s] == I]
        for p, s in enumerate(rows):
            gapM[s] = 32 * I + p
            gapA[s] = 32 * I + 16 + p
    mask_gapM = np.zeros((128, CMICRO), np.float32)
    mask_gapA = np.zeros((128, CMICRO), np.float32)
    for s in range(CMICRO):
        mask_gapM[gapM[s], :] = -smask[s, :]
        mask_gapA[gapA[s], :] = imask[s, :]
    ident_gapM = np.zeros((128, CMICRO), np.float32)
    for a in range(CMICRO):
        ident_gapM[gapM[a], a] = 1.0
    ppool_gap = np.zeros((128, CT), np.float32)
    for s in range(CMICRO):
        ppool_gap[gapA[s], tok[s]] = 1.0
    return {
        "c_maskM": bfnp(mask_gapM),
        "c_maskA": bfnp(mask_gapA),
        "c_identg": bfnp(ident_gapM),
        "c_ppool": bfnp(ppool_gap),
        "c_ident": bfnp(np.eye(128, dtype=np.float32)),
    }, gapM, gapA


def host_prepare(inputs, n_cores, T_SL):
    x = np.asarray(inputs["L0_attn_hidden_states"][0], np.float32)
    xt = np.ascontiguousarray(x.T)

    def t_(name):
        return np.ascontiguousarray(np.asarray(inputs[name], np.float32).T)

    consts, _, _ = host_consts()
    shared = {
        "wq": t_("L0_attn_q_proj_w"), "wk": t_("L0_attn_k_proj_w"),
        "wv": t_("L0_attn_v_proj_w"), "wb": t_("L0_attn_b_proj_w"),
        "wf1": t_("L0_attn_f1_w"), "wf2": t_("L0_attn_f2_w"),
        "wg1": t_("L0_attn_g1_w"), "wg2": t_("L0_attn_g2_w"),
        "wop": t_("L0_attn_o_proj_w"), "wff": t_("L0_ffn_weight"),
        "qcw": np.ascontiguousarray(np.asarray(inputs["L0_attn_q_conv_w"], np.float32)),
        "kcw": np.ascontiguousarray(np.asarray(inputs["L0_attn_k_conv_w"], np.float32)),
        "vcw": np.ascontiguousarray(np.asarray(inputs["L0_attn_v_conv_w"], np.float32)),
        "alog": np.asarray(inputs["L0_attn_A_log"], np.float32).reshape(H, 1),
        "dtb_hd": np.ascontiguousarray(
            np.asarray(inputs["L0_attn_dt_bias"], np.float32).reshape(H, DH)),
        "mlog": np.ascontiguousarray(np.asarray(inputs["L0_attn_micro_logits"], np.float32)),
        "onw": np.asarray(inputs["L0_attn_o_norm_w"], np.float32).reshape(DV, 1),
        "g2b": np.asarray(inputs["L0_attn_g2_b"], np.float32).reshape(HID, 1),
        "ffsc": np.asarray(inputs["L0_ffn_scale_weight"], np.float32).reshape(HID, 1),
        "ffb": np.asarray(inputs["L0_ffn_bias"], np.float32).reshape(HID, 1),
    }
    shared.update(consts)
    maps = []
    for ci in range(n_cores):
        t0 = ci * T_SL
        lo = t0 - HALO
        xs = np.zeros((HID, T_SL + HALO), np.float32)
        s = max(lo, 0)
        xs[:, s - lo:] = xt[:, s:t0 + T_SL]
        m = dict(shared)
        m["x_t"] = xs
        maps.append(m)
    return maps



def build_program(T_SL=1024, warm=WARM, debug=False, dbg_dump=()):
    TW = T_SL + warm
    TX = T_SL + warm + 3
    nch = TW // CT
    assert TW % CT == 0

    nc = bass.Bass("TRN2", target_bir_lowering=False, debug=debug)
    pl = Plan(nc)

    dram = {}
    def din(nm, shp, dt=F32):
        dram[nm] = nc.dram_tensor(nm, shp, dt, kind="ExternalInput").ap()
    din("x_t", [HID, TX])
    din("wq", [HID, HID]); din("wk", [HID, 4 * HID]); din("wv", [HID, 4 * HID])
    din("wb", [HID, 64]); din("wf1", [HID, 128]); din("wf2", [128, HID])
    din("wg1", [HID, 128]); din("wg2", [128, HID]); din("wop", [HID, HID])
    din("wff", [HID, HID])
    din("qcw", [HID, 4]); din("kcw", [4 * HID, 4]); din("vcw", [4 * HID, 4])
    din("alog", [H, 1]); din("dtb_hd", [H, DH]); din("mlog", [H, 4])
    din("onw", [DV, 1]); din("g2b", [HID, 1]); din("ffsc", [HID, 1]); din("ffb", [HID, 1])
    for nm in ("c_maskM", "c_maskA", "c_identg"):
        din(nm, [128, CMICRO], BF16)
    din("c_ppool", [128, CT], BF16)
    din("c_ident", [128, 128], BF16)
    dram["out"] = nc.dram_tensor("out", [HID, T_SL], F32, kind="ExternalOutput").ap()
    def dsc(nm, shp, dt):
        dram[nm] = nc.dram_tensor(nm, shp, dt).ap()
    dsc("ksp", [4 * HID, TW], BF16)
    dsc("vsph", [4 * HID, TW], BF16)
    dsc("vspl", [4 * HID, TW], BF16)
    dsc("qsp", [HID, TW], BF16)
    dsc("lsp", [HID, TW], F32)
    dsc("rowk", [64, TW], F32)
    dsc("lnb", [64, TW], F32)
    dsc("rowq", [16, TW], F32)
    dsc("gsp", [HID, T_SL], BF16)

    b = Builder(nc, pl, dram, T_SL, TW, TX, nch)
    b.dbg_dump = dbg_dump
    b.build()
    return nc, b


class Builder:
    def __init__(self, nc, pl, dram, T_SL, TW, TX, nch):
        self.nc, self.pl, self.dram = nc, pl, dram
        self.T_SL, self.TW, self.TX, self.nch = T_SL, TW, TX, nch
        self.warm = TW - T_SL
        self.cms = []
        self.tiles = {}
        self.psum_used = 0

    def sbuf(self, name, shp, dt):
        h = self.nc.sbuf_tensor(name, shp, dt)
        v = h.__enter__()
        self.cms.append(h)
        t = Tile(name)
        self.tiles[name] = (v, t)
        return v, t

    def psum(self, name, shp, dt=F32):
        h = self.nc.psum_tensor(name, shp, dt)
        v = h.__enter__()
        self.cms.append(h)
        t = Tile(name)
        self.tiles[name] = (v, t)
        return v, t

    def dtile(self, name):
        if name not in self.tiles:
            self.tiles[name] = (None, Tile(name))
        return self.tiles[name][1]

    def dma(self, queue, out_ap, in_ap, reads=(), writes=(), tag="", slow=False):
        eng = DMAQ[queue]
        def emit(e):
            if slow:
                return e.dma_start(out=out_ap, in_=in_ap,
                                   allow_slow_non_contiguous=True)
            return e.dma_start(out=out_ap, in_=in_ap)
        return self.pl.add(eng, emit, reads=reads, writes=writes,
                           dma_queue=queue, tag=tag)

    def op(self, engine, fn, reads=(), writes=(), hard=(), tag=""):
        return self.pl.add(engine, fn, reads=reads, writes=writes, hard=hard,
                           tag=tag)

    def build(self):
        self.alloc_all()
        self.stage0_consts()
        self.stage0_ln()
        self.stage1()
        self.stage2()
        self.stage3()
        self.emit_debug_dumps()
        self.pl.finalize()
        self.emit_blocks()
        for cm in reversed(self.cms):
            cm.__exit__(None, None, None)

    def alloc_all(self):
        TX, TW, T_SL = self.TX, self.TW, self.T_SL
        s = self.sbuf
        p = self.psum
        s("xn", [128, NIC * TX], BF16)
        s("xchunk", [128, 2 * 512], F32)
        s("statrow", [1, 1536], F32)
        s("ybuf", [128, TX + 8], BF16)
        s("wtf", [128, 2 * 2048], F32)
        s("wtb", [128, 2 * 2048], BF16)
        s("ktile", [128, 512], BF16)
        s("vhi", [128, 512], BF16)
        s("vlo", [128, 512], BF16)
        s("ftmp", [128, 512], F32)
        s("ftmp2", [128, 512], F32)
        s("sq", [128, 512], BF16)
        s("rowbf", [1, 1024], BF16)
        s("y1", [128, TX], BF16)
        s("yg1", [128, TX], BF16)
        s("convd", [128, 4 * 128], BF16)
        s("cwcol", [128, 4], F32)
        s("rowbuf", [64, 2048], F32)
        s("idsb", [128, 128], BF16)
        s("hd_misc", [16, 1792], F32)
        s("lnfT", [128, 16], F32)
        s("AbT", [128, 16], F32)
        s("dtbT", [128, 16], F32)
        s("wfillT", [128, 4 * 16], F32)
        s("fpowT", [128, 4 * 16], F32)
        s("PFT", [128, 16], F32)
        for i in range(8):
            p(f"pb{i}", [128, 512], F32)
        self.tiles["pbc"] = self.tiles["pb0"]
        self.tiles["pj0"] = self.tiles["pb1"]
        self.tiles["pj1"] = self.tiles["pb2"]
        self.tiles["pcv"] = self.tiles["pb3"]
        self.tiles["pss"] = self.tiles["pb4"]

    def stage0_consts(self):
        nc, d = self.nc, self.dram
        v = lambda n: self.tiles[n][0]
        t = lambda n: self.tiles[n][1]
        self.dma("qsync", v("idsb")[:, :], d["c_ident"], writes=[t("idsb")])
        hm, hmt = v("hd_misc"), t("hd_misc")
        ones, onest = self.sbuf("onescol", [128, 2], F32)
        def f(e):
            return e.memset(ones[:, :], 1.0)
        self.op("vector", f, writes=[onest])
        onesrow, onesrowt = self.sbuf("onesrow", [1, 128], BF16)
        def f(e):
            return e.memset(onesrow[:, :], 1.0)
        self.op("vector", f, writes=[onesrowt])
        onesrowf, onesrowft = self.sbuf("onesrowf", [1, 128], F32)
        def f(e):
            return e.memset(onesrowf[:, :], 1.0)
        self.op("vector", f, writes=[onesrowft])
        self.dma("qsync", hm[0:16, 129:133], d["mlog"], writes=[hmt])
        def f(e):
            return e.activation(hm[0:16, 134:138], hm[0:16, 129:133], AF.Exp)
        self.op("scalar", f, reads=[hmt], writes=[hmt])
        def f(e):
            return e.reduce_sum(hm[0:16, 138:139], hm[0:16, 134:138], mybir.AxisListType.X)
        self.op("vector", f, reads=[hmt], writes=[hmt])
        def f(e):
            return e.reciprocal(hm[0:16, 139:140], hm[0:16, 138:139])
        self.op("vector", f, reads=[hmt], writes=[hmt])
        def f(e):
            return e.tensor_scalar(hm[0:16, 134:138], hm[0:16, 134:138],
                                   hm[0:16, 139:140], None, OP.mult)
        self.op("vector", f, reads=[hmt], writes=[hmt], hard=[hmt], tag="wmix")
        m1, m1t = self.sbuf("misc1", [4, 512], F32)
        for r in range(4):
            self.dma("qpool", m1[0:1, 64 + 16 * r:80 + 16 * r],
                     hm[0:16, 134 + r:135 + r],
                     reads=[hmt], writes=[m1t], tag=f"wmixrow{r}")
        self.dma("qsync", m1[0:1, 16:32], d["alog"], writes=[m1t])
        def f(e):
            return e.activation(m1[0:1, 16:32], m1[0:1, 16:32], AF.Exp)
        self.op("scalar", f, reads=[m1t], writes=[m1t], tag="A")
        def f(e):
            return e.tensor_scalar(m1[0:1, 32:48], m1[0:1, 16:32], -1.0, None, OP.mult)
        self.op("vector", f, reads=[m1t], writes=[m1t])
        pbc, pbct = self.tiles["pbc"]
        def f(e):
            return e.matmul(pbc[:, 0:16], onesrowf[0:1, :], m1[0:1, 32:48],
                            start=True, stop=True)
        self.op("tensor", f, reads=[m1t, onesrowft], writes=[pbct])
        ab, abt = v("AbT"), t("AbT")
        def f(e):
            return e.tensor_copy(ab[:, 0:16], pbc[:, 0:16])
        self.op("vector", f, reads=[pbct], writes=[abt], tag="AbT")
        db, dbt = v("dtbT"), t("dtbT")
        self.dma("qsync", db[:, 0:16], d["dtb_hd"].transpose([1, 0]),
                 writes=[dbt], tag="dtbT", slow=True)
        lf, lft = v("lnfT"), t("lnfT")
        sc1, sc1t = self.sbuf("csc1", [128, 16], F32)
        def f(e):
            return e.activation(sc1[:, 0:16], db[:, 0:16], AF.Abs, bias=np.float32(-1e4))
        self.op("scalar", f, reads=[dbt], writes=[sc1t])
        def f(e):
            return e.activation(sc1[:, 0:16], sc1[:, 0:16], AF.Exp, scale=np.float32(-1.0))
        self.op("scalar", f, reads=[sc1t], writes=[sc1t])
        def f(e):
            return e.activation(sc1[:, 0:16], sc1[:, 0:16], AF.Ln, bias=np.float32(1.0))
        self.op("scalar", f, reads=[sc1t], writes=[sc1t])
        def f(e):
            return e.activation(lf[:, 0:16], db[:, 0:16], AF.Relu, bias=np.float32(-1e4))
        self.op("scalar", f, reads=[dbt], writes=[lft])
        def f(e):
            return e.tensor_add(lf[:, 0:16], lf[:, 0:16], sc1[:, 0:16])
        self.op("vector", f, reads=[lft, sc1t], writes=[lft])
        def f(e):
            return e.tensor_mul(lf[:, 0:16], lf[:, 0:16], ab[:, 0:16])
        self.op("vector", f, reads=[lft, abt], writes=[lft])
        def f(e):
            return e.tensor_scalar(lf[:, 0:16], lf[:, 0:16], -2.0, None, OP.max)
        self.op("vector", f, reads=[lft], writes=[lft], tag="lnfT")
        fp, fpt = v("fpowT"), t("fpowT")
        def f(e):
            return e.memset(fp[:, 0:16], 1.0)
        self.op("vector", f, writes=[fpt])
        def f(e):
            return e.activation(fp[:, 16:32], lf[:, 0:16], AF.Exp)
        self.op("scalar", f, reads=[lft], writes=[fpt], tag="fillT")
        for r in (2, 3):
            def f(e, r=r):
                return e.tensor_mul(fp[:, 16 * r:16 * r + 16],
                                    fp[:, 16 * (r - 1):16 * r], fp[:, 16:32])
            self.op("vector", f, reads=[fpt], writes=[fpt])
        wf, wft = v("wfillT"), t("wfillT")
        pf, pft = v("PFT"), t("PFT")
        for r in range(4):
            def f(e, r=r):
                return e.matmul(pbc[:, 32 + 16 * r:48 + 16 * r], onesrowf[0:1, :],
                                m1[0:1, 64 + 16 * r:80 + 16 * r], start=True, stop=True)
            self.op("tensor", f, reads=[m1t, onesrowft], writes=[pbct])
        for r in range(4):
            def f(e, r=r):
                return e.tensor_mul(wf[:, 16 * r:16 * r + 16],
                                    fp[:, 16 * r:16 * r + 16],
                                    pbc[:, 32 + 16 * r:48 + 16 * r])
            self.op("vector", f, reads=[fpt, pbct], writes=[wft])
        def f(e):
            return e.tensor_scalar(wf[:, 0:64], wf[:, 0:64], float(DH ** -0.5),
                                   None, OP.mult)
        self.op("vector", f, reads=[wft], writes=[wft])
        def f(e):
            return e.tensor_add(pf[:, 0:16], wf[:, 0:16], wf[:, 16:32])
        self.op("vector", f, reads=[wft], writes=[pft])
        def f(e):
            return e.tensor_add(pf[:, 0:16], pf[:, 0:16], wf[:, 32:48])
        self.op("vector", f, reads=[wft, pft], writes=[pft])
        def f(e):
            return e.tensor_add(pf[:, 0:16], pf[:, 0:16], wf[:, 48:64])
        self.op("vector", f, reads=[wft, pft], writes=[pft], tag="PFT")

    def stage0_ln(self):
        nc, d = self.nc, self.dram
        TX = self.TX
        v = lambda n: self.tiles[n][0]
        t = lambda n: self.tiles[n][1]
        xch, xcht = v("xchunk"), t("xchunk")
        pst, pstt = self.tiles["pb5"]
        ps2, ps2t = self.tiles["pb6"]
        sr, srt = self.tiles["statrow"]
        pbc, pbct = self.tiles["pbc"]
        xn, xnt = v("xn"), t("xn")
        ft, ftt = v("ftmp"), t("ftmp")
        idsb, idsbt = self.tiles["idsb"]
        ntile = (TX + 511) // 512
        ones, onest = self.tiles["onescol"]
        onesrow, onesrowt = self.tiles["onesrow"]
        for ti in range(ntile):
            c0 = ti * 512
            cw = min(512, TX - c0)
            for ic in range(NIC):
                buf = (ic % 2) * 512
                self.dma("qsync", xch[:, buf:buf + cw],
                         d["x_t"][ic * 128:(ic + 1) * 128, c0:c0 + cw],
                         writes=[xcht], tag=f"ln.x.{ti}.{ic}")
                def f(e, buf=buf, cw=cw, ic=ic):
                    return e.matmul(pst[0:1, 0:cw], ones[:, 0:1], xch[:, buf:buf + cw],
                             start=(ic == 0), stop=(ic == NIC - 1))
                self.op("tensor", f, reads=[xcht, onest], writes=[pstt])
                def f(e, buf=buf, cw=cw):
                    return e.tensor_mul(ft[:, 0:cw], xch[:, buf:buf + cw], xch[:, buf:buf + cw])
                self.op("vector", f, reads=[xcht], writes=[ftt])
                def f(e, cw=cw, ic=ic):
                    return e.matmul(ps2[0:1, 0:cw], ones[:, 1:2], ft[:, 0:cw],
                             start=(ic == 0), stop=(ic == NIC - 1))
                self.op("tensor", f, reads=[ftt, onest], writes=[ps2t])
            def f(e, cw=cw):
                return e.tensor_scalar(sr[0:1, 0:cw], pst[0:1, 0:cw],
                                float(1.0 / HID), None, OP.mult)
            self.op("vector", f, reads=[pstt], writes=[srt])
            def f(e, cw=cw):
                return e.tensor_mul(sr[0:1, 1024:1024 + cw], sr[0:1, 0:cw], sr[0:1, 0:cw])
            self.op("vector", f, reads=[srt], writes=[srt])
            def f(e, cw=cw):
                return e.scalar_tensor_tensor(sr[0:1, 1024:1024 + cw], ps2[0:1, 0:cw],
                                       float(1.0 / HID), sr[0:1, 1024:1024 + cw],
                                       OP.mult, OP.subtract)
            self.op("vector", f, reads=[ps2t, srt], writes=[srt], tag="var")
            def f(e, cw=cw):
                return e.activation(sr[0:1, 1024:1024 + cw], sr[0:1, 1024:1024 + cw],
                             AF.Ln, bias=np.float32(EPS))
            self.op("scalar", f, reads=[srt], writes=[srt])
            def f(e, cw=cw):
                return e.activation(sr[0:1, 512:512 + cw], sr[0:1, 1024:1024 + cw],
                             AF.Exp, scale=np.float32(-0.5))
            self.op("scalar", f, reads=[srt], writes=[srt], tag="rstd")
            rb, rbt = self.tiles["rowbf"]
            def f(e, cw=cw):
                return e.tensor_copy(rb[0:1, 0:cw], sr[0:1, 0:cw])
            self.op("vector", f, reads=[srt], writes=[rbt])
            def f(e, cw=cw):
                return e.tensor_copy(rb[0:1, 512:512 + cw], sr[0:1, 512:512 + cw])
            self.op("vector", f, reads=[srt], writes=[rbt])
            def f(e, cw=cw):
                return e.matmul(pbc[:, 0:cw], onesrow[0:1, :], rb[0:1, 0:cw],
                         start=True, stop=True)
            self.op("tensor", f, reads=[rbt, onesrowt], writes=[pbct], tag="bc.mu")
            pb2, pb2t = self.tiles["pj1"]
            def f(e, cw=cw):
                return e.matmul(pb2[:, 0:cw], onesrow[0:1, :], rb[0:1, 512:512 + cw],
                         start=True, stop=True)
            self.op("tensor", f, reads=[rbt, onesrowt], writes=[pb2t], tag="bc.rs")
            for ic in range(NIC):
                buf = (ic % 2) * 512
                self.dma("qsync", xch[:, buf:buf + cw],
                         d["x_t"][ic * 128:(ic + 1) * 128, c0:c0 + cw],
                         writes=[xcht], tag=f"ln.x2.{ti}.{ic}")
                def f(e, buf=buf, cw=cw):
                    return e.tensor_sub(ft[:, 0:cw], xch[:, buf:buf + cw], pbc[:, 0:cw])
                self.op("vector", f, reads=[xcht, pbct], writes=[ftt])
                def f(e, ic=ic, c0=c0, cw=cw):
                    return e.tensor_tensor(xn[:, ic * TX + c0: ic * TX + c0 + cw],
                                    ft[:, 0:cw], pb2[:, 0:cw], OP.mult)
                self.op("vector", f, reads=[ftt, pb2t], writes=[xnt], tag="xn")

    def stage1(self):
        nc, d = self.nc, self.dram
        TX, TW, T_SL = self.TX, self.TW, self.T_SL
        v = lambda n: self.tiles[n][0]
        t = lambda n: self.tiles[n][1]
        xn, xnt = v("xn"), t("xn")
        wtf, wtft = v("wtf"), t("wtf")
        wtb, wtbt = v("wtb"), t("wtb")
        ybuf, ybuft = v("ybuf"), t("ybuf")
        convd, convdt = v("convd"), t("convd")
        cwc, cwct = v("cwcol"), t("cwcol")
        idsb, idsbt = self.tiles["idsb"]
        onescolb, onescolbt = self.sbuf("onescolb", [128, 1], BF16)
        def f(e):
            return e.memset(onescolb[:, :], 1.0)
        self.op("vector", f, writes=[onescolbt])
        rtmp, rtmpt = self.sbuf("rtmp", [1, 1024], F32)
        g2bT, g2bTt = self.sbuf("g2bT", [128, 16], F32)
        self.dma("qsync", g2bT[:, 0:16],
                 d["g2b"].rearrange("(i p) o -> p (i o)", p=128),
                 writes=[g2bTt], slow=True, tag="g2bT")

        jobs = []
        for oc in range(16):
            jobs.append(("q", oc))
        for oc in range(64):
            jobs.append(("k", oc))
        for oc in range(64):
            jobs.append(("v", oc))
        jobs.append(("b", 0))
        jobs.append(("f1", 0))
        for oc in range(16):
            jobs.append(("f2", oc))
        jobs.append(("g1", 0))
        for oc in range(16):
            jobs.append(("g2", oc))

        W = {"q": ("wq", 128), "k": ("wk", 128), "v": ("wv", 128),
             "b": ("wb", 64), "f1": ("wf1", 128), "f2": ("wf2", 128),
             "g1": ("wg1", 128), "g2": ("wg2", 128)}

        WQRR = ("qsync", "qpool")
        def wload(jid, name, oc):
            wn, ww = W[name]
            buf = (jid % 2) * 2048
            nicj = 1 if name in ("f2", "g2") else NIC
            src = d[wn].rearrange("(i p) o -> p i o", p=128)[:, :, oc * ww:(oc + 1) * ww]
            self.dma(WQRR[jid % 2], wtf[:, buf:buf + nicj * ww].rearrange(
                "p (i o) -> p i o", i=nicj), src, writes=[wtft],
                tag=f"wl.{name}{oc}")
            def f(e, buf=buf, n=nicj * ww):
                return e.tensor_copy(wtb[:, buf:buf + n], wtf[:, buf:buf + n])
            self.op("gpsimd", f, reads=[wtft], writes=[wtbt], tag=f"wc.{name}{oc}")
            return buf, nicj, ww

        pend = {}
        pend[0] = wload(0, *jobs[0])
        for jid, (name, oc) in enumerate(jobs):
            if jid + 1 < len(jobs):
                pend[jid + 1] = wload(jid + 1, *jobs[jid + 1])
            buf, nicj, ww = pend.pop(jid)
            shift = 0 if name in ("q", "k", "v", "f1", "g1") else 3
            isconv = name in ("q", "k", "v")
            pj, pjt = self.tiles["pj0" if jid % 2 == 0 else "pj1"]
            if name in ("f2",):
                mov, movt = v("y1"), t("y1")
            elif name in ("g2",):
                mov, movt = v("yg1"), t("yg1")
            else:
                mov, movt = xn, xnt
            if name == "g2":
                tiles = [(self.warm + 3 + i, min(512, TX - (self.warm + 3 + i)))
                         for i in range(0, T_SL, 512)]
            else:
                tiles = [(shift + i, min(512, TX - shift - i))
                         for i in range(0, TX - shift, 512)]
            for ti, (c0, cw) in enumerate(tiles):
                for ic in range(nicj):
                    if name in ("f2", "g2"):
                        mv = mov[:, c0:c0 + cw]
                    else:
                        mv = mov[:, ic * TX + c0:ic * TX + c0 + cw]
                    def f(e, buf=buf, ic=ic, ww=ww, mv=mv, cw=cw, nicj=nicj, pj=pj):
                        return e.matmul(pj[0:ww, 0:cw],
                                        wtb[:, buf + ic * ww:buf + (ic + 1) * ww],
                                        mv, start=(ic == 0), stop=(ic == nicj - 1))
                    self.op("tensor", f, reads=[wtbt, movt], writes=[pjt],
                            tag=f"mm.{name}{oc}.{ti}")
                if isconv:
                    def f(e, c0=c0, cw=cw, pj=pj):
                        return e.tensor_copy(ybuf[:, c0:c0 + cw], pj[:, 0:cw])
                    self.op("vector", f, reads=[pjt], writes=[ybuft])
                elif name == "f1":
                    y1v, y1t = v("y1"), t("y1")
                    def f(e, c0=c0, cw=cw, pj=pj):
                        return e.tensor_copy(y1v[:, c0:c0 + cw], pj[:, 0:cw])
                    self.op("vector", f, reads=[pjt], writes=[y1t])
                elif name == "g1":
                    yg, ygt = v("yg1"), t("yg1")
                    def f(e, c0=c0, cw=cw, pj=pj):
                        return e.tensor_copy(yg[:, c0:c0 + cw], pj[:, 0:cw])
                    self.op("vector", f, reads=[pjt], writes=[ygt])
                elif name == "b":
                    self.ep_beta(pj, pjt, c0 - 3, cw)
                elif name == "f2":
                    self.ep_L(pj, pjt, oc, c0 - 3, cw)
                elif name == "g2":
                    self.ep_gate(pj, pjt, oc, c0 - (self.warm + 3), cw, g2bT, g2bTt)
            if isconv:
                self.ep_conv(name, oc)

    def ep_conv(self, name, oc):
        nc, d = self.nc, self.dram
        TW, TX = self.TW, self.TX
        v = lambda n: self.tiles[n][0]
        t = lambda n: self.tiles[n][1]
        ybuf, ybuft = v("ybuf"), t("ybuf")
        convd, convdt = v("convd"), t("convd")
        cwc, cwct = v("cwcol"), t("cwcol")
        idsb, idsbt = self.tiles["idsb"]
        pcv, pcvt = self.tiles["pcv"]
        pss, psst = self.tiles["pss"]
        ft, ftt = v("ftmp"), t("ftmp")
        ft2, ft2t = v("ftmp2"), t("ftmp2")
        kt, ktt = v("ktile"), t("ktile")
        sq, sqt = v("sq"), t("sq")
        rt, rtt = v("rtmp"), t("rtmp")
        onescolb, onescolbt = self.tiles["onescolb"]
        cwn = {"q": "qcw", "k": "kcw", "v": "vcw"}[name]
        self.dma("qsync", cwc[:, 0:4], d[cwn][oc * 128:(oc + 1) * 128, :],
                 writes=[cwct], tag=f"cw.{name}{oc}")
        for i in range(4):
            def f(e, i=i):
                return e.tensor_scalar(convd[:, i * 128:(i + 1) * 128],
                                       idsb[:, :], cwc[:, i:i + 1], None, OP.mult)
            self.op("vector", f, reads=[idsbt], writes=[convdt], hard=[cwct],
                    tag=f"diag.{name}{oc}.{i}")
        for tt0 in range(0, TW, 512):
            cwt = min(512, TW - tt0)
            for i in range(4):
                def f(e, tt0=tt0, i=i, cwt=cwt):
                    return e.matmul(pcv[:, 0:cwt], convd[:, i * 128:(i + 1) * 128],
                                    ybuf[:, tt0 + i:tt0 + i + cwt],
                                    start=(i == 0), stop=(i == 3))
                self.op("tensor", f, reads=[convdt, ybuft], writes=[pcvt],
                        tag=f"cv.{name}{oc}.{tt0}")
            def f(e, cwt=cwt):
                return e.activation(ft[:, 0:cwt], pcv[:, 0:cwt], AF.Silu)
            self.op("scalar", f, reads=[pcvt], writes=[ftt], tag="silu")
            def f(e, cwt=cwt):
                return e.tensor_copy(kt[:, 0:cwt], ft[:, 0:cwt])
            self.op("vector", f, reads=[ftt], writes=[ktt])
            if name == "v":
                vh, vht = v("vhi"), t("vhi")
                vl, vlt = v("vlo"), t("vlo")
                def f(e, cwt=cwt):
                    return e.tensor_copy(vh[:, 0:cwt], ft[:, 0:cwt])
                self.op("vector", f, reads=[ftt], writes=[vht])
                def f(e, cwt=cwt):
                    return e.scalar_tensor_tensor(vl[:, 0:cwt], vh[:, 0:cwt], -1.0,
                                                  ft[:, 0:cwt], OP.mult, OP.add)
                self.op("vector", f, reads=[vht, ftt], writes=[vlt])
                self.dma("qact", d["vsph"][oc * 128:(oc + 1) * 128, tt0:tt0 + cwt],
                         vh[:, 0:cwt], reads=[vht], tag=f"sp.vh{oc}")
                self.dma("qact", d["vspl"][oc * 128:(oc + 1) * 128, tt0:tt0 + cwt],
                         vl[:, 0:cwt], reads=[vlt], tag=f"sp.vl{oc}")
            else:
                dst = "qsp" if name == "q" else "ksp"
                self.dma("qact", d[dst][oc * 128:(oc + 1) * 128, tt0:tt0 + cwt],
                         kt[:, 0:cwt], reads=[ktt], tag=f"sp.{name}{oc}")
                def f(e, cwt=cwt):
                    return e.tensor_mul(sq[:, 0:cwt], kt[:, 0:cwt], kt[:, 0:cwt])
                self.op("vector", f, reads=[ktt], writes=[sqt])
                def f(e, cwt=cwt):
                    return e.matmul(pss[0:1, 0:cwt], onescolb[:, 0:1], sq[:, 0:cwt],
                                    start=True, stop=True)
                self.op("tensor", f, reads=[sqt, onescolbt], writes=[psst])
                def f(e, cwt=cwt):
                    return e.activation(rt[0:1, 0:cwt], pss[0:1, 0:cwt], AF.Ln,
                                        bias=np.float32(1e-6))
                self.op("scalar", f, reads=[psst], writes=[rtt])
                def f(e, cwt=cwt):
                    return e.tensor_scalar(rt[0:1, 512:512 + cwt], rt[0:1, 0:cwt],
                                           -0.5, None, OP.mult)
                self.op("vector", f, reads=[rtt], writes=[rtt])
                rowd = "rowq" if name == "q" else "rowk"
                self.dma("qact", d[rowd][oc:oc + 1, tt0:tt0 + cwt],
                         rt[0:1, 512:512 + cwt], reads=[rtt], tag=f"row.{name}{oc}")

    def ep_beta(self, pj, pjt, tt0, cw):
        d = self.dram
        v = lambda n: self.tiles[n][0]
        t = lambda n: self.tiles[n][1]
        ft, ftt = v("ftmp"), t("ftmp")
        ft2, ft2t = v("ftmp2"), t("ftmp2")
        def f(e):
            return e.activation(ft[0:64, 0:cw], pj[0:64, 0:cw], AF.Relu,
                                scale=np.float32(-1.0))
        self.op("scalar", f, reads=[pjt], writes=[ftt])
        def f(e):
            return e.activation(ft2[0:64, 0:cw], pj[0:64, 0:cw], AF.Abs)
        self.op("scalar", f, reads=[pjt], writes=[ft2t])
        def f(e):
            return e.activation(ft2[0:64, 0:cw], ft2[0:64, 0:cw], AF.Exp,
                                scale=np.float32(-1.0))
        self.op("scalar", f, reads=[ft2t], writes=[ft2t])
        def f(e):
            return e.activation(ft2[0:64, 0:cw], ft2[0:64, 0:cw], AF.Ln,
                                bias=np.float32(1.0))
        self.op("scalar", f, reads=[ft2t], writes=[ft2t])
        def f(e):
            return e.scalar_tensor_tensor(ft[0:64, 0:cw], ft[0:64, 0:cw], -1.0,
                                          ft2[0:64, 0:cw], OP.mult, OP.subtract)
        self.op("vector", f, reads=[ftt, ft2t], writes=[ftt], tag="lnb")
        self.dma("qact", d["lnb"][0:64, tt0:tt0 + cw], ft[0:64, 0:cw],
                 reads=[ftt], tag="sp.lnb")

    def ep_L(self, pj, pjt, h, tt0, cw):
        d = self.dram
        v = lambda n: self.tiles[n][0]
        t = lambda n: self.tiles[n][1]
        ft, ftt = v("ftmp"), t("ftmp")
        ft2, ft2t = v("ftmp2"), t("ftmp2")
        db, dbt = v("dtbT"), t("dtbT")
        ab, abt = v("AbT"), t("AbT")
        def f(e):
            return e.activation(ft[:, 0:cw], pj[:, 0:cw], AF.Relu,
                                bias=db[:, h:h + 1])
        self.op("scalar", f, reads=[pjt], writes=[ftt], hard=[dbt])
        def f(e):
            return e.activation(ft2[:, 0:cw], pj[:, 0:cw], AF.Abs,
                                bias=db[:, h:h + 1])
        self.op("scalar", f, reads=[pjt], writes=[ft2t], hard=[dbt])
        def f(e):
            return e.activation(ft2[:, 0:cw], ft2[:, 0:cw], AF.Exp,
                                scale=np.float32(-1.0))
        self.op("scalar", f, reads=[ft2t], writes=[ft2t])
        def f(e):
            return e.activation(ft2[:, 0:cw], ft2[:, 0:cw], AF.Ln,
                                bias=np.float32(1.0))
        self.op("scalar", f, reads=[ft2t], writes=[ft2t])
        def f(e):
            return e.tensor_add(ft[:, 0:cw], ft[:, 0:cw], ft2[:, 0:cw])
        self.op("vector", f, reads=[ftt, ft2t], writes=[ftt])
        def f(e):
            return e.tensor_scalar(ft[:, 0:cw], ft[:, 0:cw], ab[:, h:h + 1],
                                   -17.0, OP.mult, OP.max)
        self.op("vector", f, reads=[ftt], writes=[ftt], hard=[abt], tag="L")
        self.dma("qact", d["lsp"][h * 128:(h + 1) * 128, tt0:tt0 + cw],
                 ft[:, 0:cw], reads=[ftt], tag=f"sp.L{h}")

    def ep_gate(self, pj, pjt, oc, c0out, cw, g2bT, g2bTt):
        d = self.dram
        v = lambda n: self.tiles[n][0]
        t = lambda n: self.tiles[n][1]
        vh, vht = v("vhi"), t("vhi")
        def f(e):
            return e.tensor_scalar(vh[:, 0:cw], pj[:, 0:cw],
                                   g2bT[:, oc:oc + 1], None, OP.add)
        self.op("vector", f, reads=[pjt], writes=[vht], hard=[g2bTt], tag="gate")
        self.dma("qact", d["gsp"][oc * 128:(oc + 1) * 128, c0out:c0out + cw],
                 vh[:, 0:cw], reads=[vht], tag=f"sp.g{oc}")

    def stage2(self):
        nc, d = self.nc, self.dram
        TW, T_SL = self.TW, self.T_SL
        nch = TW // CT
        v = lambda n: self.tiles[n][0]
        t = lambda n: self.tiles[n][1]
        s = self.sbuf
        NG = 4
        Sf, Sft = s("Sf", [128, H * 128], F32)
        Sb, Sbt = s("Sb", [128, H * 128], BF16)
        ob, obt = s("obuf", [128, H * T_SL], BF16)
        self.sfT = [Tile(f"Sf{g}") for g in range(4)]
        self.sbT = [Tile(f"Sb{g}") for g in range(4)]
        self.poT = [Tile("pO0"), Tile("pO1")]
        def f(e):
            return e.memset(Sf[:, :], 0.0)
        self.op("vector", f, writes=self.sfT)
        def f(e):
            return e.memset(Sb[:, :], 0.0)
        self.op("vector", f, writes=self.sbT)
        self.sl = {}
        for j in range(NG):
            for nm, wmul, dt_ in [("kh", 4, BF16), ("vh", 4, BF16), ("vl", 4, BF16),
                                  ("vhb", 4, BF16), ("vlb", 4, BF16),
                                  ("KQ", 8, BF16), ("WH", 4, BF16), ("KG", 4, BF16),
                                  ("WG", 4, BF16),
                                  ("qs", 1, BF16), ("Ls", 1, F32), ("Ct", 1, None),
                                  ("G0", 1, F32), ("Ga", 1, F32), ("G0r", 1, F32),
                                  ("QB", 1, BF16), ("dex", 1, None),
                                  ("rows", 1, None), ("LAM", 1, F32),
                                  ("rln", 1, None)]:
                if nm == "rows":
                    self.sl[(nm, j)] = s(f"sl_{nm}{j}", [1, 9 * SLABCH * CT], F32)
                elif nm == "Ct":
                    self.sl[(nm, j)] = s(f"sl_{nm}{j}", [128, SLABCH * CT + 1], F32)
                elif nm == "dex":
                    self.sl[(nm, j)] = s(f"sl_{nm}{j}", [128, 16 * SLABCH], F32)
                elif nm == "rln":
                    self.sl[(nm, j)] = s(f"sl_{nm}{j}", [128, 4], F32)
                elif nm == "LAM":
                    for st in range(2):
                        self.sl[(nm, j, st)] = s(f"sl_{nm}{j}_{st}", [128, 16], F32)
                elif nm == "QB":
                    for st in range(2):
                        self.sl[(nm, j, st)] = s(f"sl_{nm}{j}_{st}",
                                                 [128, wmul * SLABCH * CT], dt_)
                else:
                    self.sl[(nm, j)] = s(f"sl_{nm}{j}", [128, wmul * SLABCH * CT], dt_)
        for nm, shp, dt_ in [("arg", [128, SLABCH * CMICRO], F32),
                             ("att", [128, SLABCH * CT], F32),
                             ("att2", [128, SLABCH * CT], F32),
                             ("anch", [128, 4], F32),
                             ("dexp", [128, 16], F32),
                             ("Wv", [128, 1024], BF16),
                             ("Nd", [128, 2048], BF16),
                             ("Yp", [128, 512], BF16),
                             ("Uhi", [128, 256], BF16), ("Ulo", [128, 256], BF16),
                             ("UhiU", [128, 256], BF16), ("UloU", [128, 256], BF16),
                             ("WgT", [128, 256], BF16), ("ApT", [128, 64], BF16),
                             ("Ngap", [128, 256], BF16), ("Agap", [128, 256], BF16),
                             ("mM", [128, CMICRO], BF16), ("mA", [128, CMICRO], BF16),
                             ("idg", [128, CMICRO], BF16), ("ppool", [128, CT], BF16)]:
            s(nm, shp, dt_)
        self.dma("qsync", v("mM")[:, :], d["c_maskM"], writes=[t("mM")])
        self.dma("qsync", v("mA")[:, :], d["c_maskA"], writes=[t("mA")])
        self.dma("qsync", v("idg")[:, :], d["c_identg"], writes=[t("idg")])
        self.dma("qsync", v("ppool")[:, :], d["c_ppool"], writes=[t("ppool")])
        wtfB = v("wtf").bitcast(BF16)
        self.s2buf = {}
        off = 0
        for nm, shp in [("Nd", [128, 2048]), ("Wv", [128, 1024]),
                        ("Ngap", [128, 256]), ("Agap", [128, 256]),
                        ("WgT", [128, 256]), ("ApT", [128, 64]),
                        ("Yp", [128, 512]), ("Uhi", [128, 256]),
                        ("Ulo", [128, 256]), ("UhiU", [128, 256]),
                        ("UloU", [128, 256])]:
            t1 = Tile(nm + "B1")
            ap1 = wtfB[0:shp[0], off:off + shp[1]]
            off += shp[1]
            self.s2buf[nm] = [(v(nm), t(nm)), (ap1, t1)]
        self.s2b1t = [self.s2buf[nm][1][1]
                      for nm in ("Nd", "Wv", "Ngap", "Agap", "WgT", "ApT",
                                 "Yp", "Uhi", "Ulo", "UhiU", "UloU")]
        ftm = v("ftmp")
        def f(e):
            return e.memset(ftm[0:1, 0:1], 0.0)
        self.op("vector", f, writes=[t("wtf")] + self.s2b1t + [t("ftmp")],
                tag="fenceA")
        pTz = self.tiles["pb2"][0]
        def f(e):
            return e.memset(pTz[:, 0:512], 0.0)
        self.op("vector", f, writes=[self.tiles["pb2"][1]])
        pAz = self.tiles["pb1"][0]
        def f(e):
            return e.memset(pAz[:, 256:320], 0.0)
        self.op("vector", f, writes=[self.tiles["pb1"][1]])
        for nmz in ("UhiU", "UloU"):
            for bz in range(2):
                vz, tz = self.s2buf[nmz][bz]
                def f(e, vz=vz):
                    return e.memset(vz[0:64, 0:256], 0.0)
                self.op("vector", f, writes=[tz])
        for nmz in ("Uhi", "Ulo"):
            for bz in range(2):
                vz, tz = self.s2buf[nmz][bz]
                def f(e, vz=vz):
                    return e.memset(vz[64:128, 0:256], 0.0)
                self.op("vector", f, writes=[tz])
        blocks = []
        for slab in range((nch + SLABCH - 1) // SLABCH):
            m0 = slab * SLABCH
            nm_ = min(SLABCH, nch - m0)
            for hg in range(H // NG):
                blocks.append((m0, nm_, hg))

        def emit_rounds(rounds):
            for rnd in rounds:
                for th in rnd:
                    th()

        def loads_preps(bi):
            bm0, bnm, bhg = blocks[bi]
            for j in range(NG):
                h = bhg * NG + j
                self.s2_load_slab(h, j, bm0, bnm)
                self.s2_prep(h, j, bm0, bnm, bi % 2)

        loads_preps(0)
        for bi, (m0, nm_, hg) in enumerate(blocks):
            pend = []
            for mi in range(nm_):
                A, Bh, Bt = self.s2_chunk_phases(hg, m0 + mi, mi, nm_,
                                                 (m0 + mi) % 2, bi % 2)
                na, nb = len(A), len(pend)
                for i in range(max(na, nb)):
                    if i < na:
                        for th in A[i]:
                            th()
                    if i < nb:
                        for th in pend[i]:
                            th()
                if mi < nm_ - 1:
                    pend = Bh + Bt
                else:
                    emit_rounds(Bh)
                    if bi + 1 < len(blocks):
                        loads_preps(bi + 1)
                    emit_rounds(Bt)

    def s2_load_slab(self, h, j, m0, nm_):
        d = self.dram
        t0 = m0 * CT
        tw = nm_ * CT
        q2 = "qsync" if h % 2 == 0 else "qpool"
        def sv(nm):
            return self.sl[(nm, j)]
        for nm, src in [("kh", "ksp"), ("vh", "vsph"), ("vl", "vspl")]:
            vv, tt = sv(nm)
            ap = d[src][h * 512:(h + 1) * 512, t0:t0 + tw]
            ap = ap.rearrange("(r d) t -> d r t", d=128)
            out = vv[:, 0:4 * tw].rearrange("d (r t) -> d r t", r=4)
            self.dma(q2, out, ap, writes=[tt], tag=f"sl.{nm}{h}")
        vv, tt = sv("qs")
        self.dma(q2, vv[:, 0:tw], d["qsp"][h * 128:(h + 1) * 128, t0:t0 + tw],
                 writes=[tt], tag=f"sl.q{h}")
        vv, tt = sv("Ls")
        self.dma(q2, vv[:, 0:tw], d["lsp"][h * 128:(h + 1) * 128, t0:t0 + tw],
                 writes=[tt], tag=f"sl.L{h}")
        vv, tt = sv("rows")
        W8 = SLABCH * CT
        self.dma("qsync", vv[0:1, 0:4 * tw].rearrange("o (r t) -> o r t", r=4),
                 d["rowk"][4 * h:4 * h + 4, t0:t0 + tw].unsqueeze(0),
                 writes=[tt], tag=f"sl.rk{h}")
        self.dma("qsync", vv[0:1, 4 * W8:4 * W8 + 4 * tw].rearrange("o (r t) -> o r t", r=4),
                 d["lnb"][4 * h:4 * h + 4, t0:t0 + tw].unsqueeze(0),
                 writes=[tt], tag=f"sl.lb{h}")
        self.dma("qsync", vv[0:1, 8 * W8:8 * W8 + tw],
                 d["rowq"][h:h + 1, t0:t0 + tw], writes=[tt], tag=f"sl.rq{h}")

    def s2_prep(self, h, j, m0, nm_, set_):
        nc = self.nc
        v = lambda n: self.tiles[n][0]
        t = lambda n: self.tiles[n][1]
        tw = nm_ * CT
        mw = 4 * tw
        W8 = SLABCH * CT
        def sv(nm):
            return self.sl[(nm, j)]
        kh, kht = sv("kh"); vhh, vht = sv("vh"); vll, vlt = sv("vl")
        qs, qst = sv("qs"); Ls, Lst = sv("Ls"); Ct, Ctt = sv("Ct")
        G0, G0t = sv("G0"); Ga, Gat = sv("Ga"); rows, rowst = sv("rows")
        rln, rlnt = sv("rln"); LAM, LAMt = self.sl[("LAM", j, set_)]
        KQ, KQt = sv("KQ"); WH, WHt = sv("WH"); KG, KGt = sv("KG")
        WG, WGt = sv("WG"); QB, QBt = self.sl[("QB", j, set_)]
        dx, dxt = sv("dex")
        art, artt = v("arg"), t("arg")
        att, attt = v("att"), t("att")
        at2, at2t = v("att2"), t("att2")
        lf, lft = v("lnfT"), t("lnfT")
        pfT, pfTt = v("PFT"), t("PFT")
        wfT, wfTt = v("wfillT"), t("wfillT")
        onesrowf, onesrowft = self.tiles["onesrowf"]
        pK, pKt = self.tiles["pb5"]
        pW, pWt = self.tiles["pb7"]
        pQ, pQt = self.tiles["pb7"][0][:, 256:512], self.tiles["pb7"][1]
        O = self.op
        def f(e):
            return e.memset(rln[:, 0:1], 0.0)
        O("vector", f, writes=[rlnt])
        for r in (1, 2, 3):
            def f(e, r=r):
                return e.tensor_scalar(rln[:, r:r + 1], lf[:, h:h + 1], float(r),
                                       None, OP.mult)
            O("vector", f, reads=[lft], writes=[rlnt])
        def f(e):
            return e.tensor_add(rows[0:1, 4 * W8:4 * W8 + 4 * tw],
                                rows[0:1, 4 * W8:4 * W8 + 4 * tw],
                                rows[0:1, 0:4 * tw])
        O("vector", f, reads=[rowst], writes=[rowst])
        def f(e):
            return e.matmul(pK[:, 0:mw], onesrowf[0:1, :], rows[0:1, 0:4 * tw],
                            start=True, stop=True)
        O("tensor", f, reads=[rowst, onesrowft], writes=[pKt])
        def f(e):
            return e.matmul(pW[:, 0:mw], onesrowf[0:1, :],
                            rows[0:1, 4 * W8:4 * W8 + 4 * tw], start=True, stop=True)
        O("tensor", f, reads=[rowst, onesrowft], writes=[pWt])
        def f(e):
            return e.matmul(pQ[:, 0:tw], onesrowf[0:1, :],
                            rows[0:1, 8 * W8:8 * W8 + tw], start=True, stop=True)
        O("tensor", f, reads=[rowst, onesrowft], writes=[pQt])
        def f(e):
            return e.memset(Ct[:, 0:1], 0.0)
        O("vector", f, writes=[Ctt])
        def f(e):
            return e.tensor_scalar(att[:, 0:tw], Ls[:, 0:tw], rln[:, 3:4], None,
                                   OP.add)
        O("vector", f, reads=[Lst], writes=[attt], hard=[rlnt])
        src, srct, dst, dstt = att, attt, at2, at2t
        sh = 1
        while sh < tw:
            def f(e, sh=sh, src=src, dst=dst):
                return e.tensor_copy(dst[:, 0:sh], src[:, 0:sh])
            O("vector", f, reads=[srct], writes=[dstt])
            def f(e, sh=sh, src=src, dst=dst):
                return e.tensor_add(dst[:, sh:tw], src[:, sh:tw], src[:, 0:tw - sh])
            O("vector", f, reads=[srct], writes=[dstt])
            src, srct, dst, dstt = dst, dstt, src, srct
            sh *= 2
        def f(e, src=src):
            return e.tensor_copy(Ct[:, 1:tw + 1], src[:, 0:tw])
        O("vector", f, reads=[srct], writes=[Ctt])
        def f(e):
            return e.tensor_scalar(G0[:, 0:tw], Ct[:, 1:tw + 1], rln[:, 3:4], None,
                                   OP.subtract)
        O("vector", f, reads=[Ctt], writes=[G0t], hard=[rlnt])
        G0r, G0rt = sv("G0r")
        cecv = Ct[:, 0:tw:CT].unsqueeze(2).broadcast_to([128, nm_, CT])
        def f(e, cecv=cecv):
            return e.tensor_sub(G0r[:, 0:tw].rearrange("d (m t) -> d m t", t=CT),
                                G0[:, 0:tw].rearrange("d (m t) -> d m t", t=CT),
                                cecv)
        O("vector", f, reads=[G0t, Ctt], writes=[G0rt])
        ancv = Ct[:, 0:tw:4].unsqueeze(2).broadcast_to([128, tw // 4, 4])
        def f(e, ancv=ancv):
            return e.tensor_sub(Ga[:, 0:tw].rearrange("d (b p) -> d b p", p=4),
                                G0[:, 0:tw].rearrange("d (b p) -> d b p", p=4),
                                ancv)
        O("vector", f, reads=[G0t, Ctt], writes=[Gat])
        def f(e):
            return e.tensor_sub(LAM[:, 8:8 + nm_], Ct[:, CT:tw + 1:CT],
                                Ct[:, 0:tw:CT])
        O("vector", f, reads=[Ctt], writes=[LAMt])
        def f(e):
            return e.tensor_scalar(LAM[:, 0:nm_], LAM[:, 8:8 + nm_],
                                   float(-87.0), None, OP.max)
        O("vector", f, reads=[LAMt], writes=[LAMt])
        def f(e):
            return e.activation(LAM[:, 0:nm_], LAM[:, 0:nm_], AF.Exp)
        O("scalar", f, reads=[LAMt], writes=[LAMt])
        a4 = Ct[:, 0:tw:4].rearrange("d (m I) -> d m I", I=4)
        minu = a4.unsqueeze(3).broadcast_to([128, nm_, 4, 4])
        subt = a4.unsqueeze(2).broadcast_to([128, nm_, 4, 4])
        dxv = dx[:, 0:16 * nm_].rearrange("d (m I i) -> d m I i", I=4, i=4)
        def f(e, dxv=dxv, minu=minu, subt=subt):
            return e.tensor_sub(dxv, minu, subt)
        O("vector", f, reads=[Ctt], writes=[dxt], tag="dexp.sub")
        def f(e):
            return e.tensor_scalar(dx[:, 0:16 * nm_], dx[:, 0:16 * nm_],
                                   float(-87.0), float(0.0), OP.max, OP.min)
        O("vector", f, reads=[dxt], writes=[dxt])
        def f(e):
            return e.activation(dx[:, 0:16 * nm_], dx[:, 0:16 * nm_], AF.Exp)
        O("scalar", f, reads=[dxt], writes=[dxt], tag="dexp")
        def rmaj(x):
            return x[:, 0:mw].rearrange("d (r m I p) -> d m I r p",
                                        r=4, m=nm_, I=4)
        def bmaj(x):
            return x[:, 0:mw].rearrange("d (m I r p) -> d m I r p",
                                        m=nm_, I=4, r=4)
        for r in range(4):
            def f(e, r=r):
                return e.scalar_tensor_tensor(art[:, r * tw:(r + 1) * tw],
                                              Ga[:, 0:tw], rln[:, r:r + 1],
                                              pK[:, r * tw:(r + 1) * tw],
                                              OP.add, OP.add)
            O("vector", f, reads=[Gat, pKt], writes=[artt], hard=[rlnt])
        def f(e):
            return e.activation(art[:, 0:mw], art[:, 0:mw], AF.Exp)
        O("scalar", f, reads=[artt], writes=[artt])
        kqK = KQ[:, 0:2 * mw].rearrange("d (m I h r p) -> d h m I r p",
                                        I=4, h=2, r=4, p=4)[:, 0]
        def f(e, kqK=kqK):
            return e.tensor_tensor(kqK, rmaj(kh), rmaj(art), OP.mult)
        O("vector", f, reads=[kht, artt], writes=[KQt], tag="KH")
        for r in range(4):
            def f(e, r=r):
                return e.scalar_tensor_tensor(art[:, r * tw:(r + 1) * tw],
                                              G0r[:, 0:tw], rln[:, r:r + 1],
                                              pK[:, r * tw:(r + 1) * tw],
                                              OP.add, OP.add)
            O("vector", f, reads=[G0rt, pKt], writes=[artt], hard=[rlnt])
        def f(e):
            return e.tensor_scalar(art[:, 0:mw], art[:, 0:mw], float(-87.0),
                                   None, OP.max)
        O("vector", f, reads=[artt], writes=[artt])
        def f(e):
            return e.activation(art[:, 0:mw], art[:, 0:mw], AF.Exp)
        O("scalar", f, reads=[artt], writes=[artt])
        def f(e):
            return e.tensor_scalar(art[:, 0:mw], art[:, 0:mw], -1.0, None, OP.mult)
        O("vector", f, reads=[artt], writes=[artt])
        def f(e):
            return e.tensor_tensor(bmaj(KG), rmaj(kh), rmaj(art), OP.mult)
        O("vector", f, reads=[kht, artt], writes=[KGt], tag="KG")
        for r in range(4):
            def f(e, r=r):
                return e.scalar_tensor_tensor(art[:, r * tw:(r + 1) * tw],
                                              Ga[:, 0:tw], rln[:, r:r + 1],
                                              pW[:, r * tw:(r + 1) * tw],
                                              OP.add, OP.subtract)
            O("vector", f, reads=[Gat, pWt], writes=[artt], hard=[rlnt])
        def f(e):
            return e.activation(art[:, 0:mw], art[:, 0:mw], AF.Exp,
                                scale=np.float32(-1.0))
        O("scalar", f, reads=[artt], writes=[artt])
        def f(e):
            return e.tensor_tensor(bmaj(WH), rmaj(kh), rmaj(art), OP.mult)
        O("vector", f, reads=[kht, artt], writes=[WHt], tag="WH")
        gev = LAM[:, 8:8 + nm_].unsqueeze(2).broadcast_to([128, nm_, CT])
        def f(e, gev=gev):
            return e.tensor_sub(att[:, 0:tw].rearrange("d (m t) -> d m t", t=CT),
                                gev,
                                G0r[:, 0:tw].rearrange("d (m t) -> d m t", t=CT))
        O("vector", f, reads=[LAMt, G0rt], writes=[attt])
        for r in range(4):
            def f(e, r=r):
                return e.scalar_tensor_tensor(art[:, r * tw:(r + 1) * tw],
                                              att[:, 0:tw], rln[:, r:r + 1],
                                              pW[:, r * tw:(r + 1) * tw],
                                              OP.subtract, OP.add)
            O("vector", f, reads=[attt, pWt], writes=[artt], hard=[rlnt])
        def f(e):
            return e.tensor_scalar(art[:, 0:mw], art[:, 0:mw], float(-87.0),
                                   None, OP.max)
        O("vector", f, reads=[artt], writes=[artt])
        def f(e):
            return e.activation(art[:, 0:mw], art[:, 0:mw], AF.Exp)
        O("scalar", f, reads=[artt], writes=[artt])
        def f(e):
            return e.tensor_tensor(bmaj(WG), rmaj(kh), rmaj(art), OP.mult)
        O("vector", f, reads=[kht, artt], writes=[WGt], tag="WG")
        vhb, vhbt = sv("vhb")
        vlb, vlbt = sv("vlb")
        def f(e):
            return e.tensor_copy(bmaj(vhb), rmaj(vhh))
        O("vector", f, reads=[vht], writes=[vhbt])
        def f(e):
            return e.tensor_copy(bmaj(vlb), rmaj(vll))
        O("vector", f, reads=[vlt], writes=[vlbt])
        def f(e):
            return e.tensor_add(att[:, 0:tw], Ga[:, 0:tw], pQ[:, 0:tw])
        O("vector", f, reads=[Gat, pQt], writes=[attt])
        def f(e):
            return e.activation(att[:, 0:tw], att[:, 0:tw], AF.Exp)
        O("scalar", f, reads=[attt], writes=[attt])
        def f(e):
            return e.tensor_mul(at2[:, 0:tw], qs[:, 0:tw], att[:, 0:tw])
        O("vector", f, reads=[qst, attt], writes=[at2t])
        for r in range(4):
            qlv = KQ[:, 0:2 * mw].rearrange("d (m I h r p) -> d h r m I p",
                                            I=4, h=2, r=4, p=4)[:, 1, r]
            def f(e, r=r, qlv=qlv):
                return e.tensor_scalar(qlv, at2[:, 0:tw].rearrange(
                    "d (m I p) -> d m I p", I=4, p=4),
                    wfT[:, 16 * r + h:16 * r + h + 1], None, OP.mult)
            O("vector", f, reads=[at2t], writes=[KQt], hard=[wfTt])
        def f(e):
            return e.tensor_add(att[:, 0:tw], G0r[:, 0:tw], pQ[:, 0:tw])
        O("vector", f, reads=[G0rt, pQt], writes=[attt])
        def f(e):
            return e.tensor_scalar(att[:, 0:tw], att[:, 0:tw], float(-87.0),
                                   None, OP.max)
        O("vector", f, reads=[attt], writes=[attt])
        def f(e):
            return e.activation(att[:, 0:tw], att[:, 0:tw], AF.Exp)
        O("scalar", f, reads=[attt], writes=[attt])
        def f(e):
            return e.tensor_mul(at2[:, 0:tw], qs[:, 0:tw], att[:, 0:tw])
        O("vector", f, reads=[qst, attt], writes=[at2t])
        def f(e):
            return e.tensor_scalar(QB[:, 0:tw], at2[:, 0:tw], pfT[:, h:h + 1],
                                   None, OP.mult)
        O("vector", f, reads=[at2t], writes=[QBt], hard=[pfTt], tag="QB")

    def s2_chunk_phases(self, hg, m, mi, nm_, buf, set_):
        nc = self.nc
        v = lambda n: self.tiles[n][0]
        t = lambda n: self.tiles[n][1]
        warmch = self.warm // CT
        O = self.op
        idsb, idsbt = self.tiles["idsb"]
        Wv, Wvt = self.s2buf["Wv"][buf]
        Nd, Ndt = self.s2buf["Nd"][buf]
        Ngap, Ngapt = self.s2buf["Ngap"][buf]
        Agap, Agapt = self.s2buf["Agap"][buf]
        pNM, pNMt = self.tiles["pb0"]
        pA, pAt = self.tiles["pb1"]
        pT, pTt = self.tiles["pb2"]
        pR, pRt = self.tiles["pb3"]
        pH, pHt = self.tiles["pb4"]
        pS, pSt = self.tiles["pb5"]
        pO = self.tiles["pb6"][0][:, 64 * set_:64 * set_ + 64]
        pOt = self.poT[set_]
        Yp, Ypt = self.s2buf["Yp"][buf]
        Uhi, Uhit = self.s2buf["Uhi"][buf]
        Ulo, Ulot = self.s2buf["Ulo"][buf]
        WgT, WgTt = self.s2buf["WgT"][buf]
        ApT, ApTt = self.s2buf["ApT"][buf]
        mM, mMt = self.tiles["mM"]
        mA, mAt = self.tiles["mA"]
        idg, idgt = self.tiles["idg"]
        ppool, ppoolt = self.tiles["ppool"]
        Sf, Sft = self.tiles["Sf"][0], self.sfT[hg]
        Sb, Sbt = self.tiles["Sb"][0], self.sbT[hg]
        ob, obt = self.tiles["obuf"]
        c0 = mi * CMICRO
        JJ = range(4)
        TCOL = {1: 0, 2: 256, 4: 512, 8: 768, 16: 1024, 32: 1280}
        NCOL = (1536, 1792)
        def sj(nm, j):
            if nm in ("QB", "LAM"):
                return self.sl[(nm, j, set_)]
            return self.sl[(nm, j)]
        A = []
        B = []
        def ra():
            A.append([])
            return A[-1].append
        def rb():
            B.append([])
            return B[-1].append
        r_ = ra()
        for j in JJ:
            WH, WHt_ = sj("WH", j)
            dex, dext = sj("dex", j)
            dxv = dex[:, 16 * mi:16 * mi + 16].rearrange(
                "d (I i) -> d I i", i=4).unsqueeze(3).broadcast_to([128, 4, 4, 16])
            wvv = Wv[:, 256 * j:256 * (j + 1)].rearrange(
                "d (I i c) -> d I i c", I=4, i=4)
            whv2 = WH[:, c0:c0 + CMICRO].rearrange(
                "d (i c) -> d i c", i=4).unsqueeze(1).broadcast_to([128, 4, 4, 16])
            def f(e, wvv=wvv, whv2=whv2, dxv=dxv):
                return e.tensor_tensor(wvv, whv2, dxv, OP.mult)
            r_(lambda f=f, WHt_=WHt_, dext=dext: O("vector", f, reads=[WHt_, dext],
                                                   writes=[Wvt], tag="Wv"))
        r_ = ra()
        for j in JJ:
            KQ, KQt_ = sj("KQ", j)
            for I in range(4):
                def f(e, j=j, I=I, KQ=KQ):
                    return e.matmul(pNM[32 * I:32 * I + 32, 64 * j:64 * j + 64],
                                    KQ[:, 128 * mi + 32 * I:128 * mi + 32 * I + 32],
                                    Wv[:, 256 * j + 64 * I:256 * j + 64 * I + 64],
                                    start=True, stop=True, tile_position=(0, 32 * I))
                r_(lambda f=f, KQt_=KQt_: O("tensor", f, reads=[KQt_, Wvt],
                                            writes=[pNMt]))
        r_ = ra()
        mMv = mM[:, 0:CMICRO].unsqueeze(1).broadcast_to([128, 4, CMICRO])
        mAv = mA[:, 0:CMICRO].unsqueeze(1).broadcast_to([128, 4, CMICRO])
        def f(e):
            return e.tensor_tensor(Ngap[:, 0:256].rearrange("d (j c) -> d j c", j=4),
                                   pNM[:, 0:256].rearrange("d (j c) -> d j c", j=4),
                                   mMv, OP.mult)
        r_(lambda f=f: O("vector", f, reads=[pNMt, mMt], writes=[Ngapt]))
        def f(e):
            return e.tensor_tensor(Agap[:, 0:256].rearrange("d (j c) -> d j c", j=4),
                                   pNM[:, 0:256].rearrange("d (j c) -> d j c", j=4),
                                   mAv, OP.mult)
        r_(lambda f=f: O("vector", f, reads=[pNMt, mAt], writes=[Agapt]))
        r_ = ra()
        for h in JJ:
            p, q = h // 2, h % 2
            WG, WGt_ = sj("WG", h)
            def f(e, p=p, q=q, WG=WG):
                return e.matmul(pA[64 * q:64 * q + 64, 128 * p:128 * p + 128],
                                WG[:, c0:c0 + CMICRO], idsb[:, :],
                                start=True, stop=True, tile_position=(0, 64 * q))
            r_(lambda f=f, WGt_=WGt_: O("tensor", f, reads=[WGt_, idsbt],
                                        writes=[pAt]))
        if m >= warmch:
            for h in JJ:
                p, q = h // 2, h % 2
                def f(e, h=h, p=p, q=q):
                    return e.matmul(pA[64 * q:64 * q + 64,
                                       256 + 32 * p + 16 * q:256 + 32 * p + 16 * q + 16],
                                    Agap[:, 64 * h:64 * h + 64], ppool[:, :],
                                    start=True, stop=True, tile_position=(0, 64 * q))
                r_(lambda f=f: O("tensor", f, reads=[Agapt, ppoolt], writes=[pAt]))
        r_ = ra()
        def f(e):
            return e.activation(WgT[:, 0:256], pA[:, 0:256], AF.Copy)
        r_(lambda f=f: O("scalar", f, reads=[pAt], writes=[WgTt]))
        if m >= warmch:
            def f(e):
                return e.activation(ApT[:, 0:64], pA[:, 256:320], AF.Copy)
            r_(lambda f=f: O("scalar", f, reads=[pAt], writes=[ApTt]))
        r_ = ra()
        for h in JJ:
            p, q = h // 2, h % 2
            def f(e, h=h, p=p, q=q):
                return e.matmul(pT[64 * q:64 * q + 64,
                                   128 * p + 64 * q:128 * p + 64 * q + 64],
                                idg[:, 0:64], Ngap[:, 64 * h:64 * h + 64],
                                start=True, stop=True, tile_position=(0, 64 * q))
            r_(lambda f=f: O("tensor", f, reads=[idgt, Ngapt], writes=[pTt]))
            def f(e, h=h, p=p, q=q):
                return e.matmul(pT[64 * q:64 * q + 64,
                                   256 + 128 * p + 64 * q:256 + 128 * p + 64 * q + 64],
                                Ngap[:, 64 * h:64 * h + 64], idg[:, 0:64],
                                start=True, stop=True, tile_position=(0, 64 * q))
            r_(lambda f=f: O("tensor", f, reads=[idgt, Ngapt], writes=[pTt]))
        r_ = ra()
        def f(e):
            return e.activation(Nd[:, NCOL[0]:NCOL[0] + 256], pT[:, 0:256], AF.Copy)
        r_(lambda f=f: O("scalar", f, reads=[pTt], writes=[Ndt]))
        def f(e):
            return e.activation(Nd[:, 0:256], pT[:, 256:512], AF.Copy)
        r_(lambda f=f: O("scalar", f, reads=[pTt], writes=[Ndt]))
        cn = 0
        for lv in (1, 2, 4, 8):
            r_ = ra()
            tc, nc_ = TCOL[lv], NCOL[cn]
            tc2, nc2 = TCOL[2 * lv], NCOL[1 - cn]
            for p in range(2):
                def f(e, p=p, tc=tc, nc_=nc_):
                    return e.matmul(pT[:, 128 * p:128 * p + 128],
                                    Nd[:, tc + 128 * p:tc + 128 * p + 128],
                                    Nd[:, nc_ + 128 * p:nc_ + 128 * p + 128],
                                    start=True, stop=True)
                r_(lambda f=f: O("tensor", f, reads=[Ndt], writes=[pTt]))
                def f(e, p=p, tc=tc, nc_=nc_):
                    return e.matmul(pT[:, 256 + 128 * p:256 + 128 * p + 128],
                                    Nd[:, nc_ + 128 * p:nc_ + 128 * p + 128],
                                    Nd[:, tc + 128 * p:tc + 128 * p + 128],
                                    start=True, stop=True)
                r_(lambda f=f: O("tensor", f, reads=[Ndt], writes=[pTt]))
            r_ = ra()
            def f(e, nc2=nc2):
                return e.activation(Nd[:, nc2:nc2 + 256], pT[:, 0:256], AF.Copy)
            r_(lambda f=f: O("scalar", f, reads=[pTt], writes=[Ndt]))
            def f(e, tc2=tc2):
                return e.activation(Nd[:, tc2:tc2 + 256], pT[:, 256:512], AF.Copy)
            r_(lambda f=f: O("scalar", f, reads=[pTt], writes=[Ndt]))
            cn = 1 - cn
        r_ = ra()
        for p in range(2):
            def f(e, p=p, nc_=NCOL[cn], tc=TCOL[16]):
                return e.matmul(pT[:, 256 + 128 * p:256 + 128 * p + 128],
                                Nd[:, nc_ + 128 * p:nc_ + 128 * p + 128],
                                Nd[:, tc + 128 * p:tc + 128 * p + 128],
                                start=True, stop=True)
            r_(lambda f=f: O("tensor", f, reads=[Ndt], writes=[pTt]))
        r_ = ra()
        def f(e):
            return e.activation(Nd[:, TCOL[32]:TCOL[32] + 256], pT[:, 256:512], AF.Copy)
        r_(lambda f=f: O("scalar", f, reads=[pTt], writes=[Ndt], tag="N32T"))
        r_ = rb()
        for h in JJ:
            p, q = h // 2, h % 2
            KG, KGt_ = sj("KG", h)
            vhb, vhbt_ = sj("vhb", h)
            vlb, vlbt_ = sj("vlb", h)
            hh = 4 * hg + h
            def f(e, p=p, q=q, KG=KG, hh=hh):
                return e.matmul(pR[64 * q:64 * q + 64, 128 * p:128 * p + 128],
                                KG[:, c0:c0 + CMICRO],
                                Sb[:, hh * 128:(hh + 1) * 128],
                                start=True, stop=False, tile_position=(0, 64 * q))
            r_(lambda f=f, KGt_=KGt_: O("tensor", f, reads=[KGt_, Sbt], writes=[pRt]))
            def f(e, p=p, q=q, vhb=vhb):
                return e.matmul(pR[64 * q:64 * q + 64, 128 * p:128 * p + 128],
                                vhb[:, c0:c0 + CMICRO], idsb[:, :],
                                start=False, stop=False, tile_position=(0, 64 * q))
            r_(lambda f=f, vhbt_=vhbt_: O("tensor", f, reads=[vhbt_, idsbt],
                                          writes=[pRt]))
            def f(e, p=p, q=q, vlb=vlb):
                return e.matmul(pR[64 * q:64 * q + 64, 128 * p:128 * p + 128],
                                vlb[:, c0:c0 + CMICRO], idsb[:, :],
                                start=False, stop=True, tile_position=(0, 64 * q))
            r_(lambda f=f, vlbt_=vlbt_: O("tensor", f, reads=[vlbt_, idsbt],
                                          writes=[pRt]))
        if m >= warmch:
            for h in JJ:
                hh = 4 * hg + h
                QB, QBt_ = sj("QB", h)
                def f(e, h=h, hh=hh, QB=QB):
                    return e.matmul(pO[:, 16 * h:16 * h + 16],
                                    Sb[:, hh * 128:(hh + 1) * 128],
                                    QB[:, mi * CT:(mi + 1) * CT],
                                    start=(h == 0), stop=False)
                r_(lambda f=f, QBt_=QBt_: O("tensor", f, reads=[Sbt, QBt_],
                                            writes=[pOt]))
        r_ = rb()
        def f(e):
            return e.activation(Yp[:, 0:256], pR[:, 0:256], AF.Copy)
        r_(lambda f=f: O("scalar", f, reads=[pRt], writes=[Ypt], tag="Y0"))
        cur = 0
        for lv in (32, 16, 8, 4, 2):
            r_ = rb()
            tc = TCOL[lv]
            for p in range(2):
                def f(e, p=p, tc=tc, cur=cur):
                    return e.matmul(pH[:, 128 * p:128 * p + 128],
                                    Nd[:, tc + 128 * p:tc + 128 * p + 128],
                                    Yp[:, 256 * cur + 128 * p:256 * cur + 128 * p + 128],
                                    start=True, stop=True)
                r_(lambda f=f: O("tensor", f, reads=[Ndt, Ypt], writes=[pHt]))
            def f(e, cur=cur):
                return e.tensor_add(Yp[:, 256 * (1 - cur):256 * (1 - cur) + 256],
                                    pH[:, 0:256], Yp[:, 256 * cur:256 * cur + 256])
            r_(lambda f=f: O("vector", f, reads=[pHt, Ypt], writes=[Ypt]))
            cur = 1 - cur
        r_ = rb()
        for p in range(2):
            def f(e, p=p, cur=cur):
                return e.matmul(pH[:, 128 * p:128 * p + 128],
                                Nd[:, 128 * p:128 * p + 128],
                                Yp[:, 256 * cur + 128 * p:256 * cur + 128 * p + 128],
                                start=True, stop=False)
            r_(lambda f=f: O("tensor", f, reads=[Ndt, Ypt], writes=[pHt]))
            def f(e, p=p, cur=cur):
                return e.matmul(pH[:, 128 * p:128 * p + 128], idsb[:, :],
                                Yp[:, 256 * cur + 128 * p:256 * cur + 128 * p + 128],
                                start=False, stop=True)
            r_(lambda f=f: O("tensor", f, reads=[idsbt, Ypt], writes=[pHt]))
        r_ = rb()
        UhiU, UhiUt = self.s2buf["UhiU"][buf]
        UloU, UloUt = self.s2buf["UloU"][buf]
        def f(e):
            return e.activation(Uhi[0:64, 0:256], pH[0:64, 0:256], AF.Copy)
        r_(lambda f=f: O("scalar", f, reads=[pHt], writes=[Uhit]))
        def f(e):
            return e.activation(UhiU[64:128, 0:256], pH[64:128, 0:256], AF.Copy)
        r_(lambda f=f: O("scalar", f, reads=[pHt], writes=[UhiUt]))
        def f(e):
            return e.scalar_tensor_tensor(Ulo[0:64, 0:256], Uhi[0:64, 0:256], -1.0,
                                          pH[0:64, 0:256], OP.mult, OP.add)
        r_(lambda f=f: O("vector", f, reads=[Uhit, pHt], writes=[Ulot]))
        def f(e):
            return e.scalar_tensor_tensor(UloU[64:128, 0:256], UhiU[64:128, 0:256],
                                          -1.0, pH[64:128, 0:256], OP.mult, OP.add)
        r_(lambda f=f: O("vector", f, reads=[UhiUt, pHt], writes=[UloUt]))
        r_ = rb()
        if m >= warmch:
            for p in range(2):
                for ut, utt in ((Uhi, Uhit), (UhiU, UhiUt), (Ulo, Ulot)):
                    def f(e, p=p, ut=ut):
                        return e.matmul(pO[:, 32 * p:32 * p + 32],
                                        ut[:, 128 * p:128 * p + 128],
                                        ApT[:, 32 * p:32 * p + 32],
                                        start=False, stop=False)
                    r_(lambda f=f, utt=utt: O("tensor", f, reads=[utt, ApTt],
                                              writes=[pOt]))
                def f(e, p=p):
                    return e.matmul(pO[:, 32 * p:32 * p + 32],
                                    UloU[:, 128 * p:128 * p + 128],
                                    ApT[:, 32 * p:32 * p + 32],
                                    start=False, stop=(p == 1))
                r_(lambda f=f: O("tensor", f, reads=[UloUt, ApTt], writes=[pOt]))
        for h in JJ:
            p, q = h // 2, h % 2
            uh, uht = (Uhi, Uhit) if q == 0 else (UhiU, UhiUt)
            ul, ult = (Ulo, Ulot) if q == 0 else (UloU, UloUt)
            def f(e, h=h, p=p, uh=uh):
                return e.matmul(pS[:, 128 * h:128 * h + 128],
                                WgT[:, 128 * p:128 * p + 128],
                                uh[:, 128 * p:128 * p + 128],
                                start=True, stop=False)
            r_(lambda f=f, uht=uht: O("tensor", f, reads=[WgTt, uht], writes=[pSt]))
            def f(e, h=h, p=p, ul=ul):
                return e.matmul(pS[:, 128 * h:128 * h + 128],
                                WgT[:, 128 * p:128 * p + 128],
                                ul[:, 128 * p:128 * p + 128],
                                start=False, stop=True)
            r_(lambda f=f, ult=ult: O("tensor", f, reads=[WgTt, ult], writes=[pSt]))
        r_ = rb()
        if m >= warmch:
            toff = (m - warmch) * CT
            obv = ob[:, 4 * hg * self.T_SL:(4 * hg + 4) * self.T_SL].rearrange(
                "d (j t) -> d j t", t=self.T_SL)[:, :, toff:toff + CT]
            def f(e, obv=obv):
                return e.tensor_copy(obv,
                                     pO[:, 0:64].rearrange("d (j t) -> d j t", j=4))
            r_(lambda f=f: O("vector", f, reads=[pOt], writes=[obt], tag="oev"))
        h0 = 4 * hg
        for h in JJ:
            LAM, LAMt_ = sj("LAM", h)
            hh = h0 + h
            def f(e, h=h, hh=hh, LAM=LAM):
                return e.scalar_tensor_tensor(Sf[:, hh * 128:(hh + 1) * 128],
                                              Sf[:, hh * 128:(hh + 1) * 128],
                                              LAM[:, mi:mi + 1],
                                              pS[:, 128 * h:128 * h + 128],
                                              OP.mult, OP.add)
            r_(lambda f=f, LAMt_=LAMt_: O("vector", f, reads=[Sft, pSt],
                                          writes=[Sft], hard=[LAMt_], tag="Sup"))
        def f(e, h0=h0):
            return e.tensor_copy(Sb[:, h0 * 128:(h0 + 4) * 128],
                                 Sf[:, h0 * 128:(h0 + 4) * 128])
        r_(lambda f=f: O("vector", f, reads=[Sft], writes=[Sbt]))
        return A, B[:1], B[1:]

    def stage3(self):
        nc, d = self.nc, self.dram
        T_SL, TX = self.T_SL, self.TX
        xn, xnt = self.tiles["xn"]
        v = lambda n: self.tiles[n][0]
        t = lambda n: self.tiles[n][1]
        ob, obt = self.tiles["obuf"]
        sq, sqt = self.tiles["sq"]
        rt, rtt = self.tiles["rtmp"]
        rb, rbt = self.tiles["rowbf"]
        ft, ftt = v("ftmp"), t("ftmp")
        ft2, ft2t = v("ftmp2"), t("ftmp2")
        kt, ktt = v("ktile"), t("ktile")
        xch, xcht = v("xchunk"), t("xchunk")
        wtf, wtft = v("wtf"), t("wtf")
        wtb, wtbt = v("wtb"), t("wtb")
        onesrow, onesrowt = self.tiles["onesrow"]
        onescolb, onescolbt = self.tiles["onescolb"]
        pss, psst = self.tiles["pss"]
        pbc, pbct = self.tiles["pbc"]
        O = self.op
        def f(e):
            return e.memset(ft[0:1, 0:1], 0.0)
        O("vector", f, writes=[wtft] + self.s2b1t + [ftt], tag="fenceB")
        onwc, onwct = self.sbuf("onwc", [128, 1], F32)
        self.dma("qsync", onwc[:, 0:1], d["onw"], writes=[onwct])
        ffscT, ffscTt = self.sbuf("ffscT", [128, 16], F32)
        ffbT, ffbTt = self.sbuf("ffbT", [128, 16], F32)
        self.dma("qsync", ffscT[:, 0:16],
                 d["ffsc"].rearrange("(i p) o -> p (i o)", p=128),
                 writes=[ffscTt], slow=True)
        self.dma("qsync", ffbT[:, 0:16],
                 d["ffb"].rearrange("(i p) o -> p (i o)", p=128),
                 writes=[ffbTt], slow=True)
        for h in range(H):
            for c0 in range(0, T_SL, 512):
                cw = min(512, T_SL - c0)
                oc0 = h * T_SL + c0
                def f(e, oc0=oc0, cw=cw):
                    return e.tensor_mul(sq[:, 0:cw], ob[:, oc0:oc0 + cw],
                                        ob[:, oc0:oc0 + cw])
                O("vector", f, reads=[obt], writes=[sqt])
                def f(e, cw=cw):
                    return e.matmul(pss[0:1, 0:cw], onescolb[:, 0:1], sq[:, 0:cw],
                                    start=True, stop=True)
                O("tensor", f, reads=[sqt, onescolbt], writes=[psst])
                def f(e, cw=cw):
                    return e.activation(rt[0:1, 0:cw], pss[0:1, 0:cw], AF.Ln,
                                        scale=np.float32(1.0 / DV),
                                        bias=np.float32(EPS))
                O("scalar", f, reads=[psst], writes=[rtt])
                def f(e, cw=cw):
                    return e.activation(rt[0:1, 512:512 + cw], rt[0:1, 0:cw],
                                        AF.Exp, scale=np.float32(-0.5))
                O("scalar", f, reads=[rtt], writes=[rtt])
                def f(e, cw=cw):
                    return e.tensor_copy(rb[0:1, 0:cw], rt[0:1, 512:512 + cw])
                O("vector", f, reads=[rtt], writes=[rbt])
                def f(e, cw=cw):
                    return e.matmul(pbc[:, 0:cw], onesrow[0:1, :], rb[0:1, 0:cw],
                                    start=True, stop=True)
                O("tensor", f, reads=[rbt, onesrowt], writes=[pbct])
                self.dma("qact", kt[:, 0:cw],
                         d["gsp"][h * 128:(h + 1) * 128, c0:c0 + cw],
                         writes=[ktt], tag="g.ld")
                def f(e, cw=cw):
                    return e.activation(ft[:, 0:cw], kt[:, 0:cw], AF.Sigmoid)
                O("scalar", f, reads=[ktt], writes=[ftt])
                def f(e, oc0=oc0, cw=cw):
                    return e.scalar_tensor_tensor(ft2[:, 0:cw], ob[:, oc0:oc0 + cw],
                                                  onwc[:, 0:1], pbc[:, 0:cw],
                                                  OP.mult, OP.mult)
                O("vector", f, reads=[obt, pbct], writes=[ft2t], hard=[onwct])
                def f(e, oc0=oc0, cw=cw):
                    return e.tensor_tensor(ob[:, oc0:oc0 + cw], ft2[:, 0:cw],
                                           ft[:, 0:cw], OP.mult)
                O("vector", f, reads=[ft2t, ftt], writes=[obt], tag="onrm")
        pj0, pj0t = self.tiles["pj0"]
        pj1, pj1t = self.tiles["pj1"]
        W3RR = ("qsync", "qpool", "qact")
        for oc in range(16):
            for wi, wn in enumerate(("wop", "wff")):
                src = d[wn].rearrange("(i p) o -> p i o", p=128)[:, :, oc * 128:(oc + 1) * 128]
                self.dma(W3RR[(2 * oc + wi) % 3],
                         wtf[:, wi * 2048:wi * 2048 + 2048].rearrange(
                             "p (i o) -> p i o", i=16),
                         src, writes=[wtft], tag=f"w3.{wn}{oc}")
                def f(e, wi=wi):
                    return e.tensor_copy(wtb[:, wi * 2048:wi * 2048 + 2048],
                                         wtf[:, wi * 2048:wi * 2048 + 2048])
                O("gpsimd", f, reads=[wtft], writes=[wtbt])
            for c0 in range(0, T_SL, 512):
                cw = min(512, T_SL - c0)
                for ic in range(NIC):
                    def f(e, ic=ic, c0=c0, cw=cw):
                        return e.matmul(pj0[:, 0:cw],
                                        wtb[:, ic * 128:(ic + 1) * 128],
                                        ob[:, ic * T_SL + c0:ic * T_SL + c0 + cw],
                                        start=(ic == 0), stop=(ic == NIC - 1))
                    O("tensor", f, reads=[wtbt, obt], writes=[pj0t], tag="mmop")
                xc0 = (self.TX - self.T_SL) + c0
                for ic in range(NIC):
                    def f(e, ic=ic, c0=c0, cw=cw, xc0=xc0):
                        return e.matmul(pj1[:, 0:cw],
                                        wtb[:, 2048 + ic * 128:2048 + (ic + 1) * 128],
                                        xn[:, ic * self.TX + xc0:ic * self.TX + xc0 + cw],
                                        start=(ic == 0), stop=(ic == NIC - 1))
                    O("tensor", f, reads=[wtbt, xnt], writes=[pj1t], tag="mmff")
                def f(e, cw=cw, oc=oc):
                    return e.activation(ft[:, 0:cw], pj1[:, 0:cw], AF.Silu,
                                        scale=ffscT[:, oc:oc + 1],
                                        bias=ffbT[:, oc:oc + 1])
                O("scalar", f, reads=[pj1t], writes=[ftt], hard=[ffscTt, ffbTt],
                  tag="ffsilu")
                self.dma("qsync", xch[:, 0:cw],
                         d["x_t"][oc * 128:(oc + 1) * 128, xc0:xc0 + cw],
                         writes=[xcht], tag="x3.ld")
                def f(e, cw=cw):
                    return e.tensor_tensor(ft2[:, 0:cw], pj0[:, 0:cw], ft[:, 0:cw],
                                           OP.mult)
                O("vector", f, reads=[pj0t, ftt], writes=[ft2t])
                def f(e, cw=cw):
                    return e.tensor_add(ft2[:, 0:cw], ft2[:, 0:cw], xch[:, 0:cw])
                O("vector", f, reads=[ft2t, xcht], writes=[ft2t], tag="res")
                self.dma("qact", d["out"][oc * 128:(oc + 1) * 128, c0:c0 + cw],
                         ft2[:, 0:cw], reads=[ft2t], tag=f"out.{oc}.{c0}")

    def emit_debug_dumps(self):
        for ent in self.dbg_dump:
            if isinstance(ent, tuple):
                name, pmax, cmax = ent
            else:
                name, pmax, cmax = ent, None, None
            v, t = self.tiles[name]
            shp = list(v.shape)
            if pmax:
                shp = [pmax, cmax]
            dt_ = v.dtype if hasattr(v, 'dtype') else F32
            d = self.nc.dram_tensor("dbg_" + name, shp, dt_,
                                    kind="ExternalOutput").ap()
            src = v[:, :] if not pmax else v[0:pmax, 0:cmax]
            self.dma("qpool", d, src, reads=[t], tag="dbg." + name)

    def emit_blocks(self):
        nc, pl = self.nc, self.pl
        esem = {e: nc.semaphore(f"es_{e}").__enter__() for e in ENGINES}
        qsem = {(q, l): nc.semaphore(f"qs_{q}_{l}").__enter__()
                for q in DMAQ for l in range(NLANES)}
        blkcm = nc.Block()
        blk = blkcm.__enter__()
        self.cms.append(blkcm)

        def run_stream(eng_obj, ename):
            qfinal = {}
            for op in pl.streams[ename]:
                if op.dma_queue:
                    key = (op.dma_queue, op.qpos % NLANES)
                    qfinal[key] = 16 * (op.qpos // NLANES + 1)
            seen = {}
            for op in pl.streams[ename]:
                for key, val in sorted(op.waits.items(), key=str):
                    if seen.get(key, -1) >= val:
                        continue
                    seen[key] = val
                    sem = esem[key[1]] if key[0] == "e" else qsem[(key[1], key[2])]
                    eng_obj.wait_ge(sem, val)
                inst = op.emit(eng_obj)
                if op.dma_queue:
                    inst.then_inc(qsem[(op.dma_queue, op.qpos % NLANES)], 16)
                else:
                    if inst is not None:
                        inst.then_inc(esem[ename], 1)
                    else:
                        raise RuntimeError(f"op {op.idx} {op.tag} returned None")
            for key, val in sorted(qfinal.items()):
                eng_obj.wait_ge(qsem[key], val)

        @blk.tensor
        def _(e):
            run_stream(e, "tensor")

        @blk.vector
        def _(e):
            run_stream(e, "vector")

        @blk.scalar
        def _(e):
            run_stream(e, "scalar")

        @blk.gpsimd
        def _(e):
            run_stream(e, "gpsimd")

        @blk.sync
        def _(e):
            run_stream(e, "sync")


def kernel(**inputs):
    n_cores = 8
    T_SL = 1024
    maps = host_prepare(inputs, n_cores, T_SL)
    nc, _ = build_program(T_SL)
    res = run_bass_kernel_spmd(nc, maps, core_ids=list(range(n_cores)))
    parts = [np.asarray(res.results[ci]["out"]) for ci in range(n_cores)]
    full = np.concatenate(parts, axis=1)
    return np.ascontiguousarray(full.T)[None].astype(np.float32)


# version 13
# speedup vs baseline: 1.0120x; 1.0120x over previous
import numpy as np
import ml_dtypes
import concourse.bass as bass
import concourse.mybir as mybir
from concourse.bass_utils import run_bass_kernel_spmd

F32 = mybir.dt.float32
BF16 = mybir.dt.bfloat16
AF = mybir.ActivationFunctionType
OP = mybir.AluOpType
NPBF16 = ml_dtypes.bfloat16

H = 16
DH = 128
DV = 128
RR = 4
HID = 2048
NIC = HID // 128
EPS = 1e-5
CT = 16
INNER = 4
CMICRO = 4 * CT
NI = CT // INNER
WARM = 96
HALO = WARM + 3
SLABCH = 4
NEG87 = np.float32(-87.0)


def bfnp(x):
    return np.asarray(x, np.float32).astype(NPBF16)



ENGINES = ("tensor", "vector", "scalar", "gpsimd", "sync")
DMAQ = {"qsync": "sync", "qact": "scalar", "qpool": "gpsimd"}
NLANES = 12


class Op:
    __slots__ = ("idx", "engine", "emit", "deps", "dma_queue", "pos", "tag", "qpos", "waits")

    def __init__(self, idx, engine, emit, deps, dma_queue, tag):
        self.idx = idx
        self.engine = engine
        self.emit = emit
        self.deps = deps
        self.dma_queue = dma_queue
        self.pos = None
        self.tag = tag


class Tile:
    __slots__ = ("name", "wr", "readers")

    def __init__(self, name):
        self.name = name
        self.wr = None
        self.readers = []


class Plan:
    def __init__(self, nc):
        self.nc = nc
        self.ops = []

    def add(self, engine, emit, reads=(), writes=(), hard=(), dma_queue=None,
            tag=""):
        deps = {}
        idx = len(self.ops)
        for t in reads:
            if t.wr is not None:
                deps[t.wr] = max(deps.get(t.wr, 0), 0)
            t.readers.append(idx)
        for t in hard:
            if t.wr is not None:
                deps[t.wr] = 1
            t.readers.append(idx)
        for t in writes:
            if t.wr is not None:
                deps[t.wr] = max(deps.get(t.wr, 0), 0)
            for r in t.readers:
                if r != idx:
                    deps[r] = max(deps.get(r, 0), 0)
            t.wr = idx
            t.readers = []
        self.ops.append(Op(idx, engine, emit, deps, dma_queue, tag))
        return idx

    def finalize(self):
        streams = {e: [] for e in ENGINES}
        qpos = {q: 0 for q in DMAQ}
        cpos = {e: 0 for e in ENGINES}
        for op in self.ops:
            streams[op.engine].append(op)
            if op.dma_queue:
                op.qpos = qpos[op.dma_queue]
                qpos[op.dma_queue] += 1
                op.pos = None
            else:
                op.pos = cpos[op.engine]
                cpos[op.engine] += 1
        for op in self.ops:
            waits = {}
            for d, hardflag in op.deps.items():
                dop = self.ops[d]
                if dop.dma_queue:
                    key = ("q", dop.dma_queue, dop.qpos % NLANES)
                    val = 16 * (dop.qpos // NLANES + 1)
                elif (dop.engine == op.engine and not op.dma_queue
                      and not hardflag and op.engine == "tensor"):
                    continue
                else:
                    key = ("e", dop.engine)
                    val = dop.pos + 1
                if waits.get(key, 0) < val:
                    waits[key] = val
            if op.dma_queue:
                key = ("q", op.dma_queue, op.qpos % NLANES)
                val = 16 * (op.qpos // NLANES)
                if val > 0 and waits.get(key, 0) < val:
                    waits[key] = val
            op.waits = waits
        self.streams = streams



def host_consts():
    time = np.zeros(CMICRO, np.int64)
    tok = np.zeros(CMICRO, np.int64)
    for sp in range(CMICRO):
        I, r, p = sp // 16, (sp % 16) // 4, sp % 4
        time[sp] = 4 * (4 * I + p) + r
        tok[sp] = 4 * I + p
    iblk = tok // INNER
    smask = (time[:, None] > time[None, :]).astype(np.float32)
    imask = (time[:, None] >= time[None, :]).astype(np.float32)
    gapM = np.full(CMICRO, -1, np.int64)
    gapA = np.full(CMICRO, -1, np.int64)
    for I in range(NI):
        rows = [s for s in range(CMICRO) if iblk[s] == I]
        for p, s in enumerate(rows):
            gapM[s] = 32 * I + p
            gapA[s] = 32 * I + 16 + p
    mask_gapM = np.zeros((128, CMICRO), np.float32)
    mask_gapA = np.zeros((128, CMICRO), np.float32)
    for s in range(CMICRO):
        mask_gapM[gapM[s], :] = -smask[s, :]
        mask_gapA[gapA[s], :] = imask[s, :]
    ident_gapM = np.zeros((128, CMICRO), np.float32)
    for a in range(CMICRO):
        ident_gapM[gapM[a], a] = 1.0
    ppool_gap = np.zeros((128, CT), np.float32)
    for s in range(CMICRO):
        ppool_gap[gapA[s], tok[s]] = 1.0
    return {
        "c_maskM": bfnp(mask_gapM),
        "c_maskA": bfnp(mask_gapA),
        "c_identg": bfnp(ident_gapM),
        "c_ppool": bfnp(ppool_gap),
        "c_ident": bfnp(np.eye(128, dtype=np.float32)),
    }, gapM, gapA


def host_prepare(inputs, n_cores, T_SL):
    x = np.asarray(inputs["L0_attn_hidden_states"][0], np.float32)
    xt = np.ascontiguousarray(x.T)

    def t_(name):
        return np.ascontiguousarray(np.asarray(inputs[name], np.float32).T)

    consts, _, _ = host_consts()
    shared = {
        "wq": t_("L0_attn_q_proj_w"), "wk": t_("L0_attn_k_proj_w"),
        "wv": t_("L0_attn_v_proj_w"), "wb": t_("L0_attn_b_proj_w"),
        "wf1": t_("L0_attn_f1_w"), "wf2": t_("L0_attn_f2_w"),
        "wg1": t_("L0_attn_g1_w"), "wg2": t_("L0_attn_g2_w"),
        "wop": t_("L0_attn_o_proj_w"), "wff": t_("L0_ffn_weight"),
        "qcw": np.ascontiguousarray(np.asarray(inputs["L0_attn_q_conv_w"], np.float32)),
        "kcw": np.ascontiguousarray(np.asarray(inputs["L0_attn_k_conv_w"], np.float32)),
        "vcw": np.ascontiguousarray(np.asarray(inputs["L0_attn_v_conv_w"], np.float32)),
        "alog": np.asarray(inputs["L0_attn_A_log"], np.float32).reshape(H, 1),
        "dtb_hd": np.ascontiguousarray(
            np.asarray(inputs["L0_attn_dt_bias"], np.float32).reshape(H, DH)),
        "mlog": np.ascontiguousarray(np.asarray(inputs["L0_attn_micro_logits"], np.float32)),
        "onw": np.asarray(inputs["L0_attn_o_norm_w"], np.float32).reshape(DV, 1),
        "g2b": np.asarray(inputs["L0_attn_g2_b"], np.float32).reshape(HID, 1),
        "ffsc": np.asarray(inputs["L0_ffn_scale_weight"], np.float32).reshape(HID, 1),
        "ffb": np.asarray(inputs["L0_ffn_bias"], np.float32).reshape(HID, 1),
    }
    shared.update(consts)
    maps = []
    for ci in range(n_cores):
        t0 = ci * T_SL
        lo = t0 - HALO
        xs = np.zeros((HID, T_SL + HALO), np.float32)
        s = max(lo, 0)
        xs[:, s - lo:] = xt[:, s:t0 + T_SL]
        m = dict(shared)
        m["x_t"] = xs
        maps.append(m)
    return maps



def build_program(T_SL=1024, warm=WARM, debug=False, dbg_dump=()):
    TW = T_SL + warm
    TX = T_SL + warm + 3
    nch = TW // CT
    assert TW % CT == 0

    nc = bass.Bass("TRN2", target_bir_lowering=False, debug=debug)
    pl = Plan(nc)

    dram = {}
    def din(nm, shp, dt=F32):
        dram[nm] = nc.dram_tensor(nm, shp, dt, kind="ExternalInput").ap()
    din("x_t", [HID, TX])
    din("wq", [HID, HID]); din("wk", [HID, 4 * HID]); din("wv", [HID, 4 * HID])
    din("wb", [HID, 64]); din("wf1", [HID, 128]); din("wf2", [128, HID])
    din("wg1", [HID, 128]); din("wg2", [128, HID]); din("wop", [HID, HID])
    din("wff", [HID, HID])
    din("qcw", [HID, 4]); din("kcw", [4 * HID, 4]); din("vcw", [4 * HID, 4])
    din("alog", [H, 1]); din("dtb_hd", [H, DH]); din("mlog", [H, 4])
    din("onw", [DV, 1]); din("g2b", [HID, 1]); din("ffsc", [HID, 1]); din("ffb", [HID, 1])
    for nm in ("c_maskM", "c_maskA", "c_identg"):
        din(nm, [128, CMICRO], BF16)
    din("c_ppool", [128, CT], BF16)
    din("c_ident", [128, 128], BF16)
    dram["out"] = nc.dram_tensor("out", [HID, T_SL], F32, kind="ExternalOutput").ap()
    def dsc(nm, shp, dt):
        dram[nm] = nc.dram_tensor(nm, shp, dt).ap()
    dsc("ksp", [4 * HID, TW], BF16)
    dsc("vsph", [4 * HID, TW], BF16)
    dsc("vspl", [4 * HID, TW], BF16)
    dsc("qsp", [HID, TW], BF16)
    dsc("lsp", [HID, TW], F32)
    dsc("rowk", [64, TW], F32)
    dsc("lnb", [64, TW], F32)
    dsc("rowq", [16, TW], F32)
    dsc("gsp", [HID, T_SL], BF16)

    b = Builder(nc, pl, dram, T_SL, TW, TX, nch)
    b.dbg_dump = dbg_dump
    b.build()
    return nc, b


class Builder:
    def __init__(self, nc, pl, dram, T_SL, TW, TX, nch):
        self.nc, self.pl, self.dram = nc, pl, dram
        self.T_SL, self.TW, self.TX, self.nch = T_SL, TW, TX, nch
        self.warm = TW - T_SL
        self.cms = []
        self.tiles = {}
        self.psum_used = 0

    def sbuf(self, name, shp, dt):
        h = self.nc.sbuf_tensor(name, shp, dt)
        v = h.__enter__()
        self.cms.append(h)
        t = Tile(name)
        self.tiles[name] = (v, t)
        return v, t

    def psum(self, name, shp, dt=F32):
        h = self.nc.psum_tensor(name, shp, dt)
        v = h.__enter__()
        self.cms.append(h)
        t = Tile(name)
        self.tiles[name] = (v, t)
        return v, t

    def dtile(self, name):
        if name not in self.tiles:
            self.tiles[name] = (None, Tile(name))
        return self.tiles[name][1]

    def dma(self, queue, out_ap, in_ap, reads=(), writes=(), tag="", slow=False):
        eng = DMAQ[queue]
        def emit(e):
            if slow:
                return e.dma_start(out=out_ap, in_=in_ap,
                                   allow_slow_non_contiguous=True)
            return e.dma_start(out=out_ap, in_=in_ap)
        return self.pl.add(eng, emit, reads=reads, writes=writes,
                           dma_queue=queue, tag=tag)

    def op(self, engine, fn, reads=(), writes=(), hard=(), tag=""):
        return self.pl.add(engine, fn, reads=reads, writes=writes, hard=hard,
                           tag=tag)

    def build(self):
        self.alloc_all()
        self.stage0_consts()
        self.stage0_ln()
        self.stage1()
        self.stage2()
        self.stage3()
        self.emit_debug_dumps()
        self.pl.finalize()
        self.emit_blocks()
        for cm in reversed(self.cms):
            cm.__exit__(None, None, None)

    def alloc_all(self):
        TX, TW, T_SL = self.TX, self.TW, self.T_SL
        s = self.sbuf
        p = self.psum
        s("xn", [128, NIC * TX], BF16)
        s("xchunk", [128, 2 * 512], F32)
        s("statrow", [1, 1536], F32)
        s("ybuf", [128, TX + 8], BF16)
        s("wtf", [128, 2 * 2048], F32)
        s("wtb", [128, 4 * 2048], BF16)
        s("ktile", [128, 512], BF16)
        s("vhi", [128, 512], BF16)
        s("vlo", [128, 512], BF16)
        s("ftmp", [128, 512], F32)
        s("ftmp2", [128, 512], F32)
        s("sq", [128, 512], BF16)
        s("rowbf", [1, 1024], BF16)
        s("y1", [128, TX], BF16)
        s("yg1", [128, TX], BF16)
        s("convd", [128, 4 * 128], BF16)
        s("cwcol", [128, 4], F32)
        s("idsb", [128, 128], BF16)
        s("hd_misc", [16, 1792], F32)
        s("lnfT", [128, 16], F32)
        s("AbT", [128, 16], F32)
        s("dtbT", [128, 16], F32)
        s("wfillT", [128, 4 * 16], F32)
        s("fpowT", [128, 4 * 16], F32)
        s("PFT", [128, 16], F32)
        for i in range(8):
            p(f"pb{i}", [128, 512], F32)
        self.tiles["pbc"] = self.tiles["pb0"]
        self.tiles["pj0"] = self.tiles["pb1"]
        self.tiles["pj1"] = self.tiles["pb2"]
        self.tiles["pcv"] = self.tiles["pb3"]
        self.tiles["pss"] = self.tiles["pb4"]

    def stage0_consts(self):
        nc, d = self.nc, self.dram
        v = lambda n: self.tiles[n][0]
        t = lambda n: self.tiles[n][1]
        self.dma("qsync", v("idsb")[:, :], d["c_ident"], writes=[t("idsb")])
        hm, hmt = v("hd_misc"), t("hd_misc")
        ones, onest = self.sbuf("onescol", [128, 2], F32)
        def f(e):
            return e.memset(ones[:, :], 1.0)
        self.op("vector", f, writes=[onest])
        onesrow, onesrowt = self.sbuf("onesrow", [1, 128], BF16)
        def f(e):
            return e.memset(onesrow[:, :], 1.0)
        self.op("vector", f, writes=[onesrowt])
        onesrowf, onesrowft = self.sbuf("onesrowf", [1, 128], F32)
        def f(e):
            return e.memset(onesrowf[:, :], 1.0)
        self.op("vector", f, writes=[onesrowft])
        self.dma("qsync", hm[0:16, 129:133], d["mlog"], writes=[hmt])
        def f(e):
            return e.activation(hm[0:16, 134:138], hm[0:16, 129:133], AF.Exp)
        self.op("scalar", f, reads=[hmt], writes=[hmt])
        def f(e):
            return e.reduce_sum(hm[0:16, 138:139], hm[0:16, 134:138], mybir.AxisListType.X)
        self.op("vector", f, reads=[hmt], writes=[hmt])
        def f(e):
            return e.reciprocal(hm[0:16, 139:140], hm[0:16, 138:139])
        self.op("vector", f, reads=[hmt], writes=[hmt])
        def f(e):
            return e.tensor_scalar(hm[0:16, 134:138], hm[0:16, 134:138],
                                   hm[0:16, 139:140], None, OP.mult)
        self.op("vector", f, reads=[hmt], writes=[hmt], hard=[hmt], tag="wmix")
        m1, m1t = self.sbuf("misc1", [4, 512], F32)
        for r in range(4):
            self.dma("qpool", m1[0:1, 64 + 16 * r:80 + 16 * r],
                     hm[0:16, 134 + r:135 + r],
                     reads=[hmt], writes=[m1t], tag=f"wmixrow{r}")
        self.dma("qsync", m1[0:1, 16:32], d["alog"], writes=[m1t])
        def f(e):
            return e.activation(m1[0:1, 16:32], m1[0:1, 16:32], AF.Exp)
        self.op("scalar", f, reads=[m1t], writes=[m1t], tag="A")
        def f(e):
            return e.tensor_scalar(m1[0:1, 32:48], m1[0:1, 16:32], -1.0, None, OP.mult)
        self.op("vector", f, reads=[m1t], writes=[m1t])
        pbc, pbct = self.tiles["pbc"]
        def f(e):
            return e.matmul(pbc[:, 0:16], onesrowf[0:1, :], m1[0:1, 32:48],
                            start=True, stop=True)
        self.op("tensor", f, reads=[m1t, onesrowft], writes=[pbct])
        ab, abt = v("AbT"), t("AbT")
        def f(e):
            return e.tensor_copy(ab[:, 0:16], pbc[:, 0:16])
        self.op("vector", f, reads=[pbct], writes=[abt], tag="AbT")
        db, dbt = v("dtbT"), t("dtbT")
        self.dma("qsync", db[:, 0:16], d["dtb_hd"].transpose([1, 0]),
                 writes=[dbt], tag="dtbT", slow=True)
        lf, lft = v("lnfT"), t("lnfT")
        sc1, sc1t = self.sbuf("csc1", [128, 16], F32)
        def f(e):
            return e.activation(sc1[:, 0:16], db[:, 0:16], AF.Abs, bias=np.float32(-1e4))
        self.op("scalar", f, reads=[dbt], writes=[sc1t])
        def f(e):
            return e.activation(sc1[:, 0:16], sc1[:, 0:16], AF.Exp, scale=np.float32(-1.0))
        self.op("scalar", f, reads=[sc1t], writes=[sc1t])
        def f(e):
            return e.activation(sc1[:, 0:16], sc1[:, 0:16], AF.Ln, bias=np.float32(1.0))
        self.op("scalar", f, reads=[sc1t], writes=[sc1t])
        def f(e):
            return e.activation(lf[:, 0:16], db[:, 0:16], AF.Relu, bias=np.float32(-1e4))
        self.op("scalar", f, reads=[dbt], writes=[lft])
        def f(e):
            return e.tensor_add(lf[:, 0:16], lf[:, 0:16], sc1[:, 0:16])
        self.op("vector", f, reads=[lft, sc1t], writes=[lft])
        def f(e):
            return e.tensor_mul(lf[:, 0:16], lf[:, 0:16], ab[:, 0:16])
        self.op("vector", f, reads=[lft, abt], writes=[lft])
        def f(e):
            return e.tensor_scalar(lf[:, 0:16], lf[:, 0:16], -2.0, None, OP.max)
        self.op("vector", f, reads=[lft], writes=[lft], tag="lnfT")
        fp, fpt = v("fpowT"), t("fpowT")
        def f(e):
            return e.memset(fp[:, 0:16], 1.0)
        self.op("vector", f, writes=[fpt])
        def f(e):
            return e.activation(fp[:, 16:32], lf[:, 0:16], AF.Exp)
        self.op("scalar", f, reads=[lft], writes=[fpt], tag="fillT")
        for r in (2, 3):
            def f(e, r=r):
                return e.tensor_mul(fp[:, 16 * r:16 * r + 16],
                                    fp[:, 16 * (r - 1):16 * r], fp[:, 16:32])
            self.op("vector", f, reads=[fpt], writes=[fpt])
        wf, wft = v("wfillT"), t("wfillT")
        pf, pft = v("PFT"), t("PFT")
        for r in range(4):
            def f(e, r=r):
                return e.matmul(pbc[:, 32 + 16 * r:48 + 16 * r], onesrowf[0:1, :],
                                m1[0:1, 64 + 16 * r:80 + 16 * r], start=True, stop=True)
            self.op("tensor", f, reads=[m1t, onesrowft], writes=[pbct])
        for r in range(4):
            def f(e, r=r):
                return e.tensor_mul(wf[:, 16 * r:16 * r + 16],
                                    fp[:, 16 * r:16 * r + 16],
                                    pbc[:, 32 + 16 * r:48 + 16 * r])
            self.op("vector", f, reads=[fpt, pbct], writes=[wft])
        def f(e):
            return e.tensor_scalar(wf[:, 0:64], wf[:, 0:64], float(DH ** -0.5),
                                   None, OP.mult)
        self.op("vector", f, reads=[wft], writes=[wft])
        def f(e):
            return e.tensor_add(pf[:, 0:16], wf[:, 0:16], wf[:, 16:32])
        self.op("vector", f, reads=[wft], writes=[pft])
        def f(e):
            return e.tensor_add(pf[:, 0:16], pf[:, 0:16], wf[:, 32:48])
        self.op("vector", f, reads=[wft, pft], writes=[pft])
        def f(e):
            return e.tensor_add(pf[:, 0:16], pf[:, 0:16], wf[:, 48:64])
        self.op("vector", f, reads=[wft, pft], writes=[pft], tag="PFT")

    def stage0_ln(self):
        nc, d = self.nc, self.dram
        TX = self.TX
        v = lambda n: self.tiles[n][0]
        t = lambda n: self.tiles[n][1]
        xch, xcht = v("xchunk"), t("xchunk")
        pst, pstt = self.tiles["pb5"]
        ps2, ps2t = self.tiles["pb6"]
        sr, srt = self.tiles["statrow"]
        pbc, pbct = self.tiles["pbc"]
        xn, xnt = v("xn"), t("xn")
        ft, ftt = v("ftmp"), t("ftmp")
        idsb, idsbt = self.tiles["idsb"]
        ntile = (TX + 511) // 512
        ones, onest = self.tiles["onescol"]
        onesrow, onesrowt = self.tiles["onesrow"]
        for ti in range(ntile):
            c0 = ti * 512
            cw = min(512, TX - c0)
            for ic in range(NIC):
                buf = (ic % 2) * 512
                self.dma("qsync", xch[:, buf:buf + cw],
                         d["x_t"][ic * 128:(ic + 1) * 128, c0:c0 + cw],
                         writes=[xcht], tag=f"ln.x.{ti}.{ic}")
                def f(e, buf=buf, cw=cw, ic=ic):
                    return e.matmul(pst[0:1, 0:cw], ones[:, 0:1], xch[:, buf:buf + cw],
                             start=(ic == 0), stop=(ic == NIC - 1))
                self.op("tensor", f, reads=[xcht, onest], writes=[pstt])
                def f(e, buf=buf, cw=cw):
                    return e.tensor_mul(ft[:, 0:cw], xch[:, buf:buf + cw], xch[:, buf:buf + cw])
                self.op("vector", f, reads=[xcht], writes=[ftt])
                def f(e, cw=cw, ic=ic):
                    return e.matmul(ps2[0:1, 0:cw], ones[:, 1:2], ft[:, 0:cw],
                             start=(ic == 0), stop=(ic == NIC - 1))
                self.op("tensor", f, reads=[ftt, onest], writes=[ps2t])
            def f(e, cw=cw):
                return e.tensor_scalar(sr[0:1, 0:cw], pst[0:1, 0:cw],
                                float(1.0 / HID), None, OP.mult)
            self.op("vector", f, reads=[pstt], writes=[srt])
            def f(e, cw=cw):
                return e.tensor_mul(sr[0:1, 1024:1024 + cw], sr[0:1, 0:cw], sr[0:1, 0:cw])
            self.op("vector", f, reads=[srt], writes=[srt])
            def f(e, cw=cw):
                return e.scalar_tensor_tensor(sr[0:1, 1024:1024 + cw], ps2[0:1, 0:cw],
                                       float(1.0 / HID), sr[0:1, 1024:1024 + cw],
                                       OP.mult, OP.subtract)
            self.op("vector", f, reads=[ps2t, srt], writes=[srt], tag="var")
            def f(e, cw=cw):
                return e.activation(sr[0:1, 1024:1024 + cw], sr[0:1, 1024:1024 + cw],
                             AF.Ln, bias=np.float32(EPS))
            self.op("scalar", f, reads=[srt], writes=[srt])
            def f(e, cw=cw):
                return e.activation(sr[0:1, 512:512 + cw], sr[0:1, 1024:1024 + cw],
                             AF.Exp, scale=np.float32(-0.5))
            self.op("scalar", f, reads=[srt], writes=[srt], tag="rstd")
            rb, rbt = self.tiles["rowbf"]
            def f(e, cw=cw):
                return e.tensor_copy(rb[0:1, 0:cw], sr[0:1, 0:cw])
            self.op("vector", f, reads=[srt], writes=[rbt])
            def f(e, cw=cw):
                return e.tensor_copy(rb[0:1, 512:512 + cw], sr[0:1, 512:512 + cw])
            self.op("vector", f, reads=[srt], writes=[rbt])
            def f(e, cw=cw):
                return e.matmul(pbc[:, 0:cw], onesrow[0:1, :], rb[0:1, 0:cw],
                         start=True, stop=True)
            self.op("tensor", f, reads=[rbt, onesrowt], writes=[pbct], tag="bc.mu")
            pb2, pb2t = self.tiles["pj1"]
            def f(e, cw=cw):
                return e.matmul(pb2[:, 0:cw], onesrow[0:1, :], rb[0:1, 512:512 + cw],
                         start=True, stop=True)
            self.op("tensor", f, reads=[rbt, onesrowt], writes=[pb2t], tag="bc.rs")
            for ic in range(NIC):
                buf = (ic % 2) * 512
                self.dma("qsync", xch[:, buf:buf + cw],
                         d["x_t"][ic * 128:(ic + 1) * 128, c0:c0 + cw],
                         writes=[xcht], tag=f"ln.x2.{ti}.{ic}")
                def f(e, buf=buf, cw=cw):
                    return e.tensor_sub(ft[:, 0:cw], xch[:, buf:buf + cw], pbc[:, 0:cw])
                self.op("vector", f, reads=[xcht, pbct], writes=[ftt])
                def f(e, ic=ic, c0=c0, cw=cw):
                    return e.tensor_tensor(xn[:, ic * TX + c0: ic * TX + c0 + cw],
                                    ft[:, 0:cw], pb2[:, 0:cw], OP.mult)
                self.op("vector", f, reads=[ftt, pb2t], writes=[xnt], tag="xn")

    def stage1(self):
        nc, d = self.nc, self.dram
        TX, TW, T_SL = self.TX, self.TW, self.T_SL
        v = lambda n: self.tiles[n][0]
        t = lambda n: self.tiles[n][1]
        xn, xnt = v("xn"), t("xn")
        wtf, wtft = v("wtf"), t("wtf")
        wtb, wtbt = v("wtb"), t("wtb")
        ybuf, ybuft = v("ybuf"), t("ybuf")
        convd, convdt = v("convd"), t("convd")
        cwc, cwct = v("cwcol"), t("cwcol")
        idsb, idsbt = self.tiles["idsb"]
        onescolb, onescolbt = self.sbuf("onescolb", [128, 1], BF16)
        def f(e):
            return e.memset(onescolb[:, :], 1.0)
        self.op("vector", f, writes=[onescolbt])
        rtmp, rtmpt = self.sbuf("rtmp", [1, 1024], F32)
        g2bT, g2bTt = self.sbuf("g2bT", [128, 16], F32)
        self.dma("qsync", g2bT[:, 0:16],
                 d["g2b"].rearrange("(i p) o -> p (i o)", p=128),
                 writes=[g2bTt], slow=True, tag="g2bT")

        jobs = []
        for oc in range(16):
            jobs.append(("q", oc))
        for oc in range(64):
            jobs.append(("k", oc))
        for oc in range(64):
            jobs.append(("v", oc))
        jobs.append(("b", 0))
        jobs.append(("f1", 0))
        for oc in range(16):
            jobs.append(("f2", oc))
        jobs.append(("g1", 0))
        for oc in range(16):
            jobs.append(("g2", oc))

        W = {"q": ("wq", 128), "k": ("wk", 128), "v": ("wv", 128),
             "b": ("wb", 64), "f1": ("wf1", 128), "f2": ("wf2", 128),
             "g1": ("wg1", 128), "g2": ("wg2", 128)}

        WQRR = ("qsync", "qpool")
        def wload(jid, name, oc):
            wn, ww = W[name]
            buf = (jid % 2) * 2048
            nicj = 1 if name in ("f2", "g2") else NIC
            src = d[wn].rearrange("(i p) o -> p i o", p=128)[:, :, oc * ww:(oc + 1) * ww]
            self.dma(WQRR[jid % 2], wtf[:, buf:buf + nicj * ww].rearrange(
                "p (i o) -> p i o", i=nicj), src, writes=[wtft],
                tag=f"wl.{name}{oc}")
            def f(e, buf=buf, n=nicj * ww):
                return e.tensor_copy(wtb[:, buf:buf + n], wtf[:, buf:buf + n])
            self.op("gpsimd", f, reads=[wtft], writes=[wtbt], tag=f"wc.{name}{oc}")
            return buf, nicj, ww

        pend = {}
        pend[0] = wload(0, *jobs[0])
        for jid, (name, oc) in enumerate(jobs):
            if jid + 1 < len(jobs):
                pend[jid + 1] = wload(jid + 1, *jobs[jid + 1])
            buf, nicj, ww = pend.pop(jid)
            shift = 0 if name in ("q", "k", "v", "f1", "g1") else 3
            isconv = name in ("q", "k", "v")
            pj, pjt = self.tiles["pj0" if jid % 2 == 0 else "pj1"]
            if name in ("f2",):
                mov, movt = v("y1"), t("y1")
            elif name in ("g2",):
                mov, movt = v("yg1"), t("yg1")
            else:
                mov, movt = xn, xnt
            if name == "g2":
                tiles = [(self.warm + 3 + i, min(512, TX - (self.warm + 3 + i)))
                         for i in range(0, T_SL, 512)]
            else:
                tiles = [(shift + i, min(512, TX - shift - i))
                         for i in range(0, TX - shift, 512)]
            for ti, (c0, cw) in enumerate(tiles):
                for ic in range(nicj):
                    if name in ("f2", "g2"):
                        mv = mov[:, c0:c0 + cw]
                    else:
                        mv = mov[:, ic * TX + c0:ic * TX + c0 + cw]
                    def f(e, buf=buf, ic=ic, ww=ww, mv=mv, cw=cw, nicj=nicj, pj=pj):
                        return e.matmul(pj[0:ww, 0:cw],
                                        wtb[:, buf + ic * ww:buf + (ic + 1) * ww],
                                        mv, start=(ic == 0), stop=(ic == nicj - 1))
                    self.op("tensor", f, reads=[wtbt, movt], writes=[pjt],
                            tag=f"mm.{name}{oc}.{ti}")
                if isconv:
                    def f(e, c0=c0, cw=cw, pj=pj):
                        return e.tensor_copy(ybuf[:, c0:c0 + cw], pj[:, 0:cw])
                    self.op("vector", f, reads=[pjt], writes=[ybuft])
                elif name == "f1":
                    y1v, y1t = v("y1"), t("y1")
                    def f(e, c0=c0, cw=cw, pj=pj):
                        return e.tensor_copy(y1v[:, c0:c0 + cw], pj[:, 0:cw])
                    self.op("vector", f, reads=[pjt], writes=[y1t])
                elif name == "g1":
                    yg, ygt = v("yg1"), t("yg1")
                    def f(e, c0=c0, cw=cw, pj=pj):
                        return e.tensor_copy(yg[:, c0:c0 + cw], pj[:, 0:cw])
                    self.op("vector", f, reads=[pjt], writes=[ygt])
                elif name == "b":
                    self.ep_beta(pj, pjt, c0 - 3, cw)
                elif name == "f2":
                    self.ep_L(pj, pjt, oc, c0 - 3, cw)
                elif name == "g2":
                    self.ep_gate(pj, pjt, oc, c0 - (self.warm + 3), cw, g2bT, g2bTt)
            if isconv:
                self.ep_conv(name, oc)

    def ep_conv(self, name, oc):
        nc, d = self.nc, self.dram
        TW, TX = self.TW, self.TX
        v = lambda n: self.tiles[n][0]
        t = lambda n: self.tiles[n][1]
        ybuf, ybuft = v("ybuf"), t("ybuf")
        convd, convdt = v("convd"), t("convd")
        cwc, cwct = v("cwcol"), t("cwcol")
        idsb, idsbt = self.tiles["idsb"]
        pcv, pcvt = self.tiles["pcv"]
        pss, psst = self.tiles["pss"]
        ft, ftt = v("ftmp"), t("ftmp")
        ft2, ft2t = v("ftmp2"), t("ftmp2")
        kt, ktt = v("ktile"), t("ktile")
        sq, sqt = v("sq"), t("sq")
        rt, rtt = v("rtmp"), t("rtmp")
        onescolb, onescolbt = self.tiles["onescolb"]
        cwn = {"q": "qcw", "k": "kcw", "v": "vcw"}[name]
        self.dma("qsync", cwc[:, 0:4], d[cwn][oc * 128:(oc + 1) * 128, :],
                 writes=[cwct], tag=f"cw.{name}{oc}")
        for i in range(4):
            def f(e, i=i):
                return e.tensor_scalar(convd[:, i * 128:(i + 1) * 128],
                                       idsb[:, :], cwc[:, i:i + 1], None, OP.mult)
            self.op("vector", f, reads=[idsbt], writes=[convdt], hard=[cwct],
                    tag=f"diag.{name}{oc}.{i}")
        for tt0 in range(0, TW, 512):
            cwt = min(512, TW - tt0)
            for i in range(4):
                def f(e, tt0=tt0, i=i, cwt=cwt):
                    return e.matmul(pcv[:, 0:cwt], convd[:, i * 128:(i + 1) * 128],
                                    ybuf[:, tt0 + i:tt0 + i + cwt],
                                    start=(i == 0), stop=(i == 3))
                self.op("tensor", f, reads=[convdt, ybuft], writes=[pcvt],
                        tag=f"cv.{name}{oc}.{tt0}")
            def f(e, cwt=cwt):
                return e.activation(ft[:, 0:cwt], pcv[:, 0:cwt], AF.Silu)
            self.op("scalar", f, reads=[pcvt], writes=[ftt], tag="silu")
            def f(e, cwt=cwt):
                return e.tensor_copy(kt[:, 0:cwt], ft[:, 0:cwt])
            self.op("vector", f, reads=[ftt], writes=[ktt])
            if name == "v":
                vh, vht = v("vhi"), t("vhi")
                vl, vlt = v("vlo"), t("vlo")
                def f(e, cwt=cwt):
                    return e.tensor_copy(vh[:, 0:cwt], ft[:, 0:cwt])
                self.op("vector", f, reads=[ftt], writes=[vht])
                def f(e, cwt=cwt):
                    return e.scalar_tensor_tensor(vl[:, 0:cwt], vh[:, 0:cwt], -1.0,
                                                  ft[:, 0:cwt], OP.mult, OP.add)
                self.op("vector", f, reads=[vht, ftt], writes=[vlt])
                self.dma("qact", d["vsph"][oc * 128:(oc + 1) * 128, tt0:tt0 + cwt],
                         vh[:, 0:cwt], reads=[vht], tag=f"sp.vh{oc}")
                self.dma("qact", d["vspl"][oc * 128:(oc + 1) * 128, tt0:tt0 + cwt],
                         vl[:, 0:cwt], reads=[vlt], tag=f"sp.vl{oc}")
            else:
                dst = "qsp" if name == "q" else "ksp"
                self.dma("qact", d[dst][oc * 128:(oc + 1) * 128, tt0:tt0 + cwt],
                         kt[:, 0:cwt], reads=[ktt], tag=f"sp.{name}{oc}")
                def f(e, cwt=cwt):
                    return e.tensor_mul(sq[:, 0:cwt], kt[:, 0:cwt], kt[:, 0:cwt])
                self.op("vector", f, reads=[ktt], writes=[sqt])
                def f(e, cwt=cwt):
                    return e.matmul(pss[0:1, 0:cwt], onescolb[:, 0:1], sq[:, 0:cwt],
                                    start=True, stop=True)
                self.op("tensor", f, reads=[sqt, onescolbt], writes=[psst])
                def f(e, cwt=cwt):
                    return e.activation(rt[0:1, 0:cwt], pss[0:1, 0:cwt], AF.Ln,
                                        bias=np.float32(1e-6))
                self.op("scalar", f, reads=[psst], writes=[rtt])
                def f(e, cwt=cwt):
                    return e.tensor_scalar(rt[0:1, 512:512 + cwt], rt[0:1, 0:cwt],
                                           -0.5, None, OP.mult)
                self.op("vector", f, reads=[rtt], writes=[rtt])
                rowd = "rowq" if name == "q" else "rowk"
                self.dma("qact", d[rowd][oc:oc + 1, tt0:tt0 + cwt],
                         rt[0:1, 512:512 + cwt], reads=[rtt], tag=f"row.{name}{oc}")

    def ep_beta(self, pj, pjt, tt0, cw):
        d = self.dram
        v = lambda n: self.tiles[n][0]
        t = lambda n: self.tiles[n][1]
        ft, ftt = v("ftmp"), t("ftmp")
        ft2, ft2t = v("ftmp2"), t("ftmp2")
        def f(e):
            return e.activation(ft[0:64, 0:cw], pj[0:64, 0:cw], AF.Relu,
                                scale=np.float32(-1.0))
        self.op("scalar", f, reads=[pjt], writes=[ftt])
        def f(e):
            return e.activation(ft2[0:64, 0:cw], pj[0:64, 0:cw], AF.Abs)
        self.op("scalar", f, reads=[pjt], writes=[ft2t])
        def f(e):
            return e.activation(ft2[0:64, 0:cw], ft2[0:64, 0:cw], AF.Exp,
                                scale=np.float32(-1.0))
        self.op("scalar", f, reads=[ft2t], writes=[ft2t])
        def f(e):
            return e.activation(ft2[0:64, 0:cw], ft2[0:64, 0:cw], AF.Ln,
                                bias=np.float32(1.0))
        self.op("scalar", f, reads=[ft2t], writes=[ft2t])
        def f(e):
            return e.scalar_tensor_tensor(ft[0:64, 0:cw], ft[0:64, 0:cw], -1.0,
                                          ft2[0:64, 0:cw], OP.mult, OP.subtract)
        self.op("vector", f, reads=[ftt, ft2t], writes=[ftt], tag="lnb")
        self.dma("qact", d["lnb"][0:64, tt0:tt0 + cw], ft[0:64, 0:cw],
                 reads=[ftt], tag="sp.lnb")

    def ep_L(self, pj, pjt, h, tt0, cw):
        d = self.dram
        v = lambda n: self.tiles[n][0]
        t = lambda n: self.tiles[n][1]
        ft, ftt = v("ftmp"), t("ftmp")
        ft2, ft2t = v("ftmp2"), t("ftmp2")
        db, dbt = v("dtbT"), t("dtbT")
        ab, abt = v("AbT"), t("AbT")
        def f(e):
            return e.activation(ft[:, 0:cw], pj[:, 0:cw], AF.Relu,
                                bias=db[:, h:h + 1])
        self.op("scalar", f, reads=[pjt], writes=[ftt], hard=[dbt])
        def f(e):
            return e.activation(ft2[:, 0:cw], pj[:, 0:cw], AF.Abs,
                                bias=db[:, h:h + 1])
        self.op("scalar", f, reads=[pjt], writes=[ft2t], hard=[dbt])
        def f(e):
            return e.activation(ft2[:, 0:cw], ft2[:, 0:cw], AF.Exp,
                                scale=np.float32(-1.0))
        self.op("scalar", f, reads=[ft2t], writes=[ft2t])
        def f(e):
            return e.activation(ft2[:, 0:cw], ft2[:, 0:cw], AF.Ln,
                                bias=np.float32(1.0))
        self.op("scalar", f, reads=[ft2t], writes=[ft2t])
        def f(e):
            return e.tensor_add(ft[:, 0:cw], ft[:, 0:cw], ft2[:, 0:cw])
        self.op("vector", f, reads=[ftt, ft2t], writes=[ftt])
        def f(e):
            return e.tensor_scalar(ft[:, 0:cw], ft[:, 0:cw], ab[:, h:h + 1],
                                   -17.0, OP.mult, OP.max)
        self.op("vector", f, reads=[ftt], writes=[ftt], hard=[abt], tag="L")
        self.dma("qact", d["lsp"][h * 128:(h + 1) * 128, tt0:tt0 + cw],
                 ft[:, 0:cw], reads=[ftt], tag=f"sp.L{h}")

    def ep_gate(self, pj, pjt, oc, c0out, cw, g2bT, g2bTt):
        d = self.dram
        v = lambda n: self.tiles[n][0]
        t = lambda n: self.tiles[n][1]
        vh, vht = v("vhi"), t("vhi")
        def f(e):
            return e.tensor_scalar(vh[:, 0:cw], pj[:, 0:cw],
                                   g2bT[:, oc:oc + 1], None, OP.add)
        self.op("vector", f, reads=[pjt], writes=[vht], hard=[g2bTt], tag="gate")
        self.dma("qact", d["gsp"][oc * 128:(oc + 1) * 128, c0out:c0out + cw],
                 vh[:, 0:cw], reads=[vht], tag=f"sp.g{oc}")

    def stage2(self):
        nc, d = self.nc, self.dram
        TW, T_SL = self.TW, self.T_SL
        nch = TW // CT
        v = lambda n: self.tiles[n][0]
        t = lambda n: self.tiles[n][1]
        s = self.sbuf
        NG = 4
        Sf, Sft = s("Sf", [128, H * 128], F32)
        Sb, Sbt = s("Sb", [128, H * 128], BF16)
        ob, obt = s("obuf", [128, H * T_SL], BF16)
        self.sfT = [Tile(f"Sf{g}") for g in range(4)]
        self.sbT = [Tile(f"Sb{g}") for g in range(4)]
        self.poT = [Tile("pO0"), Tile("pO1")]
        def f(e):
            return e.memset(Sf[:, :], 0.0)
        self.op("vector", f, writes=self.sfT)
        def f(e):
            return e.memset(Sb[:, :], 0.0)
        self.op("vector", f, writes=self.sbT)
        self.sl = {}
        for j in range(NG):
            for nm, wmul, dt_ in [("kh", 4, BF16), ("vh", 4, BF16), ("vl", 4, BF16),
                                  ("vhb", 4, BF16), ("vlb", 4, BF16),
                                  ("KQ", 8, BF16), ("WH", 4, BF16), ("KG", 4, BF16),
                                  ("WG", 4, BF16),
                                  ("qs", 1, BF16), ("Ls", 1, F32), ("Ct", 1, None),
                                  ("G0", 1, F32), ("Ga", 1, F32), ("G0r", 1, F32),
                                  ("QB", 1, BF16), ("dex", 1, None),
                                  ("rows", 1, None), ("LAM", 1, F32),
                                  ("rln", 1, None)]:
                if nm == "rows":
                    self.sl[(nm, j)] = s(f"sl_{nm}{j}", [1, 9 * SLABCH * CT], F32)
                elif nm == "Ct":
                    self.sl[(nm, j)] = s(f"sl_{nm}{j}", [128, SLABCH * CT + 1], F32)
                elif nm == "dex":
                    self.sl[(nm, j)] = s(f"sl_{nm}{j}", [128, 16 * SLABCH], F32)
                elif nm == "rln":
                    self.sl[(nm, j)] = s(f"sl_{nm}{j}", [128, 4], F32)
                elif nm == "LAM":
                    for st in range(2):
                        self.sl[(nm, j, st)] = s(f"sl_{nm}{j}_{st}", [128, 16], F32)
                elif nm == "QB":
                    for st in range(2):
                        self.sl[(nm, j, st)] = s(f"sl_{nm}{j}_{st}",
                                                 [128, wmul * SLABCH * CT], dt_)
                else:
                    self.sl[(nm, j)] = s(f"sl_{nm}{j}", [128, wmul * SLABCH * CT], dt_)
        for nm, shp, dt_ in [("arg", [128, SLABCH * CMICRO], F32),
                             ("att", [128, SLABCH * CT], F32),
                             ("att2", [128, SLABCH * CT], F32),
                             ("anch", [128, 4], F32),
                             ("dexp", [128, 16], F32),
                             ("Wv", [128, 1024], BF16),
                             ("Nd", [128, 2048], BF16),
                             ("Yp", [128, 512], BF16),
                             ("Uhi", [128, 256], BF16), ("Ulo", [128, 256], BF16),
                             ("UhiU", [128, 256], BF16), ("UloU", [128, 256], BF16),
                             ("WgT", [128, 256], BF16), ("ApT", [128, 64], BF16),
                             ("Ngap", [128, 256], BF16), ("Agap", [128, 256], BF16),
                             ("mM", [128, CMICRO], BF16), ("mA", [128, CMICRO], BF16),
                             ("idg", [128, CMICRO], BF16), ("ppool", [128, CT], BF16)]:
            s(nm, shp, dt_)
        self.dma("qsync", v("mM")[:, :], d["c_maskM"], writes=[t("mM")])
        self.dma("qsync", v("mA")[:, :], d["c_maskA"], writes=[t("mA")])
        self.dma("qsync", v("idg")[:, :], d["c_identg"], writes=[t("idg")])
        self.dma("qsync", v("ppool")[:, :], d["c_ppool"], writes=[t("ppool")])
        wtfB = v("wtf").bitcast(BF16)
        self.s2buf = {}
        off = 0
        for nm, shp in [("Nd", [128, 2048]), ("Wv", [128, 1024]),
                        ("Ngap", [128, 256]), ("Agap", [128, 256]),
                        ("WgT", [128, 256]), ("ApT", [128, 64]),
                        ("Yp", [128, 512]), ("Uhi", [128, 256]),
                        ("Ulo", [128, 256]), ("UhiU", [128, 256]),
                        ("UloU", [128, 256])]:
            t1 = Tile(nm + "B1")
            ap1 = wtfB[0:shp[0], off:off + shp[1]]
            off += shp[1]
            self.s2buf[nm] = [(v(nm), t(nm)), (ap1, t1)]
        self.s2b1t = [self.s2buf[nm][1][1]
                      for nm in ("Nd", "Wv", "Ngap", "Agap", "WgT", "ApT",
                                 "Yp", "Uhi", "Ulo", "UhiU", "UloU")]
        ftm = v("ftmp")
        def f(e):
            return e.memset(ftm[0:1, 0:1], 0.0)
        self.op("vector", f, writes=[t("wtf")] + self.s2b1t + [t("ftmp")],
                tag="fenceA")
        pTz = self.tiles["pb2"][0]
        def f(e):
            return e.memset(pTz[:, 0:512], 0.0)
        self.op("vector", f, writes=[self.tiles["pb2"][1]])
        pAz = self.tiles["pb1"][0]
        def f(e):
            return e.memset(pAz[:, 256:320], 0.0)
        self.op("vector", f, writes=[self.tiles["pb1"][1]])
        for nmz in ("UhiU", "UloU"):
            for bz in range(2):
                vz, tz = self.s2buf[nmz][bz]
                def f(e, vz=vz):
                    return e.memset(vz[0:64, 0:256], 0.0)
                self.op("vector", f, writes=[tz])
        for nmz in ("Uhi", "Ulo"):
            for bz in range(2):
                vz, tz = self.s2buf[nmz][bz]
                def f(e, vz=vz):
                    return e.memset(vz[64:128, 0:256], 0.0)
                self.op("vector", f, writes=[tz])
        blocks = []
        for slab in range((nch + SLABCH - 1) // SLABCH):
            m0 = slab * SLABCH
            nm_ = min(SLABCH, nch - m0)
            for hg in range(H // NG):
                blocks.append((m0, nm_, hg))

        def emit_rounds(rounds):
            for rnd in rounds:
                for th in rnd:
                    th()

        def loads_preps(bi):
            bm0, bnm, bhg = blocks[bi]
            for j in range(NG):
                h = bhg * NG + j
                self.s2_load_slab(h, j, bm0, bnm)
                self.s2_prep(h, j, bm0, bnm, bi % 2)

        loads_preps(0)
        for bi, (m0, nm_, hg) in enumerate(blocks):
            pend = []
            for mi in range(nm_):
                A, Bh, Bt = self.s2_chunk_phases(hg, m0 + mi, mi, nm_,
                                                 (m0 + mi) % 2, bi % 2)
                na, nb = len(A), len(pend)
                for i in range(max(na, nb)):
                    if i < na:
                        for th in A[i]:
                            th()
                    if i < nb:
                        for th in pend[i]:
                            th()
                if mi < nm_ - 1:
                    pend = Bh + Bt
                else:
                    emit_rounds(Bh)
                    if bi + 1 < len(blocks):
                        loads_preps(bi + 1)
                    emit_rounds(Bt)

    def s2_load_slab(self, h, j, m0, nm_):
        d = self.dram
        t0 = m0 * CT
        tw = nm_ * CT
        q2 = "qsync" if h % 2 == 0 else "qpool"
        def sv(nm):
            return self.sl[(nm, j)]
        for nm, src in [("kh", "ksp"), ("vh", "vsph"), ("vl", "vspl")]:
            vv, tt = sv(nm)
            ap = d[src][h * 512:(h + 1) * 512, t0:t0 + tw]
            ap = ap.rearrange("(r d) t -> d r t", d=128)
            out = vv[:, 0:4 * tw].rearrange("d (r t) -> d r t", r=4)
            self.dma(q2, out, ap, writes=[tt], tag=f"sl.{nm}{h}")
        vv, tt = sv("qs")
        self.dma(q2, vv[:, 0:tw], d["qsp"][h * 128:(h + 1) * 128, t0:t0 + tw],
                 writes=[tt], tag=f"sl.q{h}")
        vv, tt = sv("Ls")
        self.dma(q2, vv[:, 0:tw], d["lsp"][h * 128:(h + 1) * 128, t0:t0 + tw],
                 writes=[tt], tag=f"sl.L{h}")
        vv, tt = sv("rows")
        W8 = SLABCH * CT
        self.dma("qsync", vv[0:1, 0:4 * tw].rearrange("o (r t) -> o r t", r=4),
                 d["rowk"][4 * h:4 * h + 4, t0:t0 + tw].unsqueeze(0),
                 writes=[tt], tag=f"sl.rk{h}")
        self.dma("qsync", vv[0:1, 4 * W8:4 * W8 + 4 * tw].rearrange("o (r t) -> o r t", r=4),
                 d["lnb"][4 * h:4 * h + 4, t0:t0 + tw].unsqueeze(0),
                 writes=[tt], tag=f"sl.lb{h}")
        self.dma("qsync", vv[0:1, 8 * W8:8 * W8 + tw],
                 d["rowq"][h:h + 1, t0:t0 + tw], writes=[tt], tag=f"sl.rq{h}")

    def s2_prep(self, h, j, m0, nm_, set_):
        nc = self.nc
        v = lambda n: self.tiles[n][0]
        t = lambda n: self.tiles[n][1]
        tw = nm_ * CT
        mw = 4 * tw
        W8 = SLABCH * CT
        def sv(nm):
            return self.sl[(nm, j)]
        kh, kht = sv("kh"); vhh, vht = sv("vh"); vll, vlt = sv("vl")
        qs, qst = sv("qs"); Ls, Lst = sv("Ls"); Ct, Ctt = sv("Ct")
        G0, G0t = sv("G0"); Ga, Gat = sv("Ga"); rows, rowst = sv("rows")
        rln, rlnt = sv("rln"); LAM, LAMt = self.sl[("LAM", j, set_)]
        KQ, KQt = sv("KQ"); WH, WHt = sv("WH"); KG, KGt = sv("KG")
        WG, WGt = sv("WG"); QB, QBt = self.sl[("QB", j, set_)]
        dx, dxt = sv("dex")
        art, artt = v("arg"), t("arg")
        att, attt = v("att"), t("att")
        at2, at2t = v("att2"), t("att2")
        lf, lft = v("lnfT"), t("lnfT")
        pfT, pfTt = v("PFT"), t("PFT")
        wfT, wfTt = v("wfillT"), t("wfillT")
        onesrowf, onesrowft = self.tiles["onesrowf"]
        pK, pKt = self.tiles["pb5"]
        pW, pWt = self.tiles["pb7"]
        pQ, pQt = self.tiles["pb7"][0][:, 256:512], self.tiles["pb7"][1]
        O = self.op
        def f(e):
            return e.memset(rln[:, 0:1], 0.0)
        O("vector", f, writes=[rlnt])
        for r in (1, 2, 3):
            def f(e, r=r):
                return e.tensor_scalar(rln[:, r:r + 1], lf[:, h:h + 1], float(r),
                                       None, OP.mult)
            O("vector", f, reads=[lft], writes=[rlnt])
        def f(e):
            return e.tensor_add(rows[0:1, 4 * W8:4 * W8 + 4 * tw],
                                rows[0:1, 4 * W8:4 * W8 + 4 * tw],
                                rows[0:1, 0:4 * tw])
        O("vector", f, reads=[rowst], writes=[rowst])
        def f(e):
            return e.matmul(pK[:, 0:mw], onesrowf[0:1, :], rows[0:1, 0:4 * tw],
                            start=True, stop=True)
        O("tensor", f, reads=[rowst, onesrowft], writes=[pKt])
        def f(e):
            return e.matmul(pW[:, 0:mw], onesrowf[0:1, :],
                            rows[0:1, 4 * W8:4 * W8 + 4 * tw], start=True, stop=True)
        O("tensor", f, reads=[rowst, onesrowft], writes=[pWt])
        def f(e):
            return e.matmul(pQ[:, 0:tw], onesrowf[0:1, :],
                            rows[0:1, 8 * W8:8 * W8 + tw], start=True, stop=True)
        O("tensor", f, reads=[rowst, onesrowft], writes=[pQt])
        def f(e):
            return e.memset(Ct[:, 0:1], 0.0)
        O("vector", f, writes=[Ctt])
        def f(e):
            return e.tensor_scalar(att[:, 0:tw], Ls[:, 0:tw], rln[:, 3:4], None,
                                   OP.add)
        O("vector", f, reads=[Lst], writes=[attt], hard=[rlnt])
        src, srct, dst, dstt = att, attt, at2, at2t
        sh = 1
        while sh < tw:
            def f(e, sh=sh, src=src, dst=dst):
                return e.tensor_copy(dst[:, 0:sh], src[:, 0:sh])
            O("vector", f, reads=[srct], writes=[dstt])
            def f(e, sh=sh, src=src, dst=dst):
                return e.tensor_add(dst[:, sh:tw], src[:, sh:tw], src[:, 0:tw - sh])
            O("vector", f, reads=[srct], writes=[dstt])
            src, srct, dst, dstt = dst, dstt, src, srct
            sh *= 2
        def f(e, src=src):
            return e.tensor_copy(Ct[:, 1:tw + 1], src[:, 0:tw])
        O("vector", f, reads=[srct], writes=[Ctt])
        def f(e):
            return e.tensor_scalar(G0[:, 0:tw], Ct[:, 1:tw + 1], rln[:, 3:4], None,
                                   OP.subtract)
        O("vector", f, reads=[Ctt], writes=[G0t], hard=[rlnt])
        G0r, G0rt = sv("G0r")
        cecv = Ct[:, 0:tw:CT].unsqueeze(2).broadcast_to([128, nm_, CT])
        def f(e, cecv=cecv):
            return e.tensor_sub(G0r[:, 0:tw].rearrange("d (m t) -> d m t", t=CT),
                                G0[:, 0:tw].rearrange("d (m t) -> d m t", t=CT),
                                cecv)
        O("vector", f, reads=[G0t, Ctt], writes=[G0rt])
        ancv = Ct[:, 0:tw:4].unsqueeze(2).broadcast_to([128, tw // 4, 4])
        def f(e, ancv=ancv):
            return e.tensor_sub(Ga[:, 0:tw].rearrange("d (b p) -> d b p", p=4),
                                G0[:, 0:tw].rearrange("d (b p) -> d b p", p=4),
                                ancv)
        O("vector", f, reads=[G0t, Ctt], writes=[Gat])
        def f(e):
            return e.tensor_sub(LAM[:, 8:8 + nm_], Ct[:, CT:tw + 1:CT],
                                Ct[:, 0:tw:CT])
        O("vector", f, reads=[Ctt], writes=[LAMt])
        def f(e):
            return e.tensor_scalar(LAM[:, 0:nm_], LAM[:, 8:8 + nm_],
                                   float(-87.0), None, OP.max)
        O("vector", f, reads=[LAMt], writes=[LAMt])
        def f(e):
            return e.activation(LAM[:, 0:nm_], LAM[:, 0:nm_], AF.Exp)
        O("scalar", f, reads=[LAMt], writes=[LAMt])
        a4 = Ct[:, 0:tw:4].rearrange("d (m I) -> d m I", I=4)
        minu = a4.unsqueeze(3).broadcast_to([128, nm_, 4, 4])
        subt = a4.unsqueeze(2).broadcast_to([128, nm_, 4, 4])
        dxv = dx[:, 0:16 * nm_].rearrange("d (m I i) -> d m I i", I=4, i=4)
        def f(e, dxv=dxv, minu=minu, subt=subt):
            return e.tensor_sub(dxv, minu, subt)
        O("vector", f, reads=[Ctt], writes=[dxt], tag="dexp.sub")
        def f(e):
            return e.tensor_scalar(dx[:, 0:16 * nm_], dx[:, 0:16 * nm_],
                                   float(-87.0), float(0.0), OP.max, OP.min)
        O("vector", f, reads=[dxt], writes=[dxt])
        def f(e):
            return e.activation(dx[:, 0:16 * nm_], dx[:, 0:16 * nm_], AF.Exp)
        O("scalar", f, reads=[dxt], writes=[dxt], tag="dexp")
        def rmaj(x):
            return x[:, 0:mw].rearrange("d (r m I p) -> d m I r p",
                                        r=4, m=nm_, I=4)
        def bmaj(x):
            return x[:, 0:mw].rearrange("d (m I r p) -> d m I r p",
                                        m=nm_, I=4, r=4)
        artv = art[:, 0:mw].rearrange("d (r t) -> d r t", r=4)
        pKv = pK[:, 0:mw].rearrange("d (r t) -> d r t", r=4)
        pWv = pW[:, 0:mw].rearrange("d (r t) -> d r t", r=4)
        rlnb = rln[:, 0:4].unsqueeze(2).broadcast_to([128, 4, tw])
        Gab = Ga[:, 0:tw].unsqueeze(1).broadcast_to([128, 4, tw])
        G0rb = G0r[:, 0:tw].unsqueeze(1).broadcast_to([128, 4, tw])
        def f(e):
            return e.tensor_tensor(artv, pKv, rlnb, OP.add)
        O("vector", f, reads=[pKt, rlnt], writes=[artt])
        def f(e):
            return e.tensor_tensor(artv, artv, Gab, OP.add)
        O("vector", f, reads=[artt, Gat], writes=[artt])
        def f(e):
            return e.activation(art[:, 0:mw], art[:, 0:mw], AF.Exp)
        O("scalar", f, reads=[artt], writes=[artt])
        kqK = KQ[:, 0:2 * mw].rearrange("d (m I h r p) -> d h m I r p",
                                        I=4, h=2, r=4, p=4)[:, 0]
        def f(e, kqK=kqK):
            return e.tensor_tensor(kqK, rmaj(kh), rmaj(art), OP.mult)
        O("vector", f, reads=[kht, artt], writes=[KQt], tag="KH")
        def f(e):
            return e.tensor_tensor(artv, pKv, rlnb, OP.add)
        O("vector", f, reads=[pKt, rlnt], writes=[artt])
        def f(e):
            return e.tensor_tensor(artv, artv, G0rb, OP.add)
        O("vector", f, reads=[artt, G0rt], writes=[artt])
        def f(e):
            return e.tensor_scalar(art[:, 0:mw], art[:, 0:mw], float(-87.0),
                                   None, OP.max)
        O("vector", f, reads=[artt], writes=[artt])
        def f(e):
            return e.activation(art[:, 0:mw], art[:, 0:mw], AF.Exp)
        O("scalar", f, reads=[artt], writes=[artt])
        def f(e):
            return e.tensor_scalar(art[:, 0:mw], art[:, 0:mw], -1.0, None, OP.mult)
        O("vector", f, reads=[artt], writes=[artt])
        def f(e):
            return e.tensor_tensor(bmaj(KG), rmaj(kh), rmaj(art), OP.mult)
        O("vector", f, reads=[kht, artt], writes=[KGt], tag="KG")
        def f(e):
            return e.tensor_tensor(artv, pWv, rlnb, OP.subtract)
        O("vector", f, reads=[pWt, rlnt], writes=[artt])
        def f(e):
            return e.tensor_tensor(artv, artv, Gab, OP.subtract)
        O("vector", f, reads=[artt, Gat], writes=[artt])
        def f(e):
            return e.activation(art[:, 0:mw], art[:, 0:mw], AF.Exp)
        O("scalar", f, reads=[artt], writes=[artt])
        def f(e):
            return e.tensor_tensor(bmaj(WH), rmaj(kh), rmaj(art), OP.mult)
        O("vector", f, reads=[kht, artt], writes=[WHt], tag="WH")
        gev = LAM[:, 8:8 + nm_].unsqueeze(2).broadcast_to([128, nm_, CT])
        def f(e, gev=gev):
            return e.tensor_sub(att[:, 0:tw].rearrange("d (m t) -> d m t", t=CT),
                                gev,
                                G0r[:, 0:tw].rearrange("d (m t) -> d m t", t=CT))
        O("vector", f, reads=[LAMt, G0rt], writes=[attt])
        attb = att[:, 0:tw].unsqueeze(1).broadcast_to([128, 4, tw])
        def f(e):
            return e.tensor_tensor(artv, pWv, rlnb, OP.subtract)
        O("vector", f, reads=[pWt, rlnt], writes=[artt])
        def f(e):
            return e.tensor_tensor(artv, artv, attb, OP.add)
        O("vector", f, reads=[artt, attt], writes=[artt])
        def f(e):
            return e.tensor_scalar(art[:, 0:mw], art[:, 0:mw], float(-87.0),
                                   None, OP.max)
        O("vector", f, reads=[artt], writes=[artt])
        def f(e):
            return e.activation(art[:, 0:mw], art[:, 0:mw], AF.Exp)
        O("scalar", f, reads=[artt], writes=[artt])
        def f(e):
            return e.tensor_tensor(bmaj(WG), rmaj(kh), rmaj(art), OP.mult)
        O("vector", f, reads=[kht, artt], writes=[WGt], tag="WG")
        vhb, vhbt = sv("vhb")
        vlb, vlbt = sv("vlb")
        def f(e):
            return e.tensor_copy(bmaj(vhb), rmaj(vhh))
        O("vector", f, reads=[vht], writes=[vhbt])
        def f(e):
            return e.tensor_copy(bmaj(vlb), rmaj(vll))
        O("vector", f, reads=[vlt], writes=[vlbt])
        def f(e):
            return e.tensor_add(att[:, 0:tw], Ga[:, 0:tw], pQ[:, 0:tw])
        O("vector", f, reads=[Gat, pQt], writes=[attt])
        def f(e):
            return e.activation(att[:, 0:tw], att[:, 0:tw], AF.Exp)
        O("scalar", f, reads=[attt], writes=[attt])
        def f(e):
            return e.tensor_mul(at2[:, 0:tw], qs[:, 0:tw], att[:, 0:tw])
        O("vector", f, reads=[qst, attt], writes=[at2t])
        for r in range(4):
            qlv = KQ[:, 0:2 * mw].rearrange("d (m I h r p) -> d h r m I p",
                                            I=4, h=2, r=4, p=4)[:, 1, r]
            def f(e, r=r, qlv=qlv):
                return e.tensor_scalar(qlv, at2[:, 0:tw].rearrange(
                    "d (m I p) -> d m I p", I=4, p=4),
                    wfT[:, 16 * r + h:16 * r + h + 1], None, OP.mult)
            O("vector", f, reads=[at2t], writes=[KQt], hard=[wfTt])
        def f(e):
            return e.tensor_add(att[:, 0:tw], G0r[:, 0:tw], pQ[:, 0:tw])
        O("vector", f, reads=[G0rt, pQt], writes=[attt])
        def f(e):
            return e.tensor_scalar(att[:, 0:tw], att[:, 0:tw], float(-87.0),
                                   None, OP.max)
        O("vector", f, reads=[attt], writes=[attt])
        def f(e):
            return e.activation(att[:, 0:tw], att[:, 0:tw], AF.Exp)
        O("scalar", f, reads=[attt], writes=[attt])
        def f(e):
            return e.tensor_mul(at2[:, 0:tw], qs[:, 0:tw], att[:, 0:tw])
        O("vector", f, reads=[qst, attt], writes=[at2t])
        def f(e):
            return e.tensor_scalar(QB[:, 0:tw], at2[:, 0:tw], pfT[:, h:h + 1],
                                   None, OP.mult)
        O("vector", f, reads=[at2t], writes=[QBt], hard=[pfTt], tag="QB")

    def s2_chunk_phases(self, hg, m, mi, nm_, buf, set_):
        nc = self.nc
        v = lambda n: self.tiles[n][0]
        t = lambda n: self.tiles[n][1]
        warmch = self.warm // CT
        O = self.op
        idsb, idsbt = self.tiles["idsb"]
        Wv, Wvt = self.s2buf["Wv"][buf]
        Nd, Ndt = self.s2buf["Nd"][buf]
        Ngap, Ngapt = self.s2buf["Ngap"][buf]
        Agap, Agapt = self.s2buf["Agap"][buf]
        pNM, pNMt = self.tiles["pb0"]
        pA, pAt = self.tiles["pb1"]
        pT, pTt = self.tiles["pb2"]
        pR, pRt = self.tiles["pb3"]
        pH, pHt = self.tiles["pb4"]
        pS, pSt = self.tiles["pb5"]
        pO = self.tiles["pb6"][0][:, 64 * set_:64 * set_ + 64]
        pOt = self.poT[set_]
        Yp, Ypt = self.s2buf["Yp"][buf]
        Uhi, Uhit = self.s2buf["Uhi"][buf]
        Ulo, Ulot = self.s2buf["Ulo"][buf]
        WgT, WgTt = self.s2buf["WgT"][buf]
        ApT, ApTt = self.s2buf["ApT"][buf]
        mM, mMt = self.tiles["mM"]
        mA, mAt = self.tiles["mA"]
        idg, idgt = self.tiles["idg"]
        ppool, ppoolt = self.tiles["ppool"]
        Sf, Sft = self.tiles["Sf"][0], self.sfT[hg]
        Sb, Sbt = self.tiles["Sb"][0], self.sbT[hg]
        ob, obt = self.tiles["obuf"]
        c0 = mi * CMICRO
        JJ = range(4)
        TCOL = {1: 0, 2: 256, 4: 512, 8: 768, 16: 1024, 32: 1280}
        NCOL = (1536, 1792)
        def sj(nm, j):
            if nm in ("QB", "LAM"):
                return self.sl[(nm, j, set_)]
            return self.sl[(nm, j)]
        A = []
        B = []
        def ra():
            A.append([])
            return A[-1].append
        def rb():
            B.append([])
            return B[-1].append
        r_ = ra()
        for j in JJ:
            WH, WHt_ = sj("WH", j)
            dex, dext = sj("dex", j)
            dxv = dex[:, 16 * mi:16 * mi + 16].rearrange(
                "d (I i) -> d I i", i=4).unsqueeze(3).broadcast_to([128, 4, 4, 16])
            wvv = Wv[:, 256 * j:256 * (j + 1)].rearrange(
                "d (I i c) -> d I i c", I=4, i=4)
            whv2 = WH[:, c0:c0 + CMICRO].rearrange(
                "d (i c) -> d i c", i=4).unsqueeze(1).broadcast_to([128, 4, 4, 16])
            def f(e, wvv=wvv, whv2=whv2, dxv=dxv):
                return e.tensor_tensor(wvv, whv2, dxv, OP.mult)
            r_(lambda f=f, WHt_=WHt_, dext=dext: O("vector", f, reads=[WHt_, dext],
                                                   writes=[Wvt], tag="Wv"))
        r_ = ra()
        for j in JJ:
            KQ, KQt_ = sj("KQ", j)
            for I in range(4):
                def f(e, j=j, I=I, KQ=KQ):
                    return e.matmul(pNM[32 * I:32 * I + 32, 64 * j:64 * j + 64],
                                    KQ[:, 128 * mi + 32 * I:128 * mi + 32 * I + 32],
                                    Wv[:, 256 * j + 64 * I:256 * j + 64 * I + 64],
                                    start=True, stop=True, tile_position=(0, 32 * I))
                r_(lambda f=f, KQt_=KQt_: O("tensor", f, reads=[KQt_, Wvt],
                                            writes=[pNMt]))
        r_ = ra()
        mMv = mM[:, 0:CMICRO].unsqueeze(1).broadcast_to([128, 4, CMICRO])
        mAv = mA[:, 0:CMICRO].unsqueeze(1).broadcast_to([128, 4, CMICRO])
        def f(e):
            return e.tensor_tensor(Ngap[:, 0:256].rearrange("d (j c) -> d j c", j=4),
                                   pNM[:, 0:256].rearrange("d (j c) -> d j c", j=4),
                                   mMv, OP.mult)
        r_(lambda f=f: O("vector", f, reads=[pNMt, mMt], writes=[Ngapt]))
        def f(e):
            return e.tensor_tensor(Agap[:, 0:256].rearrange("d (j c) -> d j c", j=4),
                                   pNM[:, 0:256].rearrange("d (j c) -> d j c", j=4),
                                   mAv, OP.mult)
        r_(lambda f=f: O("vector", f, reads=[pNMt, mAt], writes=[Agapt]))
        r_ = ra()
        for h in JJ:
            p, q = h // 2, h % 2
            WG, WGt_ = sj("WG", h)
            def f(e, p=p, q=q, WG=WG):
                return e.matmul(pA[64 * q:64 * q + 64, 128 * p:128 * p + 128],
                                WG[:, c0:c0 + CMICRO], idsb[:, :],
                                start=True, stop=True, tile_position=(0, 64 * q))
            r_(lambda f=f, WGt_=WGt_: O("tensor", f, reads=[WGt_, idsbt],
                                        writes=[pAt]))
        if m >= warmch:
            for h in JJ:
                p, q = h // 2, h % 2
                def f(e, h=h, p=p, q=q):
                    return e.matmul(pA[64 * q:64 * q + 64,
                                       256 + 32 * p + 16 * q:256 + 32 * p + 16 * q + 16],
                                    Agap[:, 64 * h:64 * h + 64], ppool[:, :],
                                    start=True, stop=True, tile_position=(0, 64 * q))
                r_(lambda f=f: O("tensor", f, reads=[Agapt, ppoolt], writes=[pAt]))
        r_ = ra()
        def f(e):
            return e.activation(WgT[:, 0:256], pA[:, 0:256], AF.Copy)
        r_(lambda f=f: O("scalar", f, reads=[pAt], writes=[WgTt]))
        if m >= warmch:
            def f(e):
                return e.activation(ApT[:, 0:64], pA[:, 256:320], AF.Copy)
            r_(lambda f=f: O("scalar", f, reads=[pAt], writes=[ApTt]))
        r_ = ra()
        for h in JJ:
            p, q = h // 2, h % 2
            def f(e, h=h, p=p, q=q):
                return e.matmul(pT[64 * q:64 * q + 64,
                                   128 * p + 64 * q:128 * p + 64 * q + 64],
                                idg[:, 0:64], Ngap[:, 64 * h:64 * h + 64],
                                start=True, stop=True, tile_position=(0, 64 * q))
            r_(lambda f=f: O("tensor", f, reads=[idgt, Ngapt], writes=[pTt]))
            def f(e, h=h, p=p, q=q):
                return e.matmul(pT[64 * q:64 * q + 64,
                                   256 + 128 * p + 64 * q:256 + 128 * p + 64 * q + 64],
                                Ngap[:, 64 * h:64 * h + 64], idg[:, 0:64],
                                start=True, stop=True, tile_position=(0, 64 * q))
            r_(lambda f=f: O("tensor", f, reads=[idgt, Ngapt], writes=[pTt]))
        r_ = ra()
        def f(e):
            return e.activation(Nd[:, NCOL[0]:NCOL[0] + 256], pT[:, 0:256], AF.Copy)
        r_(lambda f=f: O("scalar", f, reads=[pTt], writes=[Ndt]))
        def f(e):
            return e.activation(Nd[:, 0:256], pT[:, 256:512], AF.Copy)
        r_(lambda f=f: O("scalar", f, reads=[pTt], writes=[Ndt]))
        cn = 0
        for lv in (1, 2, 4, 8):
            r_ = ra()
            tc, nc_ = TCOL[lv], NCOL[cn]
            tc2, nc2 = TCOL[2 * lv], NCOL[1 - cn]
            for p in range(2):
                def f(e, p=p, tc=tc, nc_=nc_):
                    return e.matmul(pT[:, 128 * p:128 * p + 128],
                                    Nd[:, tc + 128 * p:tc + 128 * p + 128],
                                    Nd[:, nc_ + 128 * p:nc_ + 128 * p + 128],
                                    start=True, stop=True)
                r_(lambda f=f: O("tensor", f, reads=[Ndt], writes=[pTt]))
                def f(e, p=p, tc=tc, nc_=nc_):
                    return e.matmul(pT[:, 256 + 128 * p:256 + 128 * p + 128],
                                    Nd[:, nc_ + 128 * p:nc_ + 128 * p + 128],
                                    Nd[:, tc + 128 * p:tc + 128 * p + 128],
                                    start=True, stop=True)
                r_(lambda f=f: O("tensor", f, reads=[Ndt], writes=[pTt]))
            r_ = ra()
            def f(e, nc2=nc2):
                return e.activation(Nd[:, nc2:nc2 + 256], pT[:, 0:256], AF.Copy)
            r_(lambda f=f: O("scalar", f, reads=[pTt], writes=[Ndt]))
            def f(e, tc2=tc2):
                return e.activation(Nd[:, tc2:tc2 + 256], pT[:, 256:512], AF.Copy)
            r_(lambda f=f: O("scalar", f, reads=[pTt], writes=[Ndt]))
            cn = 1 - cn
        r_ = ra()
        for p in range(2):
            def f(e, p=p, nc_=NCOL[cn], tc=TCOL[16]):
                return e.matmul(pT[:, 256 + 128 * p:256 + 128 * p + 128],
                                Nd[:, nc_ + 128 * p:nc_ + 128 * p + 128],
                                Nd[:, tc + 128 * p:tc + 128 * p + 128],
                                start=True, stop=True)
            r_(lambda f=f: O("tensor", f, reads=[Ndt], writes=[pTt]))
        r_ = ra()
        def f(e):
            return e.activation(Nd[:, TCOL[32]:TCOL[32] + 256], pT[:, 256:512], AF.Copy)
        r_(lambda f=f: O("scalar", f, reads=[pTt], writes=[Ndt], tag="N32T"))
        r_ = rb()
        for h in JJ:
            p, q = h // 2, h % 2
            KG, KGt_ = sj("KG", h)
            vhb, vhbt_ = sj("vhb", h)
            vlb, vlbt_ = sj("vlb", h)
            hh = 4 * hg + h
            def f(e, p=p, q=q, KG=KG, hh=hh):
                return e.matmul(pR[64 * q:64 * q + 64, 128 * p:128 * p + 128],
                                KG[:, c0:c0 + CMICRO],
                                Sb[:, hh * 128:(hh + 1) * 128],
                                start=True, stop=False, tile_position=(0, 64 * q))
            r_(lambda f=f, KGt_=KGt_: O("tensor", f, reads=[KGt_, Sbt], writes=[pRt]))
            def f(e, p=p, q=q, vhb=vhb):
                return e.matmul(pR[64 * q:64 * q + 64, 128 * p:128 * p + 128],
                                vhb[:, c0:c0 + CMICRO], idsb[:, :],
                                start=False, stop=False, tile_position=(0, 64 * q))
            r_(lambda f=f, vhbt_=vhbt_: O("tensor", f, reads=[vhbt_, idsbt],
                                          writes=[pRt]))
            def f(e, p=p, q=q, vlb=vlb):
                return e.matmul(pR[64 * q:64 * q + 64, 128 * p:128 * p + 128],
                                vlb[:, c0:c0 + CMICRO], idsb[:, :],
                                start=False, stop=True, tile_position=(0, 64 * q))
            r_(lambda f=f, vlbt_=vlbt_: O("tensor", f, reads=[vlbt_, idsbt],
                                          writes=[pRt]))
        if m >= warmch:
            for h in JJ:
                hh = 4 * hg + h
                QB, QBt_ = sj("QB", h)
                def f(e, h=h, hh=hh, QB=QB):
                    return e.matmul(pO[:, 16 * h:16 * h + 16],
                                    Sb[:, hh * 128:(hh + 1) * 128],
                                    QB[:, mi * CT:(mi + 1) * CT],
                                    start=(h == 0), stop=False)
                r_(lambda f=f, QBt_=QBt_: O("tensor", f, reads=[Sbt, QBt_],
                                            writes=[pOt]))
        r_ = rb()
        def f(e):
            return e.activation(Yp[:, 0:256], pR[:, 0:256], AF.Copy)
        r_(lambda f=f: O("scalar", f, reads=[pRt], writes=[Ypt], tag="Y0"))
        cur = 0
        for lv in (32, 16, 8, 4, 2):
            r_ = rb()
            tc = TCOL[lv]
            for p in range(2):
                def f(e, p=p, tc=tc, cur=cur):
                    return e.matmul(pH[:, 128 * p:128 * p + 128],
                                    Nd[:, tc + 128 * p:tc + 128 * p + 128],
                                    Yp[:, 256 * cur + 128 * p:256 * cur + 128 * p + 128],
                                    start=True, stop=False)
                r_(lambda f=f: O("tensor", f, reads=[Ndt, Ypt], writes=[pHt]))
                def f(e, p=p, cur=cur):
                    return e.matmul(pH[:, 128 * p:128 * p + 128], idsb[:, :],
                                    Yp[:, 256 * cur + 128 * p:256 * cur + 128 * p + 128],
                                    start=False, stop=True)
                r_(lambda f=f: O("tensor", f, reads=[idsbt, Ypt], writes=[pHt]))
            def f(e, cur=cur):
                return e.activation(Yp[:, 256 * (1 - cur):256 * (1 - cur) + 256],
                                    pH[:, 0:256], AF.Copy)
            r_(lambda f=f: O("scalar", f, reads=[pHt], writes=[Ypt]))
            cur = 1 - cur
        r_ = rb()
        for p in range(2):
            def f(e, p=p, cur=cur):
                return e.matmul(pH[:, 128 * p:128 * p + 128],
                                Nd[:, 128 * p:128 * p + 128],
                                Yp[:, 256 * cur + 128 * p:256 * cur + 128 * p + 128],
                                start=True, stop=False)
            r_(lambda f=f: O("tensor", f, reads=[Ndt, Ypt], writes=[pHt]))
            def f(e, p=p, cur=cur):
                return e.matmul(pH[:, 128 * p:128 * p + 128], idsb[:, :],
                                Yp[:, 256 * cur + 128 * p:256 * cur + 128 * p + 128],
                                start=False, stop=True)
            r_(lambda f=f: O("tensor", f, reads=[idsbt, Ypt], writes=[pHt]))
        r_ = rb()
        UhiU, UhiUt = self.s2buf["UhiU"][buf]
        UloU, UloUt = self.s2buf["UloU"][buf]
        def f(e):
            return e.activation(Uhi[0:64, 0:256], pH[0:64, 0:256], AF.Copy)
        r_(lambda f=f: O("scalar", f, reads=[pHt], writes=[Uhit]))
        def f(e):
            return e.activation(UhiU[64:128, 0:256], pH[64:128, 0:256], AF.Copy)
        r_(lambda f=f: O("scalar", f, reads=[pHt], writes=[UhiUt]))
        def f(e):
            return e.scalar_tensor_tensor(Ulo[0:64, 0:256], Uhi[0:64, 0:256], -1.0,
                                          pH[0:64, 0:256], OP.mult, OP.add)
        r_(lambda f=f: O("vector", f, reads=[Uhit, pHt], writes=[Ulot]))
        def f(e):
            return e.scalar_tensor_tensor(UloU[64:128, 0:256], UhiU[64:128, 0:256],
                                          -1.0, pH[64:128, 0:256], OP.mult, OP.add)
        r_(lambda f=f: O("vector", f, reads=[UhiUt, pHt], writes=[UloUt]))
        r_ = rb()
        if m >= warmch:
            for p in range(2):
                for ut, utt in ((Uhi, Uhit), (UhiU, UhiUt), (Ulo, Ulot)):
                    def f(e, p=p, ut=ut):
                        return e.matmul(pO[:, 32 * p:32 * p + 32],
                                        ut[:, 128 * p:128 * p + 128],
                                        ApT[:, 32 * p:32 * p + 32],
                                        start=False, stop=False)
                    r_(lambda f=f, utt=utt: O("tensor", f, reads=[utt, ApTt],
                                              writes=[pOt]))
                def f(e, p=p):
                    return e.matmul(pO[:, 32 * p:32 * p + 32],
                                    UloU[:, 128 * p:128 * p + 128],
                                    ApT[:, 32 * p:32 * p + 32],
                                    start=False, stop=(p == 1))
                r_(lambda f=f: O("tensor", f, reads=[UloUt, ApTt], writes=[pOt]))
        for h in JJ:
            p, q = h // 2, h % 2
            uh, uht = (Uhi, Uhit) if q == 0 else (UhiU, UhiUt)
            ul, ult = (Ulo, Ulot) if q == 0 else (UloU, UloUt)
            def f(e, h=h, p=p, uh=uh):
                return e.matmul(pS[:, 128 * h:128 * h + 128],
                                WgT[:, 128 * p:128 * p + 128],
                                uh[:, 128 * p:128 * p + 128],
                                start=True, stop=False)
            r_(lambda f=f, uht=uht: O("tensor", f, reads=[WgTt, uht], writes=[pSt]))
            def f(e, h=h, p=p, ul=ul):
                return e.matmul(pS[:, 128 * h:128 * h + 128],
                                WgT[:, 128 * p:128 * p + 128],
                                ul[:, 128 * p:128 * p + 128],
                                start=False, stop=True)
            r_(lambda f=f, ult=ult: O("tensor", f, reads=[WgTt, ult], writes=[pSt]))
        r_ = rb()
        if m >= warmch:
            toff = (m - warmch) * CT
            obv = ob[:, 4 * hg * self.T_SL:(4 * hg + 4) * self.T_SL].rearrange(
                "d (j t) -> d j t", t=self.T_SL)[:, :, toff:toff + CT]
            def f(e, obv=obv):
                return e.tensor_copy(obv,
                                     pO[:, 0:64].rearrange("d (j t) -> d j t", j=4))
            r_(lambda f=f: O("vector", f, reads=[pOt], writes=[obt], tag="oev"))
        h0 = 4 * hg
        for h in JJ:
            LAM, LAMt_ = sj("LAM", h)
            hh = h0 + h
            def f(e, h=h, hh=hh, LAM=LAM):
                return e.scalar_tensor_tensor(Sf[:, hh * 128:(hh + 1) * 128],
                                              Sf[:, hh * 128:(hh + 1) * 128],
                                              LAM[:, mi:mi + 1],
                                              pS[:, 128 * h:128 * h + 128],
                                              OP.mult, OP.add)
            r_(lambda f=f, LAMt_=LAMt_: O("vector", f, reads=[Sft, pSt],
                                          writes=[Sft], hard=[LAMt_], tag="Sup"))
        def f(e, h0=h0):
            return e.activation(Sb[:, h0 * 128:(h0 + 4) * 128],
                                Sf[:, h0 * 128:(h0 + 4) * 128], AF.Copy)
        r_(lambda f=f: O("scalar", f, reads=[Sft], writes=[Sbt]))
        return A, B[:1], B[1:]

    def stage3(self):
        nc, d = self.nc, self.dram
        T_SL, TX = self.T_SL, self.TX
        xn, xnt = self.tiles["xn"]
        v = lambda n: self.tiles[n][0]
        t = lambda n: self.tiles[n][1]
        ob, obt = self.tiles["obuf"]
        sq, sqt = self.tiles["sq"]
        rt, rtt = self.tiles["rtmp"]
        rb, rbt = self.tiles["rowbf"]
        ft, ftt = v("ftmp"), t("ftmp")
        ft2, ft2t = v("ftmp2"), t("ftmp2")
        kt, ktt = v("ktile"), t("ktile")
        xch, xcht = v("xchunk"), t("xchunk")
        wtf, wtft = v("wtf"), t("wtf")
        wtb, wtbt = v("wtb"), t("wtb")
        onesrow, onesrowt = self.tiles["onesrow"]
        onescolb, onescolbt = self.tiles["onescolb"]
        pss, psst = self.tiles["pss"]
        pbc, pbct = self.tiles["pbc"]
        O = self.op
        self.wtb3T = [Tile("wtb3a"), Tile("wtb3b")]
        def f(e):
            return e.memset(ft[0:1, 0:1], 0.0)
        O("vector", f, writes=[wtft, wtbt] + self.wtb3T + self.s2b1t + [ftt],
          tag="fenceB")
        onwc, onwct = self.sbuf("onwc", [128, 1], F32)
        self.dma("qsync", onwc[:, 0:1], d["onw"], writes=[onwct])
        ffscT, ffscTt = self.sbuf("ffscT", [128, 16], F32)
        ffbT, ffbTt = self.sbuf("ffbT", [128, 16], F32)
        self.dma("qsync", ffscT[:, 0:16],
                 d["ffsc"].rearrange("(i p) o -> p (i o)", p=128),
                 writes=[ffscTt], slow=True)
        self.dma("qsync", ffbT[:, 0:16],
                 d["ffb"].rearrange("(i p) o -> p (i o)", p=128),
                 writes=[ffbTt], slow=True)
        for h in range(H):
            for c0 in range(0, T_SL, 512):
                cw = min(512, T_SL - c0)
                oc0 = h * T_SL + c0
                def f(e, oc0=oc0, cw=cw):
                    return e.tensor_mul(sq[:, 0:cw], ob[:, oc0:oc0 + cw],
                                        ob[:, oc0:oc0 + cw])
                O("vector", f, reads=[obt], writes=[sqt])
                def f(e, cw=cw):
                    return e.matmul(pss[0:1, 0:cw], onescolb[:, 0:1], sq[:, 0:cw],
                                    start=True, stop=True)
                O("tensor", f, reads=[sqt, onescolbt], writes=[psst])
                def f(e, cw=cw):
                    return e.activation(rt[0:1, 0:cw], pss[0:1, 0:cw], AF.Ln,
                                        scale=np.float32(1.0 / DV),
                                        bias=np.float32(EPS))
                O("scalar", f, reads=[psst], writes=[rtt])
                def f(e, cw=cw):
                    return e.activation(rt[0:1, 512:512 + cw], rt[0:1, 0:cw],
                                        AF.Exp, scale=np.float32(-0.5))
                O("scalar", f, reads=[rtt], writes=[rtt])
                def f(e, cw=cw):
                    return e.tensor_copy(rb[0:1, 0:cw], rt[0:1, 512:512 + cw])
                O("vector", f, reads=[rtt], writes=[rbt])
                def f(e, cw=cw):
                    return e.matmul(pbc[:, 0:cw], onesrow[0:1, :], rb[0:1, 0:cw],
                                    start=True, stop=True)
                O("tensor", f, reads=[rbt, onesrowt], writes=[pbct])
                self.dma("qact", kt[:, 0:cw],
                         d["gsp"][h * 128:(h + 1) * 128, c0:c0 + cw],
                         writes=[ktt], tag="g.ld")
                def f(e, cw=cw):
                    return e.activation(ft[:, 0:cw], kt[:, 0:cw], AF.Sigmoid)
                O("scalar", f, reads=[ktt], writes=[ftt])
                def f(e, oc0=oc0, cw=cw):
                    return e.scalar_tensor_tensor(ft2[:, 0:cw], ob[:, oc0:oc0 + cw],
                                                  onwc[:, 0:1], pbc[:, 0:cw],
                                                  OP.mult, OP.mult)
                O("vector", f, reads=[obt, pbct], writes=[ft2t], hard=[onwct])
                def f(e, oc0=oc0, cw=cw):
                    return e.tensor_tensor(ob[:, oc0:oc0 + cw], ft2[:, 0:cw],
                                           ft[:, 0:cw], OP.mult)
                O("vector", f, reads=[ft2t, ftt], writes=[obt], tag="onrm")
        pj0, pj0t = self.tiles["pj0"]
        pj1, pj1t = self.tiles["pj1"]
        W3RR = ("qsync", "qpool", "qact")
        for oc in range(16):
            for wi, wn in enumerate(("wop", "wff")):
                src = d[wn].rearrange("(i p) o -> p i o", p=128)[:, :, oc * 128:(oc + 1) * 128]
                self.dma(W3RR[(2 * oc + wi) % 3],
                         wtf[:, wi * 2048:wi * 2048 + 2048].rearrange(
                             "p (i o) -> p i o", i=16),
                         src, writes=[wtft], tag=f"w3.{wn}{oc}")
                def f(e, wi=wi, oc=oc):
                    return e.tensor_copy(
                        wtb[:, (oc % 2) * 4096 + wi * 2048:
                            (oc % 2) * 4096 + wi * 2048 + 2048],
                        wtf[:, wi * 2048:wi * 2048 + 2048])
                O("gpsimd", f, reads=[wtft], writes=[self.wtb3T[oc % 2]])
            for c0 in range(0, T_SL, 512):
                cw = min(512, T_SL - c0)
                for ic in range(NIC):
                    def f(e, ic=ic, c0=c0, cw=cw, oc=oc):
                        return e.matmul(pj0[:, 0:cw],
                                        wtb[:, (oc % 2) * 4096 + ic * 128:
                                            (oc % 2) * 4096 + (ic + 1) * 128],
                                        ob[:, ic * T_SL + c0:ic * T_SL + c0 + cw],
                                        start=(ic == 0), stop=(ic == NIC - 1))
                    O("tensor", f, reads=[self.wtb3T[oc % 2], obt],
                      writes=[pj0t], tag="mmop")
                xc0 = (self.TX - self.T_SL) + c0
                for ic in range(NIC):
                    def f(e, ic=ic, c0=c0, cw=cw, xc0=xc0, oc=oc):
                        return e.matmul(pj1[:, 0:cw],
                                        wtb[:, (oc % 2) * 4096 + 2048 + ic * 128:
                                            (oc % 2) * 4096 + 2048 + (ic + 1) * 128],
                                        xn[:, ic * self.TX + xc0:ic * self.TX + xc0 + cw],
                                        start=(ic == 0), stop=(ic == NIC - 1))
                    O("tensor", f, reads=[self.wtb3T[oc % 2], xnt],
                      writes=[pj1t], tag="mmff")
                def f(e, cw=cw, oc=oc):
                    return e.activation(ft[:, 0:cw], pj1[:, 0:cw], AF.Silu,
                                        scale=ffscT[:, oc:oc + 1],
                                        bias=ffbT[:, oc:oc + 1])
                O("scalar", f, reads=[pj1t], writes=[ftt], hard=[ffscTt, ffbTt],
                  tag="ffsilu")
                self.dma("qsync", xch[:, 0:cw],
                         d["x_t"][oc * 128:(oc + 1) * 128, xc0:xc0 + cw],
                         writes=[xcht], tag="x3.ld")
                def f(e, cw=cw):
                    return e.tensor_tensor(ft2[:, 0:cw], pj0[:, 0:cw], ft[:, 0:cw],
                                           OP.mult)
                O("vector", f, reads=[pj0t, ftt], writes=[ft2t])
                def f(e, cw=cw):
                    return e.tensor_add(ft2[:, 0:cw], ft2[:, 0:cw], xch[:, 0:cw])
                O("vector", f, reads=[ft2t, xcht], writes=[ft2t], tag="res")
                self.dma("qact", d["out"][oc * 128:(oc + 1) * 128, c0:c0 + cw],
                         ft2[:, 0:cw], reads=[ft2t], tag=f"out.{oc}.{c0}")

    def emit_debug_dumps(self):
        for ent in self.dbg_dump:
            if isinstance(ent, tuple):
                name, pmax, cmax = ent
            else:
                name, pmax, cmax = ent, None, None
            v, t = self.tiles[name]
            shp = list(v.shape)
            if pmax:
                shp = [pmax, cmax]
            dt_ = v.dtype if hasattr(v, 'dtype') else F32
            d = self.nc.dram_tensor("dbg_" + name, shp, dt_,
                                    kind="ExternalOutput").ap()
            src = v[:, :] if not pmax else v[0:pmax, 0:cmax]
            self.dma("qpool", d, src, reads=[t], tag="dbg." + name)

    def emit_blocks(self):
        nc, pl = self.nc, self.pl
        esem = {e: nc.semaphore(f"es_{e}").__enter__() for e in ENGINES}
        qsem = {(q, l): nc.semaphore(f"qs_{q}_{l}").__enter__()
                for q in DMAQ for l in range(NLANES)}
        blkcm = nc.Block()
        blk = blkcm.__enter__()
        self.cms.append(blkcm)

        def run_stream(eng_obj, ename):
            qfinal = {}
            for op in pl.streams[ename]:
                if op.dma_queue:
                    key = (op.dma_queue, op.qpos % NLANES)
                    qfinal[key] = 16 * (op.qpos // NLANES + 1)
            seen = {}
            for op in pl.streams[ename]:
                for key, val in sorted(op.waits.items(), key=str):
                    if seen.get(key, -1) >= val:
                        continue
                    seen[key] = val
                    sem = esem[key[1]] if key[0] == "e" else qsem[(key[1], key[2])]
                    eng_obj.wait_ge(sem, val)
                inst = op.emit(eng_obj)
                if op.dma_queue:
                    inst.then_inc(qsem[(op.dma_queue, op.qpos % NLANES)], 16)
                else:
                    if inst is not None:
                        inst.then_inc(esem[ename], 1)
                    else:
                        raise RuntimeError(f"op {op.idx} {op.tag} returned None")
            for key, val in sorted(qfinal.items()):
                eng_obj.wait_ge(qsem[key], val)

        @blk.tensor
        def _(e):
            run_stream(e, "tensor")

        @blk.vector
        def _(e):
            run_stream(e, "vector")

        @blk.scalar
        def _(e):
            run_stream(e, "scalar")

        @blk.gpsimd
        def _(e):
            run_stream(e, "gpsimd")

        @blk.sync
        def _(e):
            run_stream(e, "sync")


def kernel(**inputs):
    n_cores = 8
    T_SL = 1024
    maps = host_prepare(inputs, n_cores, T_SL)
    nc, _ = build_program(T_SL)
    res = run_bass_kernel_spmd(nc, maps, core_ids=list(range(n_cores)))
    parts = [np.asarray(res.results[ci]["out"]) for ci in range(n_cores)]
    full = np.concatenate(parts, axis=1)
    return np.ascontiguousarray(full.T)[None].astype(np.float32)


# version 14
# speedup vs baseline: 1.0733x; 1.0606x over previous
import numpy as np
import ml_dtypes
import concourse.bass as bass
import concourse.mybir as mybir
from concourse.bass_utils import run_bass_kernel_spmd

F32 = mybir.dt.float32
BF16 = mybir.dt.bfloat16
AF = mybir.ActivationFunctionType
OP = mybir.AluOpType
NPBF16 = ml_dtypes.bfloat16

H = 16
DH = 128
DV = 128
RR = 4
HID = 2048
NIC = HID // 128
EPS = 1e-5
CT = 16
INNER = 4
CMICRO = 4 * CT
NI = CT // INNER
WARM = 96
HALO = WARM + 3
SLABCH = 4
NEG87 = np.float32(-87.0)


def bfnp(x):
    return np.asarray(x, np.float32).astype(NPBF16)



ENGINES = ("tensor", "vector", "scalar", "gpsimd", "sync")
DMAQ = {"qsync": "sync", "qact": "scalar", "qpool": "gpsimd"}
NLANES = 12


class Op:
    __slots__ = ("idx", "engine", "emit", "deps", "dma_queue", "pos", "tag", "qpos", "waits")

    def __init__(self, idx, engine, emit, deps, dma_queue, tag):
        self.idx = idx
        self.engine = engine
        self.emit = emit
        self.deps = deps
        self.dma_queue = dma_queue
        self.pos = None
        self.tag = tag


class Tile:
    __slots__ = ("name", "wr", "readers")

    def __init__(self, name):
        self.name = name
        self.wr = None
        self.readers = []


class Plan:
    def __init__(self, nc):
        self.nc = nc
        self.ops = []

    def add(self, engine, emit, reads=(), writes=(), hard=(), dma_queue=None,
            tag=""):
        deps = {}
        idx = len(self.ops)
        for t in reads:
            if t.wr is not None:
                deps[t.wr] = max(deps.get(t.wr, 0), 0)
            t.readers.append(idx)
        for t in hard:
            if t.wr is not None:
                deps[t.wr] = 1
            t.readers.append(idx)
        for t in writes:
            if t.wr is not None:
                deps[t.wr] = max(deps.get(t.wr, 0), 0)
            for r in t.readers:
                if r != idx:
                    deps[r] = max(deps.get(r, 0), 0)
            t.wr = idx
            t.readers = []
        self.ops.append(Op(idx, engine, emit, deps, dma_queue, tag))
        return idx

    def finalize(self):
        streams = {e: [] for e in ENGINES}
        qpos = {q: 0 for q in DMAQ}
        cpos = {e: 0 for e in ENGINES}
        for op in self.ops:
            streams[op.engine].append(op)
            if op.dma_queue:
                op.qpos = qpos[op.dma_queue]
                qpos[op.dma_queue] += 1
                op.pos = None
            else:
                op.pos = cpos[op.engine]
                cpos[op.engine] += 1
        for op in self.ops:
            waits = {}
            for d, hardflag in op.deps.items():
                dop = self.ops[d]
                if dop.dma_queue:
                    key = ("q", dop.dma_queue, dop.qpos % NLANES)
                    val = 16 * (dop.qpos // NLANES + 1)
                elif (dop.engine == op.engine and not op.dma_queue
                      and not hardflag and op.engine == "tensor"):
                    continue
                else:
                    key = ("e", dop.engine)
                    val = dop.pos + 1
                if waits.get(key, 0) < val:
                    waits[key] = val
            if op.dma_queue:
                key = ("q", op.dma_queue, op.qpos % NLANES)
                val = 16 * (op.qpos // NLANES)
                if val > 0 and waits.get(key, 0) < val:
                    waits[key] = val
            op.waits = waits
        self.streams = streams



def host_consts():
    time = np.zeros(CMICRO, np.int64)
    tok = np.zeros(CMICRO, np.int64)
    for sp in range(CMICRO):
        I, r, p = sp // 16, (sp % 16) // 4, sp % 4
        time[sp] = 4 * (4 * I + p) + r
        tok[sp] = 4 * I + p
    iblk = tok // INNER
    smask = (time[:, None] > time[None, :]).astype(np.float32)
    imask = (time[:, None] >= time[None, :]).astype(np.float32)
    gapM = np.full(CMICRO, -1, np.int64)
    gapA = np.full(CMICRO, -1, np.int64)
    for I in range(NI):
        rows = [s for s in range(CMICRO) if iblk[s] == I]
        for p, s in enumerate(rows):
            gapM[s] = 32 * I + p
            gapA[s] = 32 * I + 16 + p
    mask_gapM = np.zeros((128, CMICRO), np.float32)
    mask_gapA = np.zeros((128, CMICRO), np.float32)
    for s in range(CMICRO):
        mask_gapM[gapM[s], :] = -smask[s, :]
        mask_gapA[gapA[s], :] = imask[s, :]
    ident_gapM = np.zeros((128, CMICRO), np.float32)
    for a in range(CMICRO):
        ident_gapM[gapM[a], a] = 1.0
    ppool_gap = np.zeros((128, CT), np.float32)
    for s in range(CMICRO):
        ppool_gap[gapA[s], tok[s]] = 1.0
    return {
        "c_maskM": bfnp(mask_gapM),
        "c_maskA": bfnp(mask_gapA),
        "c_identg": bfnp(ident_gapM),
        "c_ppool": bfnp(ppool_gap),
        "c_ident": bfnp(np.eye(128, dtype=np.float32)),
    }, gapM, gapA


def host_prepare(inputs, n_cores, T_SL):
    x = np.asarray(inputs["L0_attn_hidden_states"][0], np.float32)
    xt = np.ascontiguousarray(x.T)

    def t_(name):
        return np.ascontiguousarray(np.asarray(inputs[name], np.float32).T)

    consts, _, _ = host_consts()
    shared = {
        "wq": t_("L0_attn_q_proj_w"), "wk": t_("L0_attn_k_proj_w"),
        "wv": t_("L0_attn_v_proj_w"), "wb": t_("L0_attn_b_proj_w"),
        "wf1": t_("L0_attn_f1_w"), "wf2": t_("L0_attn_f2_w"),
        "wg1": t_("L0_attn_g1_w"), "wg2": t_("L0_attn_g2_w"),
        "wop": t_("L0_attn_o_proj_w"), "wff": t_("L0_ffn_weight"),
        "qcw": np.ascontiguousarray(np.asarray(inputs["L0_attn_q_conv_w"], np.float32)),
        "kcw": np.ascontiguousarray(np.asarray(inputs["L0_attn_k_conv_w"], np.float32)),
        "vcw": np.ascontiguousarray(np.asarray(inputs["L0_attn_v_conv_w"], np.float32)),
        "alog": np.asarray(inputs["L0_attn_A_log"], np.float32).reshape(H, 1),
        "dtb_hd": np.ascontiguousarray(
            np.asarray(inputs["L0_attn_dt_bias"], np.float32).reshape(H, DH)),
        "mlog": np.ascontiguousarray(np.asarray(inputs["L0_attn_micro_logits"], np.float32)),
        "onw": np.asarray(inputs["L0_attn_o_norm_w"], np.float32).reshape(DV, 1),
        "g2b": np.asarray(inputs["L0_attn_g2_b"], np.float32).reshape(HID, 1),
        "ffsc": np.asarray(inputs["L0_ffn_scale_weight"], np.float32).reshape(HID, 1),
        "ffb": np.asarray(inputs["L0_ffn_bias"], np.float32).reshape(HID, 1),
    }
    shared.update(consts)
    maps = []
    for ci in range(n_cores):
        t0 = ci * T_SL
        lo = t0 - HALO
        xs = np.zeros((HID, T_SL + HALO), np.float32)
        s = max(lo, 0)
        xs[:, s - lo:] = xt[:, s:t0 + T_SL]
        m = dict(shared)
        m["x_t"] = xs
        maps.append(m)
    return maps



def build_program(T_SL=1024, warm=WARM, debug=False, dbg_dump=()):
    TW = T_SL + warm
    TX = T_SL + warm + 3
    nch = TW // CT
    assert TW % CT == 0

    nc = bass.Bass("TRN2", target_bir_lowering=False, debug=debug)
    pl = Plan(nc)

    dram = {}
    def din(nm, shp, dt=F32):
        dram[nm] = nc.dram_tensor(nm, shp, dt, kind="ExternalInput").ap()
    din("x_t", [HID, TX])
    din("wq", [HID, HID]); din("wk", [HID, 4 * HID]); din("wv", [HID, 4 * HID])
    din("wb", [HID, 64]); din("wf1", [HID, 128]); din("wf2", [128, HID])
    din("wg1", [HID, 128]); din("wg2", [128, HID]); din("wop", [HID, HID])
    din("wff", [HID, HID])
    din("qcw", [HID, 4]); din("kcw", [4 * HID, 4]); din("vcw", [4 * HID, 4])
    din("alog", [H, 1]); din("dtb_hd", [H, DH]); din("mlog", [H, 4])
    din("onw", [DV, 1]); din("g2b", [HID, 1]); din("ffsc", [HID, 1]); din("ffb", [HID, 1])
    for nm in ("c_maskM", "c_maskA", "c_identg"):
        din(nm, [128, CMICRO], BF16)
    din("c_ppool", [128, CT], BF16)
    din("c_ident", [128, 128], BF16)
    dram["out"] = nc.dram_tensor("out", [HID, T_SL], F32, kind="ExternalOutput").ap()
    def dsc(nm, shp, dt):
        dram[nm] = nc.dram_tensor(nm, shp, dt).ap()
    dsc("ksp", [4 * HID, TW], BF16)
    dsc("vsph", [4 * HID, TW], BF16)
    dsc("vspl", [4 * HID, TW], BF16)
    dsc("qsp", [HID, TW], BF16)
    dsc("lsp", [HID, TW], F32)
    dsc("rowk", [64, TW], F32)
    dsc("lnb", [64, TW], F32)
    dsc("rowq", [16, TW], F32)
    dsc("gsp", [HID, T_SL], BF16)

    b = Builder(nc, pl, dram, T_SL, TW, TX, nch)
    b.dbg_dump = dbg_dump
    b.build()
    return nc, b


class Builder:
    def __init__(self, nc, pl, dram, T_SL, TW, TX, nch):
        self.nc, self.pl, self.dram = nc, pl, dram
        self.T_SL, self.TW, self.TX, self.nch = T_SL, TW, TX, nch
        self.warm = TW - T_SL
        self.cms = []
        self.tiles = {}
        self.psum_used = 0

    def sbuf(self, name, shp, dt):
        h = self.nc.sbuf_tensor(name, shp, dt)
        v = h.__enter__()
        self.cms.append(h)
        t = Tile(name)
        self.tiles[name] = (v, t)
        return v, t

    def psum(self, name, shp, dt=F32):
        h = self.nc.psum_tensor(name, shp, dt)
        v = h.__enter__()
        self.cms.append(h)
        t = Tile(name)
        self.tiles[name] = (v, t)
        return v, t

    def dtile(self, name):
        if name not in self.tiles:
            self.tiles[name] = (None, Tile(name))
        return self.tiles[name][1]

    def dma(self, queue, out_ap, in_ap, reads=(), writes=(), tag="", slow=False):
        eng = DMAQ[queue]
        def emit(e):
            if slow:
                return e.dma_start(out=out_ap, in_=in_ap,
                                   allow_slow_non_contiguous=True)
            return e.dma_start(out=out_ap, in_=in_ap)
        return self.pl.add(eng, emit, reads=reads, writes=writes,
                           dma_queue=queue, tag=tag)

    def op(self, engine, fn, reads=(), writes=(), hard=(), tag=""):
        return self.pl.add(engine, fn, reads=reads, writes=writes, hard=hard,
                           tag=tag)

    def build(self):
        self.alloc_all()
        self.stage0_consts()
        self.stage0_ln()
        self.stage1()
        self.stage2()
        self.stage3()
        self.emit_debug_dumps()
        self.pl.finalize()
        self.emit_blocks()
        for cm in reversed(self.cms):
            cm.__exit__(None, None, None)

    def alloc_all(self):
        TX, TW, T_SL = self.TX, self.TW, self.T_SL
        s = self.sbuf
        p = self.psum
        s("xn", [128, NIC * TX], BF16)
        s("xchunk", [128, 2 * 512], F32)
        s("statrow", [1, 1536], F32)
        s("ybuf", [128, TX + 8], BF16)
        s("wtf", [128, 2 * 2048], F32)
        s("wtb", [128, 4 * 2048], BF16)
        s("ktile", [128, 512], BF16)
        s("vhi", [128, 512], BF16)
        s("vlo", [128, 512], BF16)
        s("ftmp", [128, 512], F32)
        s("ftmp2", [128, 512], F32)
        s("sq", [128, 512], BF16)
        s("rowbf", [1, 1024], BF16)
        s("y1", [128, TX], BF16)
        s("yg1", [128, TX], BF16)
        s("convd", [128, 4 * 128], BF16)
        s("cwcol", [128, 4], F32)
        s("idsb", [128, 128], BF16)
        s("hd_misc", [16, 1792], F32)
        s("lnfT", [128, 16], F32)
        s("rlnA", [128, 64], F32)
        s("AbT", [128, 16], F32)
        s("dtbT", [128, 16], F32)
        s("wfillT", [128, 4 * 16], F32)
        s("fpowT", [128, 4 * 16], F32)
        s("PFT", [128, 16], F32)
        for i in range(8):
            p(f"pb{i}", [128, 512], F32)
        self.tiles["pbc"] = self.tiles["pb0"]
        self.tiles["pj0"] = self.tiles["pb1"]
        self.tiles["pj1"] = self.tiles["pb2"]
        self.tiles["pcv"] = self.tiles["pb3"]
        self.tiles["pss"] = self.tiles["pb4"]

    def stage0_consts(self):
        nc, d = self.nc, self.dram
        v = lambda n: self.tiles[n][0]
        t = lambda n: self.tiles[n][1]
        self.dma("qsync", v("idsb")[:, :], d["c_ident"], writes=[t("idsb")])
        hm, hmt = v("hd_misc"), t("hd_misc")
        ones, onest = self.sbuf("onescol", [128, 2], F32)
        def f(e):
            return e.memset(ones[:, :], 1.0)
        self.op("vector", f, writes=[onest])
        onesrow, onesrowt = self.sbuf("onesrow", [1, 128], BF16)
        def f(e):
            return e.memset(onesrow[:, :], 1.0)
        self.op("vector", f, writes=[onesrowt])
        onesrowf, onesrowft = self.sbuf("onesrowf", [1, 128], F32)
        def f(e):
            return e.memset(onesrowf[:, :], 1.0)
        self.op("vector", f, writes=[onesrowft])
        self.dma("qsync", hm[0:16, 129:133], d["mlog"], writes=[hmt])
        def f(e):
            return e.activation(hm[0:16, 134:138], hm[0:16, 129:133], AF.Exp)
        self.op("scalar", f, reads=[hmt], writes=[hmt])
        def f(e):
            return e.reduce_sum(hm[0:16, 138:139], hm[0:16, 134:138], mybir.AxisListType.X)
        self.op("vector", f, reads=[hmt], writes=[hmt])
        def f(e):
            return e.reciprocal(hm[0:16, 139:140], hm[0:16, 138:139])
        self.op("vector", f, reads=[hmt], writes=[hmt])
        def f(e):
            return e.tensor_scalar(hm[0:16, 134:138], hm[0:16, 134:138],
                                   hm[0:16, 139:140], None, OP.mult)
        self.op("vector", f, reads=[hmt], writes=[hmt], hard=[hmt], tag="wmix")
        m1, m1t = self.sbuf("misc1", [4, 512], F32)
        for r in range(4):
            self.dma("qpool", m1[0:1, 64 + 16 * r:80 + 16 * r],
                     hm[0:16, 134 + r:135 + r],
                     reads=[hmt], writes=[m1t], tag=f"wmixrow{r}")
        self.dma("qsync", m1[0:1, 16:32], d["alog"], writes=[m1t])
        def f(e):
            return e.activation(m1[0:1, 16:32], m1[0:1, 16:32], AF.Exp)
        self.op("scalar", f, reads=[m1t], writes=[m1t], tag="A")
        def f(e):
            return e.tensor_scalar(m1[0:1, 32:48], m1[0:1, 16:32], -1.0, None, OP.mult)
        self.op("vector", f, reads=[m1t], writes=[m1t])
        pbc, pbct = self.tiles["pbc"]
        def f(e):
            return e.matmul(pbc[:, 0:16], onesrowf[0:1, :], m1[0:1, 32:48],
                            start=True, stop=True)
        self.op("tensor", f, reads=[m1t, onesrowft], writes=[pbct])
        ab, abt = v("AbT"), t("AbT")
        def f(e):
            return e.tensor_copy(ab[:, 0:16], pbc[:, 0:16])
        self.op("vector", f, reads=[pbct], writes=[abt], tag="AbT")
        db, dbt = v("dtbT"), t("dtbT")
        self.dma("qsync", db[:, 0:16], d["dtb_hd"].transpose([1, 0]),
                 writes=[dbt], tag="dtbT", slow=True)
        lf, lft = v("lnfT"), t("lnfT")
        sc1, sc1t = self.sbuf("csc1", [128, 16], F32)
        def f(e):
            return e.activation(sc1[:, 0:16], db[:, 0:16], AF.Abs, bias=np.float32(-1e4))
        self.op("scalar", f, reads=[dbt], writes=[sc1t])
        def f(e):
            return e.activation(sc1[:, 0:16], sc1[:, 0:16], AF.Exp, scale=np.float32(-1.0))
        self.op("scalar", f, reads=[sc1t], writes=[sc1t])
        def f(e):
            return e.activation(sc1[:, 0:16], sc1[:, 0:16], AF.Ln, bias=np.float32(1.0))
        self.op("scalar", f, reads=[sc1t], writes=[sc1t])
        def f(e):
            return e.activation(lf[:, 0:16], db[:, 0:16], AF.Relu, bias=np.float32(-1e4))
        self.op("scalar", f, reads=[dbt], writes=[lft])
        def f(e):
            return e.tensor_add(lf[:, 0:16], lf[:, 0:16], sc1[:, 0:16])
        self.op("vector", f, reads=[lft, sc1t], writes=[lft])
        def f(e):
            return e.tensor_mul(lf[:, 0:16], lf[:, 0:16], ab[:, 0:16])
        self.op("vector", f, reads=[lft, abt], writes=[lft])
        def f(e):
            return e.tensor_scalar(lf[:, 0:16], lf[:, 0:16], -2.0, None, OP.max)
        self.op("vector", f, reads=[lft], writes=[lft], tag="lnfT")
        rlA, rlAt = v("rlnA"), t("rlnA")
        for r in range(4):
            def f(e, r=r):
                return e.tensor_scalar(rlA[:, r:64:4], lf[:, 0:16], float(r),
                                       None, OP.mult)
            self.op("vector", f, reads=[lft], writes=[rlAt])
        fp, fpt = v("fpowT"), t("fpowT")
        def f(e):
            return e.memset(fp[:, 0:16], 1.0)
        self.op("vector", f, writes=[fpt])
        def f(e):
            return e.activation(fp[:, 16:32], lf[:, 0:16], AF.Exp)
        self.op("scalar", f, reads=[lft], writes=[fpt], tag="fillT")
        for r in (2, 3):
            def f(e, r=r):
                return e.tensor_mul(fp[:, 16 * r:16 * r + 16],
                                    fp[:, 16 * (r - 1):16 * r], fp[:, 16:32])
            self.op("vector", f, reads=[fpt], writes=[fpt])
        wf, wft = v("wfillT"), t("wfillT")
        pf, pft = v("PFT"), t("PFT")
        for r in range(4):
            def f(e, r=r):
                return e.matmul(pbc[:, 32 + 16 * r:48 + 16 * r], onesrowf[0:1, :],
                                m1[0:1, 64 + 16 * r:80 + 16 * r], start=True, stop=True)
            self.op("tensor", f, reads=[m1t, onesrowft], writes=[pbct])
        for r in range(4):
            def f(e, r=r):
                return e.tensor_mul(wf[:, 16 * r:16 * r + 16],
                                    fp[:, 16 * r:16 * r + 16],
                                    pbc[:, 32 + 16 * r:48 + 16 * r])
            self.op("vector", f, reads=[fpt, pbct], writes=[wft])
        def f(e):
            return e.tensor_scalar(wf[:, 0:64], wf[:, 0:64], float(DH ** -0.5),
                                   None, OP.mult)
        self.op("vector", f, reads=[wft], writes=[wft])
        def f(e):
            return e.tensor_add(pf[:, 0:16], wf[:, 0:16], wf[:, 16:32])
        self.op("vector", f, reads=[wft], writes=[pft])
        def f(e):
            return e.tensor_add(pf[:, 0:16], pf[:, 0:16], wf[:, 32:48])
        self.op("vector", f, reads=[wft, pft], writes=[pft])
        def f(e):
            return e.tensor_add(pf[:, 0:16], pf[:, 0:16], wf[:, 48:64])
        self.op("vector", f, reads=[wft, pft], writes=[pft], tag="PFT")

    def stage0_ln(self):
        nc, d = self.nc, self.dram
        TX = self.TX
        v = lambda n: self.tiles[n][0]
        t = lambda n: self.tiles[n][1]
        xch, xcht = v("xchunk"), t("xchunk")
        pst, pstt = self.tiles["pb5"]
        ps2, ps2t = self.tiles["pb6"]
        sr, srt = self.tiles["statrow"]
        pbc, pbct = self.tiles["pbc"]
        xn, xnt = v("xn"), t("xn")
        ft, ftt = v("ftmp"), t("ftmp")
        idsb, idsbt = self.tiles["idsb"]
        ntile = (TX + 511) // 512
        ones, onest = self.tiles["onescol"]
        onesrow, onesrowt = self.tiles["onesrow"]
        for ti in range(ntile):
            c0 = ti * 512
            cw = min(512, TX - c0)
            for ic in range(NIC):
                buf = (ic % 2) * 512
                self.dma("qsync", xch[:, buf:buf + cw],
                         d["x_t"][ic * 128:(ic + 1) * 128, c0:c0 + cw],
                         writes=[xcht], tag=f"ln.x.{ti}.{ic}")
                def f(e, buf=buf, cw=cw, ic=ic):
                    return e.matmul(pst[0:1, 0:cw], ones[:, 0:1], xch[:, buf:buf + cw],
                             start=(ic == 0), stop=(ic == NIC - 1))
                self.op("tensor", f, reads=[xcht, onest], writes=[pstt])
                def f(e, buf=buf, cw=cw):
                    return e.tensor_mul(ft[:, 0:cw], xch[:, buf:buf + cw], xch[:, buf:buf + cw])
                self.op("vector", f, reads=[xcht], writes=[ftt])
                def f(e, cw=cw, ic=ic):
                    return e.matmul(ps2[0:1, 0:cw], ones[:, 1:2], ft[:, 0:cw],
                             start=(ic == 0), stop=(ic == NIC - 1))
                self.op("tensor", f, reads=[ftt, onest], writes=[ps2t])
            def f(e, cw=cw):
                return e.tensor_scalar(sr[0:1, 0:cw], pst[0:1, 0:cw],
                                float(1.0 / HID), None, OP.mult)
            self.op("vector", f, reads=[pstt], writes=[srt])
            def f(e, cw=cw):
                return e.tensor_mul(sr[0:1, 1024:1024 + cw], sr[0:1, 0:cw], sr[0:1, 0:cw])
            self.op("vector", f, reads=[srt], writes=[srt])
            def f(e, cw=cw):
                return e.scalar_tensor_tensor(sr[0:1, 1024:1024 + cw], ps2[0:1, 0:cw],
                                       float(1.0 / HID), sr[0:1, 1024:1024 + cw],
                                       OP.mult, OP.subtract)
            self.op("vector", f, reads=[ps2t, srt], writes=[srt], tag="var")
            def f(e, cw=cw):
                return e.activation(sr[0:1, 1024:1024 + cw], sr[0:1, 1024:1024 + cw],
                             AF.Ln, bias=np.float32(EPS))
            self.op("scalar", f, reads=[srt], writes=[srt])
            def f(e, cw=cw):
                return e.activation(sr[0:1, 512:512 + cw], sr[0:1, 1024:1024 + cw],
                             AF.Exp, scale=np.float32(-0.5))
            self.op("scalar", f, reads=[srt], writes=[srt], tag="rstd")
            rb, rbt = self.tiles["rowbf"]
            def f(e, cw=cw):
                return e.tensor_copy(rb[0:1, 0:cw], sr[0:1, 0:cw])
            self.op("vector", f, reads=[srt], writes=[rbt])
            def f(e, cw=cw):
                return e.tensor_copy(rb[0:1, 512:512 + cw], sr[0:1, 512:512 + cw])
            self.op("vector", f, reads=[srt], writes=[rbt])
            def f(e, cw=cw):
                return e.matmul(pbc[:, 0:cw], onesrow[0:1, :], rb[0:1, 0:cw],
                         start=True, stop=True)
            self.op("tensor", f, reads=[rbt, onesrowt], writes=[pbct], tag="bc.mu")
            pb2, pb2t = self.tiles["pj1"]
            def f(e, cw=cw):
                return e.matmul(pb2[:, 0:cw], onesrow[0:1, :], rb[0:1, 512:512 + cw],
                         start=True, stop=True)
            self.op("tensor", f, reads=[rbt, onesrowt], writes=[pb2t], tag="bc.rs")
            for ic in range(NIC):
                buf = (ic % 2) * 512
                self.dma("qsync", xch[:, buf:buf + cw],
                         d["x_t"][ic * 128:(ic + 1) * 128, c0:c0 + cw],
                         writes=[xcht], tag=f"ln.x2.{ti}.{ic}")
                def f(e, buf=buf, cw=cw):
                    return e.tensor_sub(ft[:, 0:cw], xch[:, buf:buf + cw], pbc[:, 0:cw])
                self.op("vector", f, reads=[xcht, pbct], writes=[ftt])
                def f(e, ic=ic, c0=c0, cw=cw):
                    return e.tensor_tensor(xn[:, ic * TX + c0: ic * TX + c0 + cw],
                                    ft[:, 0:cw], pb2[:, 0:cw], OP.mult)
                self.op("vector", f, reads=[ftt, pb2t], writes=[xnt], tag="xn")

    def stage1(self):
        nc, d = self.nc, self.dram
        TX, TW, T_SL = self.TX, self.TW, self.T_SL
        v = lambda n: self.tiles[n][0]
        t = lambda n: self.tiles[n][1]
        xn, xnt = v("xn"), t("xn")
        wtf, wtft = v("wtf"), t("wtf")
        wtb, wtbt = v("wtb"), t("wtb")
        ybuf, ybuft = v("ybuf"), t("ybuf")
        convd, convdt = v("convd"), t("convd")
        cwc, cwct = v("cwcol"), t("cwcol")
        idsb, idsbt = self.tiles["idsb"]
        onescolb, onescolbt = self.sbuf("onescolb", [128, 1], BF16)
        def f(e):
            return e.memset(onescolb[:, :], 1.0)
        self.op("vector", f, writes=[onescolbt])
        rtmp, rtmpt = self.sbuf("rtmp", [1, 1024], F32)
        g2bT, g2bTt = self.sbuf("g2bT", [128, 16], F32)
        self.dma("qsync", g2bT[:, 0:16],
                 d["g2b"].rearrange("(i p) o -> p (i o)", p=128),
                 writes=[g2bTt], slow=True, tag="g2bT")

        jobs = []
        for oc in range(16):
            jobs.append(("q", oc))
        for oc in range(64):
            jobs.append(("k", oc))
        for oc in range(64):
            jobs.append(("v", oc))
        jobs.append(("b", 0))
        jobs.append(("f1", 0))
        for oc in range(16):
            jobs.append(("f2", oc))
        jobs.append(("g1", 0))
        for oc in range(16):
            jobs.append(("g2", oc))

        W = {"q": ("wq", 128), "k": ("wk", 128), "v": ("wv", 128),
             "b": ("wb", 64), "f1": ("wf1", 128), "f2": ("wf2", 128),
             "g1": ("wg1", 128), "g2": ("wg2", 128)}

        WQRR = ("qsync", "qpool")
        def wload(jid, name, oc):
            wn, ww = W[name]
            buf = (jid % 2) * 2048
            nicj = 1 if name in ("f2", "g2") else NIC
            src = d[wn].rearrange("(i p) o -> p i o", p=128)[:, :, oc * ww:(oc + 1) * ww]
            self.dma(WQRR[jid % 2], wtf[:, buf:buf + nicj * ww].rearrange(
                "p (i o) -> p i o", i=nicj), src, writes=[wtft],
                tag=f"wl.{name}{oc}")
            def f(e, buf=buf, n=nicj * ww):
                return e.tensor_copy(wtb[:, buf:buf + n], wtf[:, buf:buf + n])
            self.op("gpsimd", f, reads=[wtft], writes=[wtbt], tag=f"wc.{name}{oc}")
            return buf, nicj, ww

        pend = {}
        pend[0] = wload(0, *jobs[0])
        for jid, (name, oc) in enumerate(jobs):
            if jid + 1 < len(jobs):
                pend[jid + 1] = wload(jid + 1, *jobs[jid + 1])
            buf, nicj, ww = pend.pop(jid)
            shift = 0 if name in ("q", "k", "v", "f1", "g1") else 3
            isconv = name in ("q", "k", "v")
            pj, pjt = self.tiles["pj0" if jid % 2 == 0 else "pj1"]
            if name in ("f2",):
                mov, movt = v("y1"), t("y1")
            elif name in ("g2",):
                mov, movt = v("yg1"), t("yg1")
            else:
                mov, movt = xn, xnt
            if name == "g2":
                tiles = [(self.warm + 3 + i, min(512, TX - (self.warm + 3 + i)))
                         for i in range(0, T_SL, 512)]
            else:
                tiles = [(shift + i, min(512, TX - shift - i))
                         for i in range(0, TX - shift, 512)]
            for ti, (c0, cw) in enumerate(tiles):
                for ic in range(nicj):
                    if name in ("f2", "g2"):
                        mv = mov[:, c0:c0 + cw]
                    else:
                        mv = mov[:, ic * TX + c0:ic * TX + c0 + cw]
                    def f(e, buf=buf, ic=ic, ww=ww, mv=mv, cw=cw, nicj=nicj, pj=pj):
                        return e.matmul(pj[0:ww, 0:cw],
                                        wtb[:, buf + ic * ww:buf + (ic + 1) * ww],
                                        mv, start=(ic == 0), stop=(ic == nicj - 1))
                    self.op("tensor", f, reads=[wtbt, movt], writes=[pjt],
                            tag=f"mm.{name}{oc}.{ti}")
                if isconv:
                    def f(e, c0=c0, cw=cw, pj=pj):
                        return e.tensor_copy(ybuf[:, c0:c0 + cw], pj[:, 0:cw])
                    self.op("vector", f, reads=[pjt], writes=[ybuft])
                elif name == "f1":
                    y1v, y1t = v("y1"), t("y1")
                    def f(e, c0=c0, cw=cw, pj=pj):
                        return e.tensor_copy(y1v[:, c0:c0 + cw], pj[:, 0:cw])
                    self.op("vector", f, reads=[pjt], writes=[y1t])
                elif name == "g1":
                    yg, ygt = v("yg1"), t("yg1")
                    def f(e, c0=c0, cw=cw, pj=pj):
                        return e.tensor_copy(yg[:, c0:c0 + cw], pj[:, 0:cw])
                    self.op("vector", f, reads=[pjt], writes=[ygt])
                elif name == "b":
                    self.ep_beta(pj, pjt, c0 - 3, cw)
                elif name == "f2":
                    self.ep_L(pj, pjt, oc, c0 - 3, cw)
                elif name == "g2":
                    self.ep_gate(pj, pjt, oc, c0 - (self.warm + 3), cw, g2bT, g2bTt)
            if isconv:
                self.ep_conv(name, oc)

    def ep_conv(self, name, oc):
        nc, d = self.nc, self.dram
        TW, TX = self.TW, self.TX
        v = lambda n: self.tiles[n][0]
        t = lambda n: self.tiles[n][1]
        ybuf, ybuft = v("ybuf"), t("ybuf")
        convd, convdt = v("convd"), t("convd")
        cwc, cwct = v("cwcol"), t("cwcol")
        idsb, idsbt = self.tiles["idsb"]
        pcv, pcvt = self.tiles["pcv"]
        pss, psst = self.tiles["pss"]
        ft, ftt = v("ftmp"), t("ftmp")
        ft2, ft2t = v("ftmp2"), t("ftmp2")
        kt, ktt = v("ktile"), t("ktile")
        sq, sqt = v("sq"), t("sq")
        rt, rtt = v("rtmp"), t("rtmp")
        onescolb, onescolbt = self.tiles["onescolb"]
        cwn = {"q": "qcw", "k": "kcw", "v": "vcw"}[name]
        self.dma("qsync", cwc[:, 0:4], d[cwn][oc * 128:(oc + 1) * 128, :],
                 writes=[cwct], tag=f"cw.{name}{oc}")
        for i in range(4):
            def f(e, i=i):
                return e.tensor_scalar(convd[:, i * 128:(i + 1) * 128],
                                       idsb[:, :], cwc[:, i:i + 1], None, OP.mult)
            self.op("vector", f, reads=[idsbt], writes=[convdt], hard=[cwct],
                    tag=f"diag.{name}{oc}.{i}")
        for tt0 in range(0, TW, 512):
            cwt = min(512, TW - tt0)
            for i in range(4):
                def f(e, tt0=tt0, i=i, cwt=cwt):
                    return e.matmul(pcv[:, 0:cwt], convd[:, i * 128:(i + 1) * 128],
                                    ybuf[:, tt0 + i:tt0 + i + cwt],
                                    start=(i == 0), stop=(i == 3))
                self.op("tensor", f, reads=[convdt, ybuft], writes=[pcvt],
                        tag=f"cv.{name}{oc}.{tt0}")
            def f(e, cwt=cwt):
                return e.activation(ft[:, 0:cwt], pcv[:, 0:cwt], AF.Silu)
            self.op("scalar", f, reads=[pcvt], writes=[ftt], tag="silu")
            def f(e, cwt=cwt):
                return e.tensor_copy(kt[:, 0:cwt], ft[:, 0:cwt])
            self.op("vector", f, reads=[ftt], writes=[ktt])
            if name == "v":
                vh, vht = v("vhi"), t("vhi")
                vl, vlt = v("vlo"), t("vlo")
                def f(e, cwt=cwt):
                    return e.tensor_copy(vh[:, 0:cwt], ft[:, 0:cwt])
                self.op("vector", f, reads=[ftt], writes=[vht])
                def f(e, cwt=cwt):
                    return e.scalar_tensor_tensor(vl[:, 0:cwt], vh[:, 0:cwt], -1.0,
                                                  ft[:, 0:cwt], OP.mult, OP.add)
                self.op("vector", f, reads=[vht, ftt], writes=[vlt])
                self.dma("qact", d["vsph"][oc * 128:(oc + 1) * 128, tt0:tt0 + cwt],
                         vh[:, 0:cwt], reads=[vht], tag=f"sp.vh{oc}")
                self.dma("qact", d["vspl"][oc * 128:(oc + 1) * 128, tt0:tt0 + cwt],
                         vl[:, 0:cwt], reads=[vlt], tag=f"sp.vl{oc}")
            else:
                dst = "qsp" if name == "q" else "ksp"
                self.dma("qact", d[dst][oc * 128:(oc + 1) * 128, tt0:tt0 + cwt],
                         kt[:, 0:cwt], reads=[ktt], tag=f"sp.{name}{oc}")
                def f(e, cwt=cwt):
                    return e.tensor_mul(sq[:, 0:cwt], kt[:, 0:cwt], kt[:, 0:cwt])
                self.op("vector", f, reads=[ktt], writes=[sqt])
                def f(e, cwt=cwt):
                    return e.matmul(pss[0:1, 0:cwt], onescolb[:, 0:1], sq[:, 0:cwt],
                                    start=True, stop=True)
                self.op("tensor", f, reads=[sqt, onescolbt], writes=[psst])
                def f(e, cwt=cwt):
                    return e.activation(rt[0:1, 0:cwt], pss[0:1, 0:cwt], AF.Ln,
                                        bias=np.float32(1e-6))
                self.op("scalar", f, reads=[psst], writes=[rtt])
                def f(e, cwt=cwt):
                    return e.tensor_scalar(rt[0:1, 512:512 + cwt], rt[0:1, 0:cwt],
                                           -0.5, None, OP.mult)
                self.op("vector", f, reads=[rtt], writes=[rtt])
                rowd = "rowq" if name == "q" else "rowk"
                self.dma("qact", d[rowd][oc:oc + 1, tt0:tt0 + cwt],
                         rt[0:1, 512:512 + cwt], reads=[rtt], tag=f"row.{name}{oc}")

    def ep_beta(self, pj, pjt, tt0, cw):
        d = self.dram
        v = lambda n: self.tiles[n][0]
        t = lambda n: self.tiles[n][1]
        ft, ftt = v("ftmp"), t("ftmp")
        ft2, ft2t = v("ftmp2"), t("ftmp2")
        def f(e):
            return e.activation(ft[0:64, 0:cw], pj[0:64, 0:cw], AF.Relu,
                                scale=np.float32(-1.0))
        self.op("scalar", f, reads=[pjt], writes=[ftt])
        def f(e):
            return e.activation(ft2[0:64, 0:cw], pj[0:64, 0:cw], AF.Abs)
        self.op("scalar", f, reads=[pjt], writes=[ft2t])
        def f(e):
            return e.activation(ft2[0:64, 0:cw], ft2[0:64, 0:cw], AF.Exp,
                                scale=np.float32(-1.0))
        self.op("scalar", f, reads=[ft2t], writes=[ft2t])
        def f(e):
            return e.activation(ft2[0:64, 0:cw], ft2[0:64, 0:cw], AF.Ln,
                                bias=np.float32(1.0))
        self.op("scalar", f, reads=[ft2t], writes=[ft2t])
        def f(e):
            return e.scalar_tensor_tensor(ft[0:64, 0:cw], ft[0:64, 0:cw], -1.0,
                                          ft2[0:64, 0:cw], OP.mult, OP.subtract)
        self.op("vector", f, reads=[ftt, ft2t], writes=[ftt], tag="lnb")
        self.dma("qact", d["lnb"][0:64, tt0:tt0 + cw], ft[0:64, 0:cw],
                 reads=[ftt], tag="sp.lnb")

    def ep_L(self, pj, pjt, h, tt0, cw):
        d = self.dram
        v = lambda n: self.tiles[n][0]
        t = lambda n: self.tiles[n][1]
        ft, ftt = v("ftmp"), t("ftmp")
        ft2, ft2t = v("ftmp2"), t("ftmp2")
        db, dbt = v("dtbT"), t("dtbT")
        ab, abt = v("AbT"), t("AbT")
        def f(e):
            return e.activation(ft[:, 0:cw], pj[:, 0:cw], AF.Relu,
                                bias=db[:, h:h + 1])
        self.op("scalar", f, reads=[pjt], writes=[ftt], hard=[dbt])
        def f(e):
            return e.activation(ft2[:, 0:cw], pj[:, 0:cw], AF.Abs,
                                bias=db[:, h:h + 1])
        self.op("scalar", f, reads=[pjt], writes=[ft2t], hard=[dbt])
        def f(e):
            return e.activation(ft2[:, 0:cw], ft2[:, 0:cw], AF.Exp,
                                scale=np.float32(-1.0))
        self.op("scalar", f, reads=[ft2t], writes=[ft2t])
        def f(e):
            return e.activation(ft2[:, 0:cw], ft2[:, 0:cw], AF.Ln,
                                bias=np.float32(1.0))
        self.op("scalar", f, reads=[ft2t], writes=[ft2t])
        def f(e):
            return e.tensor_add(ft[:, 0:cw], ft[:, 0:cw], ft2[:, 0:cw])
        self.op("vector", f, reads=[ftt, ft2t], writes=[ftt])
        def f(e):
            return e.tensor_scalar(ft[:, 0:cw], ft[:, 0:cw], ab[:, h:h + 1],
                                   -17.0, OP.mult, OP.max)
        self.op("vector", f, reads=[ftt], writes=[ftt], hard=[abt], tag="L")
        self.dma("qact", d["lsp"][h * 128:(h + 1) * 128, tt0:tt0 + cw],
                 ft[:, 0:cw], reads=[ftt], tag=f"sp.L{h}")

    def ep_gate(self, pj, pjt, oc, c0out, cw, g2bT, g2bTt):
        d = self.dram
        v = lambda n: self.tiles[n][0]
        t = lambda n: self.tiles[n][1]
        vh, vht = v("vhi"), t("vhi")
        def f(e):
            return e.tensor_scalar(vh[:, 0:cw], pj[:, 0:cw],
                                   g2bT[:, oc:oc + 1], None, OP.add)
        self.op("vector", f, reads=[pjt], writes=[vht], hard=[g2bTt], tag="gate")
        self.dma("qact", d["gsp"][oc * 128:(oc + 1) * 128, c0out:c0out + cw],
                 vh[:, 0:cw], reads=[vht], tag=f"sp.g{oc}")

    def stage2(self):
        nc, d = self.nc, self.dram
        TW, T_SL = self.TW, self.T_SL
        nch = TW // CT
        v = lambda n: self.tiles[n][0]
        t = lambda n: self.tiles[n][1]
        s = self.sbuf
        NG = 4
        Sf, Sft = s("Sf", [128, H * 128], F32)
        Sb, Sbt = s("Sb", [128, H * 128], BF16)
        ob, obt = s("obuf", [128, H * T_SL], BF16)
        self.sfT = [Tile(f"Sf{g}") for g in range(4)]
        self.sbT = [Tile(f"Sb{g}") for g in range(4)]
        self.poT = [Tile("pO0"), Tile("pO1")]
        def f(e):
            return e.memset(Sf[:, :], 0.0)
        self.op("vector", f, writes=self.sfT)
        def f(e):
            return e.memset(Sb[:, :], 0.0)
        self.op("vector", f, writes=self.sbT)
        self.sl = {}
        for j in range(NG):
            for nm, wmul, dt_ in [("kh", 4, BF16), ("vh", 4, BF16), ("vl", 4, BF16),
                                  ("vhb", 4, BF16), ("vlb", 4, BF16),
                                  ("KQ", 8, BF16), ("WH", 4, BF16), ("KG", 4, BF16),
                                  ("WG", 4, BF16),
                                  ("qs", 1, BF16), ("Ls", 1, F32), ("Ct", 1, None),
                                  ("G0", 1, F32), ("Ga", 1, F32), ("G0r", 1, F32),
                                  ("QB", 1, BF16), ("dex", 1, None),
                                  ("rows", 1, None), ("LAM", 1, F32)]:
                if nm == "rows":
                    self.sl[(nm, j)] = s(f"sl_{nm}{j}", [1, 9 * SLABCH * CT], F32)
                elif nm == "Ct":
                    self.sl[(nm, j)] = s(f"sl_{nm}{j}", [128, SLABCH * CT + 1], F32)
                elif nm == "dex":
                    self.sl[(nm, j)] = s(f"sl_{nm}{j}", [128, 16 * SLABCH], F32)
                elif nm == "LAM":
                    for st in range(2):
                        self.sl[(nm, j, st)] = s(f"sl_{nm}{j}_{st}", [128, 16], F32)
                elif nm == "QB":
                    for st in range(2):
                        self.sl[(nm, j, st)] = s(f"sl_{nm}{j}_{st}",
                                                 [128, wmul * SLABCH * CT], dt_)
                else:
                    self.sl[(nm, j)] = s(f"sl_{nm}{j}", [128, wmul * SLABCH * CT], dt_)
        for nm, shp, dt_ in [("arg", [128, SLABCH * CMICRO], F32),
                             ("att", [128, SLABCH * CT + 64], F32),
                             ("att2", [128, SLABCH * CT + 64], F32),
                             ("anch", [128, 4], F32),
                             ("dexp", [128, 16], F32),
                             ("Wv", [128, 1024], BF16),
                             ("Nd", [128, 2048], BF16),
                             ("Yp", [128, 512], BF16),
                             ("Uhi", [128, 256], BF16), ("Ulo", [128, 256], BF16),
                             ("UhiU", [128, 256], BF16), ("UloU", [128, 256], BF16),
                             ("WgT", [128, 256], BF16), ("ApT", [128, 64], BF16),
                             ("Ngap", [128, 256], BF16), ("Agap", [128, 256], BF16),
                             ("mM", [128, CMICRO], BF16), ("mA", [128, CMICRO], BF16),
                             ("idg", [128, CMICRO], BF16), ("ppool", [128, CT], BF16)]:
            s(nm, shp, dt_)
        self.dma("qsync", v("mM")[:, :], d["c_maskM"], writes=[t("mM")])
        self.dma("qsync", v("mA")[:, :], d["c_maskA"], writes=[t("mA")])
        self.dma("qsync", v("idg")[:, :], d["c_identg"], writes=[t("idg")])
        self.dma("qsync", v("ppool")[:, :], d["c_ppool"], writes=[t("ppool")])
        wtfB = v("wtf").bitcast(BF16)
        self.s2buf = {}
        off = 0
        for nm, shp in [("Nd", [128, 2048]), ("Wv", [128, 1024]),
                        ("Ngap", [128, 256]), ("Agap", [128, 256]),
                        ("WgT", [128, 256]), ("ApT", [128, 64]),
                        ("Yp", [128, 512]), ("Uhi", [128, 256]),
                        ("Ulo", [128, 256]), ("UhiU", [128, 256]),
                        ("UloU", [128, 256])]:
            t1 = Tile(nm + "B1")
            ap1 = wtfB[0:shp[0], off:off + shp[1]]
            off += shp[1]
            self.s2buf[nm] = [(v(nm), t(nm)), (ap1, t1)]
        self.s2b1t = [self.s2buf[nm][1][1]
                      for nm in ("Nd", "Wv", "Ngap", "Agap", "WgT", "ApT",
                                 "Yp", "Uhi", "Ulo", "UhiU", "UloU")]
        ftm = v("ftmp")
        def f(e):
            return e.memset(ftm[0:1, 0:1], 0.0)
        self.op("vector", f, writes=[t("wtf")] + self.s2b1t + [t("ftmp")],
                tag="fenceA")
        pTz = self.tiles["pb2"][0]
        def f(e):
            return e.memset(pTz[:, 0:512], 0.0)
        self.op("vector", f, writes=[self.tiles["pb2"][1]])
        pAz = self.tiles["pb1"][0]
        def f(e):
            return e.memset(pAz[:, 256:320], 0.0)
        self.op("vector", f, writes=[self.tiles["pb1"][1]])
        for nmz in ("att", "att2"):
            vz = v(nmz)
            def f(e, vz=vz):
                return e.memset(vz[:, 0:64], 0.0)
            self.op("vector", f, writes=[t(nmz)])
        for nmz in ("UhiU", "UloU"):
            for bz in range(2):
                vz, tz = self.s2buf[nmz][bz]
                def f(e, vz=vz):
                    return e.memset(vz[0:64, 0:256], 0.0)
                self.op("vector", f, writes=[tz])
        for nmz in ("Uhi", "Ulo"):
            for bz in range(2):
                vz, tz = self.s2buf[nmz][bz]
                def f(e, vz=vz):
                    return e.memset(vz[64:128, 0:256], 0.0)
                self.op("vector", f, writes=[tz])
        blocks = []
        for slab in range((nch + SLABCH - 1) // SLABCH):
            m0 = slab * SLABCH
            nm_ = min(SLABCH, nch - m0)
            for hg in range(H // NG):
                blocks.append((m0, nm_, hg))

        def emit_rounds(rounds):
            for rnd in rounds:
                for th in rnd:
                    th()

        def loads_preps(bi):
            bm0, bnm, bhg = blocks[bi]
            for j in range(NG):
                h = bhg * NG + j
                self.s2_load_slab(h, j, bm0, bnm)
                self.s2_prep(h, j, bm0, bnm, bi % 2)

        loads_preps(0)
        for bi, (m0, nm_, hg) in enumerate(blocks):
            pend = []
            for mi in range(nm_):
                A, Bh, Bt = self.s2_chunk_phases(hg, m0 + mi, mi, nm_,
                                                 (m0 + mi) % 2, bi % 2)
                na, nb = len(A), len(pend)
                for i in range(max(na, nb)):
                    if i < na:
                        for th in A[i]:
                            th()
                    if i < nb:
                        for th in pend[i]:
                            th()
                if mi < nm_ - 1:
                    pend = Bh + Bt
                else:
                    emit_rounds(Bh)
                    if bi + 1 < len(blocks):
                        loads_preps(bi + 1)
                    emit_rounds(Bt)

    def s2_load_slab(self, h, j, m0, nm_):
        d = self.dram
        t0 = m0 * CT
        tw = nm_ * CT
        q2 = "qsync" if h % 2 == 0 else "qpool"
        def sv(nm):
            return self.sl[(nm, j)]
        for nm, src in [("kh", "ksp"), ("vh", "vsph"), ("vl", "vspl")]:
            vv, tt = sv(nm)
            ap = d[src][h * 512:(h + 1) * 512, t0:t0 + tw]
            ap = ap.rearrange("(r d) t -> d r t", d=128)
            out = vv[:, 0:4 * tw].rearrange("d (r t) -> d r t", r=4)
            self.dma(q2, out, ap, writes=[tt], tag=f"sl.{nm}{h}")
        vv, tt = sv("qs")
        self.dma(q2, vv[:, 0:tw], d["qsp"][h * 128:(h + 1) * 128, t0:t0 + tw],
                 writes=[tt], tag=f"sl.q{h}")
        vv, tt = sv("Ls")
        self.dma(q2, vv[:, 0:tw], d["lsp"][h * 128:(h + 1) * 128, t0:t0 + tw],
                 writes=[tt], tag=f"sl.L{h}")
        vv, tt = sv("rows")
        W8 = SLABCH * CT
        self.dma("qsync", vv[0:1, 0:4 * tw].rearrange("o (r t) -> o r t", r=4),
                 d["rowk"][4 * h:4 * h + 4, t0:t0 + tw].unsqueeze(0),
                 writes=[tt], tag=f"sl.rk{h}")
        self.dma("qsync", vv[0:1, 4 * W8:4 * W8 + 4 * tw].rearrange("o (r t) -> o r t", r=4),
                 d["lnb"][4 * h:4 * h + 4, t0:t0 + tw].unsqueeze(0),
                 writes=[tt], tag=f"sl.lb{h}")
        self.dma("qsync", vv[0:1, 8 * W8:8 * W8 + tw],
                 d["rowq"][h:h + 1, t0:t0 + tw], writes=[tt], tag=f"sl.rq{h}")

    def s2_prep(self, h, j, m0, nm_, set_):
        nc = self.nc
        v = lambda n: self.tiles[n][0]
        t = lambda n: self.tiles[n][1]
        tw = nm_ * CT
        mw = 4 * tw
        W8 = SLABCH * CT
        def sv(nm):
            return self.sl[(nm, j)]
        kh, kht = sv("kh"); vhh, vht = sv("vh"); vll, vlt = sv("vl")
        qs, qst = sv("qs"); Ls, Lst = sv("Ls"); Ct, Ctt = sv("Ct")
        G0, G0t = sv("G0"); Ga, Gat = sv("Ga"); rows, rowst = sv("rows")
        LAM, LAMt = self.sl[("LAM", j, set_)]
        KQ, KQt = sv("KQ"); WH, WHt = sv("WH"); KG, KGt = sv("KG")
        WG, WGt = sv("WG"); QB, QBt = self.sl[("QB", j, set_)]
        dx, dxt = sv("dex")
        art, artt = v("arg"), t("arg")
        att, attt = v("att"), t("att")
        at2, at2t = v("att2"), t("att2")
        lf, lft = v("lnfT"), t("lnfT")
        pfT, pfTt = v("PFT"), t("PFT")
        wfT, wfTt = v("wfillT"), t("wfillT")
        onesrowf, onesrowft = self.tiles["onesrowf"]
        pK, pKt = self.tiles["pb5"]
        pW, pWt = self.tiles["pb7"]
        pQ, pQt = self.tiles["pb7"][0][:, 256:512], self.tiles["pb7"][1]
        O = self.op
        rln, rlnt = v("rlnA")[:, 4 * h:4 * h + 4], t("rlnA")
        def f(e):
            return e.tensor_add(rows[0:1, 4 * W8:4 * W8 + 4 * tw],
                                rows[0:1, 4 * W8:4 * W8 + 4 * tw],
                                rows[0:1, 0:4 * tw])
        O("vector", f, reads=[rowst], writes=[rowst])
        def f(e):
            return e.matmul(pK[:, 0:mw], onesrowf[0:1, :], rows[0:1, 0:4 * tw],
                            start=True, stop=True)
        O("tensor", f, reads=[rowst, onesrowft], writes=[pKt])
        def f(e):
            return e.matmul(pW[:, 0:mw], onesrowf[0:1, :],
                            rows[0:1, 4 * W8:4 * W8 + 4 * tw], start=True, stop=True)
        O("tensor", f, reads=[rowst, onesrowft], writes=[pWt])
        def f(e):
            return e.matmul(pQ[:, 0:tw], onesrowf[0:1, :],
                            rows[0:1, 8 * W8:8 * W8 + tw], start=True, stop=True)
        O("tensor", f, reads=[rowst, onesrowft], writes=[pQt])
        def f(e):
            return e.memset(Ct[:, 0:1], 0.0)
        O("vector", f, writes=[Ctt])
        def f(e):
            return e.tensor_scalar(att[:, 64:64 + tw], Ls[:, 0:tw], rln[:, 3:4],
                                   None, OP.add)
        O("vector", f, reads=[Lst], writes=[attt], hard=[rlnt])
        src, srct, dst, dstt = att, attt, at2, at2t
        sh = 1
        while sh < tw:
            def f(e, sh=sh, src=src, dst=dst):
                return e.tensor_add(dst[:, 64:64 + tw], src[:, 64:64 + tw],
                                    src[:, 64 - sh:64 - sh + tw])
            O("vector", f, reads=[srct], writes=[dstt])
            src, srct, dst, dstt = dst, dstt, src, srct
            sh *= 2
        def f(e, src=src):
            return e.tensor_copy(Ct[:, 1:tw + 1], src[:, 64:64 + tw])
        O("vector", f, reads=[srct], writes=[Ctt])
        def f(e):
            return e.tensor_scalar(G0[:, 0:tw], Ct[:, 1:tw + 1], rln[:, 3:4], None,
                                   OP.subtract)
        O("vector", f, reads=[Ctt], writes=[G0t], hard=[rlnt])
        G0r, G0rt = sv("G0r")
        cecv = Ct[:, 0:tw:CT].unsqueeze(2).broadcast_to([128, nm_, CT])
        def f(e, cecv=cecv):
            return e.tensor_sub(G0r[:, 0:tw].rearrange("d (m t) -> d m t", t=CT),
                                G0[:, 0:tw].rearrange("d (m t) -> d m t", t=CT),
                                cecv)
        O("vector", f, reads=[G0t, Ctt], writes=[G0rt])
        ancv = Ct[:, 0:tw:4].unsqueeze(2).broadcast_to([128, tw // 4, 4])
        def f(e, ancv=ancv):
            return e.tensor_sub(Ga[:, 0:tw].rearrange("d (b p) -> d b p", p=4),
                                G0[:, 0:tw].rearrange("d (b p) -> d b p", p=4),
                                ancv)
        O("vector", f, reads=[G0t, Ctt], writes=[Gat])
        def f(e):
            return e.tensor_sub(LAM[:, 8:8 + nm_], Ct[:, CT:tw + 1:CT],
                                Ct[:, 0:tw:CT])
        O("vector", f, reads=[Ctt], writes=[LAMt])
        def f(e):
            return e.tensor_scalar(LAM[:, 0:nm_], LAM[:, 8:8 + nm_],
                                   float(-87.0), None, OP.max)
        O("vector", f, reads=[LAMt], writes=[LAMt])
        def f(e):
            return e.activation(LAM[:, 0:nm_], LAM[:, 0:nm_], AF.Exp)
        O("scalar", f, reads=[LAMt], writes=[LAMt])
        a4 = Ct[:, 0:tw:4].rearrange("d (m I) -> d m I", I=4)
        minu = a4.unsqueeze(3).broadcast_to([128, nm_, 4, 4])
        subt = a4.unsqueeze(2).broadcast_to([128, nm_, 4, 4])
        dxv = dx[:, 0:16 * nm_].rearrange("d (m I i) -> d m I i", I=4, i=4)
        def f(e, dxv=dxv, minu=minu, subt=subt):
            return e.tensor_sub(dxv, minu, subt)
        O("vector", f, reads=[Ctt], writes=[dxt], tag="dexp.sub")
        def f(e):
            return e.tensor_scalar(dx[:, 0:16 * nm_], dx[:, 0:16 * nm_],
                                   float(-87.0), float(0.0), OP.max, OP.min)
        O("vector", f, reads=[dxt], writes=[dxt])
        def f(e):
            return e.activation(dx[:, 0:16 * nm_], dx[:, 0:16 * nm_], AF.Exp)
        O("scalar", f, reads=[dxt], writes=[dxt], tag="dexp")
        def rmaj(x):
            return x[:, 0:mw].rearrange("d (r m I p) -> d m I r p",
                                        r=4, m=nm_, I=4)
        def bmaj(x):
            return x[:, 0:mw].rearrange("d (m I r p) -> d m I r p",
                                        m=nm_, I=4, r=4)
        artv = art[:, 0:mw].rearrange("d (r t) -> d r t", r=4)
        pKv = pK[:, 0:mw].rearrange("d (r t) -> d r t", r=4)
        pWv = pW[:, 0:mw].rearrange("d (r t) -> d r t", r=4)
        rlnb = rln[:, 0:4].unsqueeze(2).broadcast_to([128, 4, tw])
        Gab = Ga[:, 0:tw].unsqueeze(1).broadcast_to([128, 4, tw])
        G0rb = G0r[:, 0:tw].unsqueeze(1).broadcast_to([128, 4, tw])
        def f(e):
            return e.tensor_tensor(artv, pKv, rlnb, OP.add)
        O("vector", f, reads=[pKt, rlnt], writes=[artt])
        def f(e):
            return e.tensor_tensor(artv, artv, Gab, OP.add)
        O("vector", f, reads=[artt, Gat], writes=[artt])
        def f(e):
            return e.activation(art[:, 0:mw], art[:, 0:mw], AF.Exp)
        O("scalar", f, reads=[artt], writes=[artt])
        kqK = KQ[:, 0:2 * mw].rearrange("d (m I h r p) -> d h m I r p",
                                        I=4, h=2, r=4, p=4)[:, 0]
        def f(e, kqK=kqK):
            return e.tensor_tensor(kqK, rmaj(kh), rmaj(art), OP.mult)
        O("vector", f, reads=[kht, artt], writes=[KQt], tag="KH")
        def f(e):
            return e.tensor_tensor(artv, pKv, rlnb, OP.add)
        O("vector", f, reads=[pKt, rlnt], writes=[artt])
        def f(e):
            return e.tensor_tensor(artv, artv, G0rb, OP.add)
        O("vector", f, reads=[artt, G0rt], writes=[artt])
        def f(e):
            return e.tensor_scalar(art[:, 0:mw], art[:, 0:mw], float(-87.0),
                                   None, OP.max)
        O("vector", f, reads=[artt], writes=[artt])
        def f(e):
            return e.activation(art[:, 0:mw], art[:, 0:mw], AF.Exp)
        O("scalar", f, reads=[artt], writes=[artt])
        def f(e):
            return e.tensor_scalar(art[:, 0:mw], art[:, 0:mw], -1.0, None, OP.mult)
        O("vector", f, reads=[artt], writes=[artt])
        def f(e):
            return e.tensor_tensor(bmaj(KG), rmaj(kh), rmaj(art), OP.mult)
        O("vector", f, reads=[kht, artt], writes=[KGt], tag="KG")
        def f(e):
            return e.tensor_tensor(artv, pWv, rlnb, OP.subtract)
        O("vector", f, reads=[pWt, rlnt], writes=[artt])
        def f(e):
            return e.tensor_tensor(artv, artv, Gab, OP.subtract)
        O("vector", f, reads=[artt, Gat], writes=[artt])
        def f(e):
            return e.activation(art[:, 0:mw], art[:, 0:mw], AF.Exp)
        O("scalar", f, reads=[artt], writes=[artt])
        def f(e):
            return e.tensor_tensor(bmaj(WH), rmaj(kh), rmaj(art), OP.mult)
        O("vector", f, reads=[kht, artt], writes=[WHt], tag="WH")
        gev = LAM[:, 8:8 + nm_].unsqueeze(2).broadcast_to([128, nm_, CT])
        def f(e, gev=gev):
            return e.tensor_sub(att[:, 64:64 + tw].rearrange("d (m t) -> d m t", t=CT),
                                gev,
                                G0r[:, 0:tw].rearrange("d (m t) -> d m t", t=CT))
        O("vector", f, reads=[LAMt, G0rt], writes=[attt])
        attb = att[:, 64:64 + tw].unsqueeze(1).broadcast_to([128, 4, tw])
        def f(e):
            return e.tensor_tensor(artv, pWv, rlnb, OP.subtract)
        O("vector", f, reads=[pWt, rlnt], writes=[artt])
        def f(e):
            return e.tensor_tensor(artv, artv, attb, OP.add)
        O("vector", f, reads=[artt, attt], writes=[artt])
        def f(e):
            return e.tensor_scalar(art[:, 0:mw], art[:, 0:mw], float(-87.0),
                                   None, OP.max)
        O("vector", f, reads=[artt], writes=[artt])
        def f(e):
            return e.activation(art[:, 0:mw], art[:, 0:mw], AF.Exp)
        O("scalar", f, reads=[artt], writes=[artt])
        def f(e):
            return e.tensor_tensor(bmaj(WG), rmaj(kh), rmaj(art), OP.mult)
        O("vector", f, reads=[kht, artt], writes=[WGt], tag="WG")
        vhb, vhbt = sv("vhb")
        vlb, vlbt = sv("vlb")
        def f(e):
            return e.activation(bmaj(vhb), rmaj(vhh), AF.Copy)
        O("scalar", f, reads=[vht], writes=[vhbt])
        def f(e):
            return e.activation(bmaj(vlb), rmaj(vll), AF.Copy)
        O("scalar", f, reads=[vlt], writes=[vlbt])
        def f(e):
            return e.tensor_add(att[:, 64:64 + tw], Ga[:, 0:tw], pQ[:, 0:tw])
        O("vector", f, reads=[Gat, pQt], writes=[attt])
        def f(e):
            return e.activation(att[:, 64:64 + tw], att[:, 64:64 + tw], AF.Exp)
        O("scalar", f, reads=[attt], writes=[attt])
        def f(e):
            return e.tensor_mul(at2[:, 64:64 + tw], qs[:, 0:tw], att[:, 64:64 + tw])
        O("vector", f, reads=[qst, attt], writes=[at2t])
        qlv = KQ[:, 0:2 * mw].rearrange("d (m I h r p) -> d h r (m I) p",
                                        I=4, h=2, r=4, p=4)[:, 1]
        qbv = at2[:, 64:64 + tw].rearrange("d (mI p) -> d mI p", p=4)\
            .unsqueeze(1).broadcast_to([128, 4, tw // 4, 4])
        wfv = wfT[:, h:h + 49:16].unsqueeze(2).unsqueeze(3)\
            .broadcast_to([128, 4, tw // 4, 4])
        def f(e, qlv=qlv, qbv=qbv, wfv=wfv):
            return e.tensor_tensor(qlv, qbv, wfv, OP.mult)
        O("vector", f, reads=[at2t, wfTt], writes=[KQt])
        def f(e):
            return e.tensor_add(att[:, 64:64 + tw], G0r[:, 0:tw], pQ[:, 0:tw])
        O("vector", f, reads=[G0rt, pQt], writes=[attt])
        def f(e):
            return e.tensor_scalar(att[:, 64:64 + tw], att[:, 64:64 + tw],
                                   float(-87.0), None, OP.max)
        O("vector", f, reads=[attt], writes=[attt])
        def f(e):
            return e.activation(att[:, 64:64 + tw], att[:, 64:64 + tw], AF.Exp)
        O("scalar", f, reads=[attt], writes=[attt])
        def f(e):
            return e.tensor_mul(at2[:, 64:64 + tw], qs[:, 0:tw], att[:, 64:64 + tw])
        O("vector", f, reads=[qst, attt], writes=[at2t])
        def f(e):
            return e.tensor_scalar(QB[:, 0:tw], at2[:, 64:64 + tw], pfT[:, h:h + 1],
                                   None, OP.mult)
        O("vector", f, reads=[at2t], writes=[QBt], hard=[pfTt], tag="QB")

    def s2_chunk_phases(self, hg, m, mi, nm_, buf, set_):
        nc = self.nc
        v = lambda n: self.tiles[n][0]
        t = lambda n: self.tiles[n][1]
        warmch = self.warm // CT
        O = self.op
        idsb, idsbt = self.tiles["idsb"]
        Wv, Wvt = self.s2buf["Wv"][buf]
        Nd, Ndt = self.s2buf["Nd"][buf]
        Ngap, Ngapt = self.s2buf["Ngap"][buf]
        Agap, Agapt = self.s2buf["Agap"][buf]
        pNM, pNMt = self.tiles["pb0"]
        pA, pAt = self.tiles["pb1"]
        pT, pTt = self.tiles["pb2"]
        pR, pRt = self.tiles["pb3"]
        pH, pHt = self.tiles["pb4"]
        pS, pSt = self.tiles["pb5"]
        pO = self.tiles["pb6"][0][:, 64 * set_:64 * set_ + 64]
        pOt = self.poT[set_]
        Yp, Ypt = self.s2buf["Yp"][buf]
        Uhi, Uhit = self.s2buf["Uhi"][buf]
        Ulo, Ulot = self.s2buf["Ulo"][buf]
        WgT, WgTt = self.s2buf["WgT"][buf]
        ApT, ApTt = self.s2buf["ApT"][buf]
        mM, mMt = self.tiles["mM"]
        mA, mAt = self.tiles["mA"]
        idg, idgt = self.tiles["idg"]
        ppool, ppoolt = self.tiles["ppool"]
        Sf, Sft = self.tiles["Sf"][0], self.sfT[hg]
        Sb, Sbt = self.tiles["Sb"][0], self.sbT[hg]
        ob, obt = self.tiles["obuf"]
        c0 = mi * CMICRO
        JJ = range(4)
        TCOL = {1: 0, 2: 256, 4: 512, 8: 768, 16: 1024, 32: 1280}
        NCOL = (1536, 1792)
        def sj(nm, j):
            if nm in ("QB", "LAM"):
                return self.sl[(nm, j, set_)]
            return self.sl[(nm, j)]
        A = []
        B = []
        def ra():
            A.append([])
            return A[-1].append
        def rb():
            B.append([])
            return B[-1].append
        r_ = ra()
        for j in JJ:
            WH, WHt_ = sj("WH", j)
            dex, dext = sj("dex", j)
            dxv = dex[:, 16 * mi:16 * mi + 16].rearrange(
                "d (I i) -> d I i", i=4).unsqueeze(3).broadcast_to([128, 4, 4, 16])
            wvv = Wv[:, 256 * j:256 * (j + 1)].rearrange(
                "d (I i c) -> d I i c", I=4, i=4)
            whv2 = WH[:, c0:c0 + CMICRO].rearrange(
                "d (i c) -> d i c", i=4).unsqueeze(1).broadcast_to([128, 4, 4, 16])
            def f(e, wvv=wvv, whv2=whv2, dxv=dxv):
                return e.tensor_tensor(wvv, whv2, dxv, OP.mult)
            r_(lambda f=f, WHt_=WHt_, dext=dext: O("vector", f, reads=[WHt_, dext],
                                                   writes=[Wvt], tag="Wv"))
        r_ = ra()
        for j in JJ:
            KQ, KQt_ = sj("KQ", j)
            for I in range(4):
                def f(e, j=j, I=I, KQ=KQ):
                    return e.matmul(pNM[32 * I:32 * I + 32, 64 * j:64 * j + 64],
                                    KQ[:, 128 * mi + 32 * I:128 * mi + 32 * I + 32],
                                    Wv[:, 256 * j + 64 * I:256 * j + 64 * I + 64],
                                    start=True, stop=True, tile_position=(0, 32 * I))
                r_(lambda f=f, KQt_=KQt_: O("tensor", f, reads=[KQt_, Wvt],
                                            writes=[pNMt]))
        r_ = ra()
        mMv = mM[:, 0:CMICRO].unsqueeze(1).broadcast_to([128, 4, CMICRO])
        mAv = mA[:, 0:CMICRO].unsqueeze(1).broadcast_to([128, 4, CMICRO])
        def f(e):
            return e.tensor_tensor(Ngap[:, 0:256].rearrange("d (j c) -> d j c", j=4),
                                   pNM[:, 0:256].rearrange("d (j c) -> d j c", j=4),
                                   mMv, OP.mult)
        r_(lambda f=f: O("vector", f, reads=[pNMt, mMt], writes=[Ngapt]))
        def f(e):
            return e.tensor_tensor(Agap[:, 0:256].rearrange("d (j c) -> d j c", j=4),
                                   pNM[:, 0:256].rearrange("d (j c) -> d j c", j=4),
                                   mAv, OP.mult)
        r_(lambda f=f: O("vector", f, reads=[pNMt, mAt], writes=[Agapt]))
        r_ = ra()
        for h in JJ:
            p, q = h // 2, h % 2
            WG, WGt_ = sj("WG", h)
            def f(e, p=p, q=q, WG=WG):
                return e.matmul(pA[64 * q:64 * q + 64, 128 * p:128 * p + 128],
                                WG[:, c0:c0 + CMICRO], idsb[:, :],
                                start=True, stop=True, tile_position=(0, 64 * q))
            r_(lambda f=f, WGt_=WGt_: O("tensor", f, reads=[WGt_, idsbt],
                                        writes=[pAt]))
        if m >= warmch:
            for h in JJ:
                p, q = h // 2, h % 2
                def f(e, h=h, p=p, q=q):
                    return e.matmul(pA[64 * q:64 * q + 64,
                                       256 + 32 * p + 16 * q:256 + 32 * p + 16 * q + 16],
                                    Agap[:, 64 * h:64 * h + 64], ppool[:, :],
                                    start=True, stop=True, tile_position=(0, 64 * q))
                r_(lambda f=f: O("tensor", f, reads=[Agapt, ppoolt], writes=[pAt]))
        r_ = ra()
        def f(e):
            return e.activation(WgT[:, 0:256], pA[:, 0:256], AF.Copy)
        r_(lambda f=f: O("scalar", f, reads=[pAt], writes=[WgTt]))
        if m >= warmch:
            def f(e):
                return e.activation(ApT[:, 0:64], pA[:, 256:320], AF.Copy)
            r_(lambda f=f: O("scalar", f, reads=[pAt], writes=[ApTt]))
        r_ = ra()
        for h in JJ:
            p, q = h // 2, h % 2
            def f(e, h=h, p=p, q=q):
                return e.matmul(pT[64 * q:64 * q + 64,
                                   128 * p + 64 * q:128 * p + 64 * q + 64],
                                idg[:, 0:64], Ngap[:, 64 * h:64 * h + 64],
                                start=True, stop=True, tile_position=(0, 64 * q))
            r_(lambda f=f: O("tensor", f, reads=[idgt, Ngapt], writes=[pTt]))
            def f(e, h=h, p=p, q=q):
                return e.matmul(pT[64 * q:64 * q + 64,
                                   256 + 128 * p + 64 * q:256 + 128 * p + 64 * q + 64],
                                Ngap[:, 64 * h:64 * h + 64], idg[:, 0:64],
                                start=True, stop=True, tile_position=(0, 64 * q))
            r_(lambda f=f: O("tensor", f, reads=[idgt, Ngapt], writes=[pTt]))
        r_ = ra()
        def f(e):
            return e.activation(Nd[:, NCOL[0]:NCOL[0] + 256], pT[:, 0:256], AF.Copy)
        r_(lambda f=f: O("scalar", f, reads=[pTt], writes=[Ndt]))
        def f(e):
            return e.activation(Nd[:, 0:256], pT[:, 256:512], AF.Copy)
        r_(lambda f=f: O("scalar", f, reads=[pTt], writes=[Ndt]))
        cn = 0
        for lv in (1, 2, 4, 8):
            r_ = ra()
            tc, nc_ = TCOL[lv], NCOL[cn]
            tc2, nc2 = TCOL[2 * lv], NCOL[1 - cn]
            for p in range(2):
                def f(e, p=p, tc=tc, nc_=nc_):
                    return e.matmul(pT[:, 128 * p:128 * p + 128],
                                    Nd[:, tc + 128 * p:tc + 128 * p + 128],
                                    Nd[:, nc_ + 128 * p:nc_ + 128 * p + 128],
                                    start=True, stop=True)
                r_(lambda f=f: O("tensor", f, reads=[Ndt], writes=[pTt]))
                def f(e, p=p, tc=tc, nc_=nc_):
                    return e.matmul(pT[:, 256 + 128 * p:256 + 128 * p + 128],
                                    Nd[:, nc_ + 128 * p:nc_ + 128 * p + 128],
                                    Nd[:, tc + 128 * p:tc + 128 * p + 128],
                                    start=True, stop=True)
                r_(lambda f=f: O("tensor", f, reads=[Ndt], writes=[pTt]))
            r_ = ra()
            def f(e, nc2=nc2):
                return e.activation(Nd[:, nc2:nc2 + 256], pT[:, 0:256], AF.Copy)
            r_(lambda f=f: O("scalar", f, reads=[pTt], writes=[Ndt]))
            def f(e, tc2=tc2):
                return e.activation(Nd[:, tc2:tc2 + 256], pT[:, 256:512], AF.Copy)
            r_(lambda f=f: O("scalar", f, reads=[pTt], writes=[Ndt]))
            cn = 1 - cn
        r_ = ra()
        for p in range(2):
            def f(e, p=p, nc_=NCOL[cn], tc=TCOL[16]):
                return e.matmul(pT[:, 256 + 128 * p:256 + 128 * p + 128],
                                Nd[:, nc_ + 128 * p:nc_ + 128 * p + 128],
                                Nd[:, tc + 128 * p:tc + 128 * p + 128],
                                start=True, stop=True)
            r_(lambda f=f: O("tensor", f, reads=[Ndt], writes=[pTt]))
        r_ = ra()
        def f(e):
            return e.activation(Nd[:, TCOL[32]:TCOL[32] + 256], pT[:, 256:512], AF.Copy)
        r_(lambda f=f: O("scalar", f, reads=[pTt], writes=[Ndt], tag="N32T"))
        r_ = rb()
        for h in JJ:
            p, q = h // 2, h % 2
            KG, KGt_ = sj("KG", h)
            vhb, vhbt_ = sj("vhb", h)
            vlb, vlbt_ = sj("vlb", h)
            hh = 4 * hg + h
            def f(e, p=p, q=q, KG=KG, hh=hh):
                return e.matmul(pR[64 * q:64 * q + 64, 128 * p:128 * p + 128],
                                KG[:, c0:c0 + CMICRO],
                                Sb[:, hh * 128:(hh + 1) * 128],
                                start=True, stop=False, tile_position=(0, 64 * q))
            r_(lambda f=f, KGt_=KGt_: O("tensor", f, reads=[KGt_, Sbt], writes=[pRt]))
            def f(e, p=p, q=q, vhb=vhb):
                return e.matmul(pR[64 * q:64 * q + 64, 128 * p:128 * p + 128],
                                vhb[:, c0:c0 + CMICRO], idsb[:, :],
                                start=False, stop=False, tile_position=(0, 64 * q))
            r_(lambda f=f, vhbt_=vhbt_: O("tensor", f, reads=[vhbt_, idsbt],
                                          writes=[pRt]))
            def f(e, p=p, q=q, vlb=vlb):
                return e.matmul(pR[64 * q:64 * q + 64, 128 * p:128 * p + 128],
                                vlb[:, c0:c0 + CMICRO], idsb[:, :],
                                start=False, stop=True, tile_position=(0, 64 * q))
            r_(lambda f=f, vlbt_=vlbt_: O("tensor", f, reads=[vlbt_, idsbt],
                                          writes=[pRt]))
        if m >= warmch:
            for h in JJ:
                hh = 4 * hg + h
                QB, QBt_ = sj("QB", h)
                def f(e, h=h, hh=hh, QB=QB):
                    return e.matmul(pO[:, 16 * h:16 * h + 16],
                                    Sb[:, hh * 128:(hh + 1) * 128],
                                    QB[:, mi * CT:(mi + 1) * CT],
                                    start=(h == 0), stop=False)
                r_(lambda f=f, QBt_=QBt_: O("tensor", f, reads=[Sbt, QBt_],
                                            writes=[pOt]))
        r_ = rb()
        def f(e):
            return e.activation(Yp[:, 0:256], pR[:, 0:256], AF.Copy)
        r_(lambda f=f: O("scalar", f, reads=[pRt], writes=[Ypt], tag="Y0"))
        cur = 0
        for lv in (32, 16, 8, 4, 2):
            r_ = rb()
            tc = TCOL[lv]
            for p in range(2):
                def f(e, p=p, tc=tc, cur=cur):
                    return e.matmul(pH[:, 128 * p:128 * p + 128],
                                    Nd[:, tc + 128 * p:tc + 128 * p + 128],
                                    Yp[:, 256 * cur + 128 * p:256 * cur + 128 * p + 128],
                                    start=True, stop=False)
                r_(lambda f=f: O("tensor", f, reads=[Ndt, Ypt], writes=[pHt]))
                def f(e, p=p, cur=cur):
                    return e.matmul(pH[:, 128 * p:128 * p + 128], idsb[:, :],
                                    Yp[:, 256 * cur + 128 * p:256 * cur + 128 * p + 128],
                                    start=False, stop=True)
                r_(lambda f=f: O("tensor", f, reads=[idsbt, Ypt], writes=[pHt]))
            def f(e, cur=cur):
                return e.activation(Yp[:, 256 * (1 - cur):256 * (1 - cur) + 256],
                                    pH[:, 0:256], AF.Copy)
            r_(lambda f=f: O("scalar", f, reads=[pHt], writes=[Ypt]))
            cur = 1 - cur
        r_ = rb()
        for p in range(2):
            def f(e, p=p, cur=cur):
                return e.matmul(pH[:, 128 * p:128 * p + 128],
                                Nd[:, 128 * p:128 * p + 128],
                                Yp[:, 256 * cur + 128 * p:256 * cur + 128 * p + 128],
                                start=True, stop=False)
            r_(lambda f=f: O("tensor", f, reads=[Ndt, Ypt], writes=[pHt]))
            def f(e, p=p, cur=cur):
                return e.matmul(pH[:, 128 * p:128 * p + 128], idsb[:, :],
                                Yp[:, 256 * cur + 128 * p:256 * cur + 128 * p + 128],
                                start=False, stop=True)
            r_(lambda f=f: O("tensor", f, reads=[idsbt, Ypt], writes=[pHt]))
        r_ = rb()
        UhiU, UhiUt = self.s2buf["UhiU"][buf]
        UloU, UloUt = self.s2buf["UloU"][buf]
        def f(e):
            return e.activation(Uhi[0:64, 0:256], pH[0:64, 0:256], AF.Copy)
        r_(lambda f=f: O("scalar", f, reads=[pHt], writes=[Uhit]))
        def f(e):
            return e.activation(UhiU[64:128, 0:256], pH[64:128, 0:256], AF.Copy)
        r_(lambda f=f: O("scalar", f, reads=[pHt], writes=[UhiUt]))
        def f(e):
            return e.scalar_tensor_tensor(Ulo[0:64, 0:256], Uhi[0:64, 0:256], -1.0,
                                          pH[0:64, 0:256], OP.mult, OP.add)
        r_(lambda f=f: O("vector", f, reads=[Uhit, pHt], writes=[Ulot]))
        def f(e):
            return e.scalar_tensor_tensor(UloU[64:128, 0:256], UhiU[64:128, 0:256],
                                          -1.0, pH[64:128, 0:256], OP.mult, OP.add)
        r_(lambda f=f: O("vector", f, reads=[UhiUt, pHt], writes=[UloUt]))
        r_ = rb()
        if m >= warmch:
            for p in range(2):
                for ut, utt in ((Uhi, Uhit), (UhiU, UhiUt), (Ulo, Ulot)):
                    def f(e, p=p, ut=ut):
                        return e.matmul(pO[:, 32 * p:32 * p + 32],
                                        ut[:, 128 * p:128 * p + 128],
                                        ApT[:, 32 * p:32 * p + 32],
                                        start=False, stop=False)
                    r_(lambda f=f, utt=utt: O("tensor", f, reads=[utt, ApTt],
                                              writes=[pOt]))
                def f(e, p=p):
                    return e.matmul(pO[:, 32 * p:32 * p + 32],
                                    UloU[:, 128 * p:128 * p + 128],
                                    ApT[:, 32 * p:32 * p + 32],
                                    start=False, stop=(p == 1))
                r_(lambda f=f: O("tensor", f, reads=[UloUt, ApTt], writes=[pOt]))
        for h in JJ:
            p, q = h // 2, h % 2
            uh, uht = (Uhi, Uhit) if q == 0 else (UhiU, UhiUt)
            ul, ult = (Ulo, Ulot) if q == 0 else (UloU, UloUt)
            def f(e, h=h, p=p, uh=uh):
                return e.matmul(pS[:, 128 * h:128 * h + 128],
                                WgT[:, 128 * p:128 * p + 128],
                                uh[:, 128 * p:128 * p + 128],
                                start=True, stop=False)
            r_(lambda f=f, uht=uht: O("tensor", f, reads=[WgTt, uht], writes=[pSt]))
            def f(e, h=h, p=p, ul=ul):
                return e.matmul(pS[:, 128 * h:128 * h + 128],
                                WgT[:, 128 * p:128 * p + 128],
                                ul[:, 128 * p:128 * p + 128],
                                start=False, stop=True)
            r_(lambda f=f, ult=ult: O("tensor", f, reads=[WgTt, ult], writes=[pSt]))
        r_ = rb()
        if m >= warmch:
            toff = (m - warmch) * CT
            obv = ob[:, 4 * hg * self.T_SL:(4 * hg + 4) * self.T_SL].rearrange(
                "d (j t) -> d j t", t=self.T_SL)[:, :, toff:toff + CT]
            def f(e, obv=obv):
                return e.tensor_copy(obv,
                                     pO[:, 0:64].rearrange("d (j t) -> d j t", j=4))
            r_(lambda f=f: O("vector", f, reads=[pOt], writes=[obt], tag="oev"))
        h0 = 4 * hg
        for h in JJ:
            LAM, LAMt_ = sj("LAM", h)
            hh = h0 + h
            def f(e, h=h, hh=hh, LAM=LAM):
                return e.scalar_tensor_tensor(Sf[:, hh * 128:(hh + 1) * 128],
                                              Sf[:, hh * 128:(hh + 1) * 128],
                                              LAM[:, mi:mi + 1],
                                              pS[:, 128 * h:128 * h + 128],
                                              OP.mult, OP.add)
            r_(lambda f=f, LAMt_=LAMt_: O("vector", f, reads=[Sft, pSt],
                                          writes=[Sft], hard=[LAMt_], tag="Sup"))
        def f(e, h0=h0):
            return e.activation(Sb[:, h0 * 128:(h0 + 4) * 128],
                                Sf[:, h0 * 128:(h0 + 4) * 128], AF.Copy)
        r_(lambda f=f: O("scalar", f, reads=[Sft], writes=[Sbt]))
        return A, B[:1], B[1:]

    def stage3(self):
        nc, d = self.nc, self.dram
        T_SL, TX = self.T_SL, self.TX
        xn, xnt = self.tiles["xn"]
        v = lambda n: self.tiles[n][0]
        t = lambda n: self.tiles[n][1]
        ob, obt = self.tiles["obuf"]
        sq, sqt = self.tiles["sq"]
        rt, rtt = self.tiles["rtmp"]
        rb, rbt = self.tiles["rowbf"]
        ft, ftt = v("ftmp"), t("ftmp")
        ft2, ft2t = v("ftmp2"), t("ftmp2")
        kt, ktt = v("ktile"), t("ktile")
        xch, xcht = v("xchunk"), t("xchunk")
        wtf, wtft = v("wtf"), t("wtf")
        wtb, wtbt = v("wtb"), t("wtb")
        onesrow, onesrowt = self.tiles["onesrow"]
        onescolb, onescolbt = self.tiles["onescolb"]
        pss, psst = self.tiles["pss"]
        pbc, pbct = self.tiles["pbc"]
        O = self.op
        self.wtb3T = [Tile("wtb3a"), Tile("wtb3b")]
        def f(e):
            return e.memset(ft[0:1, 0:1], 0.0)
        O("vector", f, writes=[wtft, wtbt] + self.wtb3T + self.s2b1t + [ftt],
          tag="fenceB")
        onwc, onwct = self.sbuf("onwc", [128, 1], F32)
        self.dma("qsync", onwc[:, 0:1], d["onw"], writes=[onwct])
        ffscT, ffscTt = self.sbuf("ffscT", [128, 16], F32)
        ffbT, ffbTt = self.sbuf("ffbT", [128, 16], F32)
        self.dma("qsync", ffscT[:, 0:16],
                 d["ffsc"].rearrange("(i p) o -> p (i o)", p=128),
                 writes=[ffscTt], slow=True)
        self.dma("qsync", ffbT[:, 0:16],
                 d["ffb"].rearrange("(i p) o -> p (i o)", p=128),
                 writes=[ffbTt], slow=True)
        for h in range(H):
            for c0 in range(0, T_SL, 512):
                cw = min(512, T_SL - c0)
                oc0 = h * T_SL + c0
                def f(e, oc0=oc0, cw=cw):
                    return e.tensor_mul(sq[:, 0:cw], ob[:, oc0:oc0 + cw],
                                        ob[:, oc0:oc0 + cw])
                O("vector", f, reads=[obt], writes=[sqt])
                def f(e, cw=cw):
                    return e.matmul(pss[0:1, 0:cw], onescolb[:, 0:1], sq[:, 0:cw],
                                    start=True, stop=True)
                O("tensor", f, reads=[sqt, onescolbt], writes=[psst])
                def f(e, cw=cw):
                    return e.activation(rt[0:1, 0:cw], pss[0:1, 0:cw], AF.Ln,
                                        scale=np.float32(1.0 / DV),
                                        bias=np.float32(EPS))
                O("scalar", f, reads=[psst], writes=[rtt])
                def f(e, cw=cw):
                    return e.activation(rt[0:1, 512:512 + cw], rt[0:1, 0:cw],
                                        AF.Exp, scale=np.float32(-0.5))
                O("scalar", f, reads=[rtt], writes=[rtt])
                def f(e, cw=cw):
                    return e.tensor_copy(rb[0:1, 0:cw], rt[0:1, 512:512 + cw])
                O("vector", f, reads=[rtt], writes=[rbt])
                def f(e, cw=cw):
                    return e.matmul(pbc[:, 0:cw], onesrow[0:1, :], rb[0:1, 0:cw],
                                    start=True, stop=True)
                O("tensor", f, reads=[rbt, onesrowt], writes=[pbct])
                self.dma("qact", kt[:, 0:cw],
                         d["gsp"][h * 128:(h + 1) * 128, c0:c0 + cw],
                         writes=[ktt], tag="g.ld")
                def f(e, cw=cw):
                    return e.activation(ft[:, 0:cw], kt[:, 0:cw], AF.Sigmoid)
                O("scalar", f, reads=[ktt], writes=[ftt])
                def f(e, oc0=oc0, cw=cw):
                    return e.scalar_tensor_tensor(ft2[:, 0:cw], ob[:, oc0:oc0 + cw],
                                                  onwc[:, 0:1], pbc[:, 0:cw],
                                                  OP.mult, OP.mult)
                O("vector", f, reads=[obt, pbct], writes=[ft2t], hard=[onwct])
                def f(e, oc0=oc0, cw=cw):
                    return e.tensor_tensor(ob[:, oc0:oc0 + cw], ft2[:, 0:cw],
                                           ft[:, 0:cw], OP.mult)
                O("vector", f, reads=[ft2t, ftt], writes=[obt], tag="onrm")
        pj0, pj0t = self.tiles["pj0"]
        pj1, pj1t = self.tiles["pj1"]
        W3RR = ("qsync", "qpool", "qact")
        for oc in range(16):
            for wi, wn in enumerate(("wop", "wff")):
                src = d[wn].rearrange("(i p) o -> p i o", p=128)[:, :, oc * 128:(oc + 1) * 128]
                self.dma(W3RR[(2 * oc + wi) % 3],
                         wtf[:, wi * 2048:wi * 2048 + 2048].rearrange(
                             "p (i o) -> p i o", i=16),
                         src, writes=[wtft], tag=f"w3.{wn}{oc}")
                def f(e, wi=wi, oc=oc):
                    return e.tensor_copy(
                        wtb[:, (oc % 2) * 4096 + wi * 2048:
                            (oc % 2) * 4096 + wi * 2048 + 2048],
                        wtf[:, wi * 2048:wi * 2048 + 2048])
                O("gpsimd", f, reads=[wtft], writes=[self.wtb3T[oc % 2]])
            for c0 in range(0, T_SL, 512):
                cw = min(512, T_SL - c0)
                for ic in range(NIC):
                    def f(e, ic=ic, c0=c0, cw=cw, oc=oc):
                        return e.matmul(pj0[:, 0:cw],
                                        wtb[:, (oc % 2) * 4096 + ic * 128:
                                            (oc % 2) * 4096 + (ic + 1) * 128],
                                        ob[:, ic * T_SL + c0:ic * T_SL + c0 + cw],
                                        start=(ic == 0), stop=(ic == NIC - 1))
                    O("tensor", f, reads=[self.wtb3T[oc % 2], obt],
                      writes=[pj0t], tag="mmop")
                xc0 = (self.TX - self.T_SL) + c0
                for ic in range(NIC):
                    def f(e, ic=ic, c0=c0, cw=cw, xc0=xc0, oc=oc):
                        return e.matmul(pj1[:, 0:cw],
                                        wtb[:, (oc % 2) * 4096 + 2048 + ic * 128:
                                            (oc % 2) * 4096 + 2048 + (ic + 1) * 128],
                                        xn[:, ic * self.TX + xc0:ic * self.TX + xc0 + cw],
                                        start=(ic == 0), stop=(ic == NIC - 1))
                    O("tensor", f, reads=[self.wtb3T[oc % 2], xnt],
                      writes=[pj1t], tag="mmff")
                def f(e, cw=cw, oc=oc):
                    return e.activation(ft[:, 0:cw], pj1[:, 0:cw], AF.Silu,
                                        scale=ffscT[:, oc:oc + 1],
                                        bias=ffbT[:, oc:oc + 1])
                O("scalar", f, reads=[pj1t], writes=[ftt], hard=[ffscTt, ffbTt],
                  tag="ffsilu")
                self.dma("qsync", xch[:, 0:cw],
                         d["x_t"][oc * 128:(oc + 1) * 128, xc0:xc0 + cw],
                         writes=[xcht], tag="x3.ld")
                def f(e, cw=cw):
                    return e.tensor_tensor(ft2[:, 0:cw], pj0[:, 0:cw], ft[:, 0:cw],
                                           OP.mult)
                O("vector", f, reads=[pj0t, ftt], writes=[ft2t])
                def f(e, cw=cw):
                    return e.tensor_add(ft2[:, 0:cw], ft2[:, 0:cw], xch[:, 0:cw])
                O("vector", f, reads=[ft2t, xcht], writes=[ft2t], tag="res")
                self.dma("qact", d["out"][oc * 128:(oc + 1) * 128, c0:c0 + cw],
                         ft2[:, 0:cw], reads=[ft2t], tag=f"out.{oc}.{c0}")

    def emit_debug_dumps(self):
        for ent in self.dbg_dump:
            if isinstance(ent, tuple):
                name, pmax, cmax = ent
            else:
                name, pmax, cmax = ent, None, None
            v, t = self.tiles[name]
            shp = list(v.shape)
            if pmax:
                shp = [pmax, cmax]
            dt_ = v.dtype if hasattr(v, 'dtype') else F32
            d = self.nc.dram_tensor("dbg_" + name, shp, dt_,
                                    kind="ExternalOutput").ap()
            src = v[:, :] if not pmax else v[0:pmax, 0:cmax]
            self.dma("qpool", d, src, reads=[t], tag="dbg." + name)

    def emit_blocks(self):
        nc, pl = self.nc, self.pl
        esem = {e: nc.semaphore(f"es_{e}").__enter__() for e in ENGINES}
        qsem = {(q, l): nc.semaphore(f"qs_{q}_{l}").__enter__()
                for q in DMAQ for l in range(NLANES)}
        blkcm = nc.Block()
        blk = blkcm.__enter__()
        self.cms.append(blkcm)

        def run_stream(eng_obj, ename):
            qfinal = {}
            for op in pl.streams[ename]:
                if op.dma_queue:
                    key = (op.dma_queue, op.qpos % NLANES)
                    qfinal[key] = 16 * (op.qpos // NLANES + 1)
            seen = {}
            for op in pl.streams[ename]:
                for key, val in sorted(op.waits.items(), key=str):
                    if seen.get(key, -1) >= val:
                        continue
                    seen[key] = val
                    sem = esem[key[1]] if key[0] == "e" else qsem[(key[1], key[2])]
                    eng_obj.wait_ge(sem, val)
                inst = op.emit(eng_obj)
                if op.dma_queue:
                    inst.then_inc(qsem[(op.dma_queue, op.qpos % NLANES)], 16)
                else:
                    if inst is not None:
                        inst.then_inc(esem[ename], 1)
                    else:
                        raise RuntimeError(f"op {op.idx} {op.tag} returned None")
            for key, val in sorted(qfinal.items()):
                eng_obj.wait_ge(qsem[key], val)

        @blk.tensor
        def _(e):
            run_stream(e, "tensor")

        @blk.vector
        def _(e):
            run_stream(e, "vector")

        @blk.scalar
        def _(e):
            run_stream(e, "scalar")

        @blk.gpsimd
        def _(e):
            run_stream(e, "gpsimd")

        @blk.sync
        def _(e):
            run_stream(e, "sync")


def kernel(**inputs):
    n_cores = 8
    T_SL = 1024
    maps = host_prepare(inputs, n_cores, T_SL)
    nc, _ = build_program(T_SL)
    res = run_bass_kernel_spmd(nc, maps, core_ids=list(range(n_cores)))
    parts = [np.asarray(res.results[ci]["out"]) for ci in range(n_cores)]
    full = np.concatenate(parts, axis=1)
    return np.ascontiguousarray(full.T)[None].astype(np.float32)


# version 17 (final)
# speedup vs baseline: 1.1113x; 1.0354x over previous
import numpy as np
import ml_dtypes
import concourse.bass as bass
import concourse.mybir as mybir
from concourse.bass_utils import run_bass_kernel_spmd

F32 = mybir.dt.float32
BF16 = mybir.dt.bfloat16
AF = mybir.ActivationFunctionType
OP = mybir.AluOpType
NPBF16 = ml_dtypes.bfloat16

H = 16
DH = 128
DV = 128
RR = 4
HID = 2048
NIC = HID // 128
EPS = 1e-5
CT = 16
INNER = 4
CMICRO = 4 * CT
NI = CT // INNER
WARM = 96
HALO = WARM + 3
SLABCH = 4
NEG87 = np.float32(-87.0)


def bfnp(x):
    return np.asarray(x, np.float32).astype(NPBF16)



ENGINES = ("tensor", "vector", "scalar", "gpsimd", "sync")
DMAQ = {"qsync": "sync", "qact": "scalar", "qpool": "gpsimd"}
NLANES = 12


class Op:
    __slots__ = ("idx", "engine", "emit", "deps", "dma_queue", "pos", "tag", "qpos", "waits")

    def __init__(self, idx, engine, emit, deps, dma_queue, tag):
        self.idx = idx
        self.engine = engine
        self.emit = emit
        self.deps = deps
        self.dma_queue = dma_queue
        self.pos = None
        self.tag = tag


class Tile:
    __slots__ = ("name", "wr", "readers")

    def __init__(self, name):
        self.name = name
        self.wr = None
        self.readers = []


class Plan:
    def __init__(self, nc):
        self.nc = nc
        self.ops = []

    def add(self, engine, emit, reads=(), writes=(), hard=(), dma_queue=None,
            tag=""):
        deps = {}
        idx = len(self.ops)
        for t in reads:
            if t.wr is not None:
                deps[t.wr] = max(deps.get(t.wr, 0), 0)
            t.readers.append(idx)
        for t in hard:
            if t.wr is not None:
                deps[t.wr] = 1
            t.readers.append(idx)
        for t in writes:
            if t.wr is not None:
                deps[t.wr] = max(deps.get(t.wr, 0), 0)
            for r in t.readers:
                if r != idx:
                    deps[r] = max(deps.get(r, 0), 0)
            t.wr = idx
            t.readers = []
        self.ops.append(Op(idx, engine, emit, deps, dma_queue, tag))
        return idx

    def finalize(self):
        streams = {e: [] for e in ENGINES}
        qpos = {q: 0 for q in DMAQ}
        cpos = {e: 0 for e in ENGINES}
        for op in self.ops:
            streams[op.engine].append(op)
            if op.dma_queue:
                op.qpos = qpos[op.dma_queue]
                qpos[op.dma_queue] += 1
                op.pos = None
            else:
                op.pos = cpos[op.engine]
                cpos[op.engine] += 1
        for op in self.ops:
            waits = {}
            for d, hardflag in op.deps.items():
                dop = self.ops[d]
                if dop.dma_queue:
                    key = ("q", dop.dma_queue, dop.qpos % NLANES)
                    val = 16 * (dop.qpos // NLANES + 1)
                elif (dop.engine == op.engine and not op.dma_queue
                      and not hardflag and op.engine == "tensor"):
                    continue
                else:
                    key = ("e", dop.engine)
                    val = dop.pos + 1
                if waits.get(key, 0) < val:
                    waits[key] = val
            if op.dma_queue:
                key = ("q", op.dma_queue, op.qpos % NLANES)
                val = 16 * (op.qpos // NLANES)
                if val > 0 and waits.get(key, 0) < val:
                    waits[key] = val
            op.waits = waits
        self.streams = streams



def host_consts():
    time = np.zeros(CMICRO, np.int64)
    tok = np.zeros(CMICRO, np.int64)
    for sp in range(CMICRO):
        I, r, p = sp // 16, (sp % 16) // 4, sp % 4
        time[sp] = 4 * (4 * I + p) + r
        tok[sp] = 4 * I + p
    iblk = tok // INNER
    smask = (time[:, None] > time[None, :]).astype(np.float32)
    imask = (time[:, None] >= time[None, :]).astype(np.float32)
    gapM = np.full(CMICRO, -1, np.int64)
    gapA = np.full(CMICRO, -1, np.int64)
    for I in range(NI):
        rows = [s for s in range(CMICRO) if iblk[s] == I]
        for p, s in enumerate(rows):
            gapM[s] = 32 * I + p
            gapA[s] = 32 * I + 16 + p
    mask_gapM = np.zeros((128, CMICRO), np.float32)
    mask_gapA = np.zeros((128, CMICRO), np.float32)
    for s in range(CMICRO):
        mask_gapM[gapM[s], :] = -smask[s, :]
        mask_gapA[gapA[s], :] = imask[s, :]
    ident_gapM = np.zeros((128, CMICRO), np.float32)
    for a in range(CMICRO):
        ident_gapM[gapM[a], a] = 1.0
    ppool_gap = np.zeros((128, CT), np.float32)
    for s in range(CMICRO):
        ppool_gap[gapA[s], tok[s]] = 1.0
    return {
        "c_maskM": bfnp(mask_gapM),
        "c_maskA": bfnp(mask_gapA),
        "c_identg": bfnp(ident_gapM),
        "c_ppool": bfnp(ppool_gap),
        "c_ident": bfnp(np.eye(128, dtype=np.float32)),
    }, gapM, gapA


def host_prepare(inputs, n_cores, T_SL):
    x = np.asarray(inputs["L0_attn_hidden_states"][0], np.float32)
    xt = np.ascontiguousarray(x.T)

    def t_(name):
        return np.ascontiguousarray(np.asarray(inputs[name], np.float32).T)

    consts, _, _ = host_consts()
    shared = {
        "wq": t_("L0_attn_q_proj_w"), "wk": t_("L0_attn_k_proj_w"),
        "wv": t_("L0_attn_v_proj_w"), "wb": t_("L0_attn_b_proj_w"),
        "wf1": t_("L0_attn_f1_w"), "wf2": t_("L0_attn_f2_w"),
        "wg1": t_("L0_attn_g1_w"), "wg2": t_("L0_attn_g2_w"),
        "wop": t_("L0_attn_o_proj_w"), "wff": t_("L0_ffn_weight"),
        "qcw": np.ascontiguousarray(np.asarray(inputs["L0_attn_q_conv_w"], np.float32)),
        "kcw": np.ascontiguousarray(np.asarray(inputs["L0_attn_k_conv_w"], np.float32)),
        "vcw": np.ascontiguousarray(np.asarray(inputs["L0_attn_v_conv_w"], np.float32)),
        "alog": np.asarray(inputs["L0_attn_A_log"], np.float32).reshape(H, 1),
        "dtb_hd": np.ascontiguousarray(
            np.asarray(inputs["L0_attn_dt_bias"], np.float32).reshape(H, DH)),
        "mlog": np.ascontiguousarray(np.asarray(inputs["L0_attn_micro_logits"], np.float32)),
        "onw": np.asarray(inputs["L0_attn_o_norm_w"], np.float32).reshape(DV, 1),
        "g2b": np.asarray(inputs["L0_attn_g2_b"], np.float32).reshape(HID, 1),
        "ffsc": np.asarray(inputs["L0_ffn_scale_weight"], np.float32).reshape(HID, 1),
        "ffb": np.asarray(inputs["L0_ffn_bias"], np.float32).reshape(HID, 1),
    }
    shared.update(consts)
    maps = []
    for ci in range(n_cores):
        t0 = ci * T_SL
        lo = t0 - HALO
        xs = np.zeros((HID, T_SL + HALO), np.float32)
        s = max(lo, 0)
        xs[:, s - lo:] = xt[:, s:t0 + T_SL]
        m = dict(shared)
        m["x_t"] = xs
        maps.append(m)
    return maps



def build_program(T_SL=1024, warm=WARM, debug=False, dbg_dump=()):
    TW = T_SL + warm
    TX = T_SL + warm + 3
    nch = TW // CT
    assert TW % CT == 0

    nc = bass.Bass("TRN2", target_bir_lowering=False, debug=debug)
    pl = Plan(nc)

    dram = {}
    def din(nm, shp, dt=F32):
        dram[nm] = nc.dram_tensor(nm, shp, dt, kind="ExternalInput").ap()
    din("x_t", [HID, TX])
    din("wq", [HID, HID]); din("wk", [HID, 4 * HID]); din("wv", [HID, 4 * HID])
    din("wb", [HID, 64]); din("wf1", [HID, 128]); din("wf2", [128, HID])
    din("wg1", [HID, 128]); din("wg2", [128, HID]); din("wop", [HID, HID])
    din("wff", [HID, HID])
    din("qcw", [HID, 4]); din("kcw", [4 * HID, 4]); din("vcw", [4 * HID, 4])
    din("alog", [H, 1]); din("dtb_hd", [H, DH]); din("mlog", [H, 4])
    din("onw", [DV, 1]); din("g2b", [HID, 1]); din("ffsc", [HID, 1]); din("ffb", [HID, 1])
    for nm in ("c_maskM", "c_maskA", "c_identg"):
        din(nm, [128, CMICRO], BF16)
    din("c_ppool", [128, CT], BF16)
    din("c_ident", [128, 128], BF16)
    dram["out"] = nc.dram_tensor("out", [HID, T_SL], F32, kind="ExternalOutput").ap()
    def dsc(nm, shp, dt):
        dram[nm] = nc.dram_tensor(nm, shp, dt).ap()
    dsc("ksp", [4 * HID, TW], BF16)
    dsc("vsph", [4 * HID, TW], BF16)
    dsc("vspl", [4 * HID, TW], BF16)
    dsc("qsp", [HID, TW], BF16)
    dsc("lsp", [HID, TW], F32)
    dsc("rowk", [64, TW], F32)
    dsc("lnb", [64, TW], F32)
    dsc("rowq", [16, TW], F32)
    dsc("gsp", [HID, T_SL], BF16)

    b = Builder(nc, pl, dram, T_SL, TW, TX, nch)
    b.dbg_dump = dbg_dump
    b.build()
    return nc, b


class Builder:
    def __init__(self, nc, pl, dram, T_SL, TW, TX, nch):
        self.nc, self.pl, self.dram = nc, pl, dram
        self.T_SL, self.TW, self.TX, self.nch = T_SL, TW, TX, nch
        self.warm = TW - T_SL
        self.cms = []
        self.tiles = {}
        self.psum_used = 0

    def sbuf(self, name, shp, dt):
        h = self.nc.sbuf_tensor(name, shp, dt)
        v = h.__enter__()
        self.cms.append(h)
        t = Tile(name)
        self.tiles[name] = (v, t)
        return v, t

    def psum(self, name, shp, dt=F32):
        h = self.nc.psum_tensor(name, shp, dt)
        v = h.__enter__()
        self.cms.append(h)
        t = Tile(name)
        self.tiles[name] = (v, t)
        return v, t

    def dtile(self, name):
        if name not in self.tiles:
            self.tiles[name] = (None, Tile(name))
        return self.tiles[name][1]

    def dma(self, queue, out_ap, in_ap, reads=(), writes=(), tag="", slow=False):
        eng = DMAQ[queue]
        def emit(e):
            if slow:
                return e.dma_start(out=out_ap, in_=in_ap,
                                   allow_slow_non_contiguous=True)
            return e.dma_start(out=out_ap, in_=in_ap)
        return self.pl.add(eng, emit, reads=reads, writes=writes,
                           dma_queue=queue, tag=tag)

    def op(self, engine, fn, reads=(), writes=(), hard=(), tag=""):
        return self.pl.add(engine, fn, reads=reads, writes=writes, hard=hard,
                           tag=tag)

    def build(self):
        self.alloc_all()
        self.stage0_consts()
        self.stage0_ln()
        self.stage1()
        self.stage2()
        self.stage3()
        self.emit_debug_dumps()
        self.pl.finalize()
        self.emit_blocks()
        for cm in reversed(self.cms):
            cm.__exit__(None, None, None)

    def alloc_all(self):
        TX, TW, T_SL = self.TX, self.TW, self.T_SL
        s = self.sbuf
        p = self.psum
        s("xn", [128, NIC * TX], BF16)
        s("xchunk", [128, 2 * 512], F32)
        s("statrow", [1, 1536], F32)
        s("ybuf", [128, TX + 8], BF16)
        s("wtf", [128, 2 * 2048], F32)
        s("wtb", [128, 4 * 2048], BF16)
        s("ktile", [128, 512], BF16)
        s("vhi", [128, 512], BF16)
        s("vlo", [128, 512], BF16)
        s("ftmp", [128, 512], F32)
        s("ftmp2", [128, 512], F32)
        s("sq", [128, 512], BF16)
        s("rowbf", [1, 1024], BF16)
        s("y1", [128, TX], BF16)
        s("yg1", [128, TX], BF16)
        s("convd", [128, 4 * 128], BF16)
        s("cwall", [128, 144 * 4], F32)
        s("idsb", [128, 128], BF16)
        s("hd_misc", [16, 1792], F32)
        s("lnfT", [128, 16], F32)
        s("rlnA", [128, 64], F32)
        s("AbT", [128, 16], F32)
        s("dtbT", [128, 16], F32)
        s("wfillT", [128, 4 * 16], F32)
        s("fpowT", [128, 4 * 16], F32)
        s("PFT", [128, 16], F32)
        for i in range(8):
            p(f"pb{i}", [128, 512], F32)
        self.tiles["pbc"] = self.tiles["pb0"]
        self.tiles["pj0"] = self.tiles["pb1"]
        self.tiles["pj1"] = self.tiles["pb2"]
        self.tiles["pcv"] = self.tiles["pb3"]
        self.tiles["pss"] = self.tiles["pb4"]

    def stage0_consts(self):
        nc, d = self.nc, self.dram
        v = lambda n: self.tiles[n][0]
        t = lambda n: self.tiles[n][1]
        self.dma("qsync", v("idsb")[:, :], d["c_ident"], writes=[t("idsb")])
        hm, hmt = v("hd_misc"), t("hd_misc")
        ones, onest = self.sbuf("onescol", [128, 2], F32)
        def f(e):
            return e.memset(ones[:, :], 1.0)
        self.op("vector", f, writes=[onest])
        onesrow, onesrowt = self.sbuf("onesrow", [1, 128], BF16)
        def f(e):
            return e.memset(onesrow[:, :], 1.0)
        self.op("vector", f, writes=[onesrowt])
        onesrowf, onesrowft = self.sbuf("onesrowf", [1, 128], F32)
        def f(e):
            return e.memset(onesrowf[:, :], 1.0)
        self.op("vector", f, writes=[onesrowft])
        self.dma("qsync", hm[0:16, 129:133], d["mlog"], writes=[hmt])
        def f(e):
            return e.activation(hm[0:16, 134:138], hm[0:16, 129:133], AF.Exp)
        self.op("scalar", f, reads=[hmt], writes=[hmt])
        def f(e):
            return e.reduce_sum(hm[0:16, 138:139], hm[0:16, 134:138], mybir.AxisListType.X)
        self.op("vector", f, reads=[hmt], writes=[hmt])
        def f(e):
            return e.reciprocal(hm[0:16, 139:140], hm[0:16, 138:139])
        self.op("vector", f, reads=[hmt], writes=[hmt])
        def f(e):
            return e.tensor_scalar(hm[0:16, 134:138], hm[0:16, 134:138],
                                   hm[0:16, 139:140], None, OP.mult)
        self.op("vector", f, reads=[hmt], writes=[hmt], hard=[hmt], tag="wmix")
        m1, m1t = self.sbuf("misc1", [4, 512], F32)
        for r in range(4):
            self.dma("qpool", m1[0:1, 64 + 16 * r:80 + 16 * r],
                     hm[0:16, 134 + r:135 + r],
                     reads=[hmt], writes=[m1t], tag=f"wmixrow{r}")
        self.dma("qsync", m1[0:1, 16:32], d["alog"], writes=[m1t])
        def f(e):
            return e.activation(m1[0:1, 16:32], m1[0:1, 16:32], AF.Exp)
        self.op("scalar", f, reads=[m1t], writes=[m1t], tag="A")
        def f(e):
            return e.tensor_scalar(m1[0:1, 32:48], m1[0:1, 16:32], -1.0, None, OP.mult)
        self.op("vector", f, reads=[m1t], writes=[m1t])
        pbc, pbct = self.tiles["pbc"]
        def f(e):
            return e.matmul(pbc[:, 0:16], onesrowf[0:1, :], m1[0:1, 32:48],
                            start=True, stop=True)
        self.op("tensor", f, reads=[m1t, onesrowft], writes=[pbct])
        ab, abt = v("AbT"), t("AbT")
        def f(e):
            return e.tensor_copy(ab[:, 0:16], pbc[:, 0:16])
        self.op("vector", f, reads=[pbct], writes=[abt], tag="AbT")
        db, dbt = v("dtbT"), t("dtbT")
        self.dma("qsync", db[:, 0:16], d["dtb_hd"].transpose([1, 0]),
                 writes=[dbt], tag="dtbT", slow=True)
        lf, lft = v("lnfT"), t("lnfT")
        sc1, sc1t = self.sbuf("csc1", [128, 16], F32)
        def f(e):
            return e.activation(sc1[:, 0:16], db[:, 0:16], AF.Abs, bias=np.float32(-1e4))
        self.op("scalar", f, reads=[dbt], writes=[sc1t])
        def f(e):
            return e.activation(sc1[:, 0:16], sc1[:, 0:16], AF.Exp, scale=np.float32(-1.0))
        self.op("scalar", f, reads=[sc1t], writes=[sc1t])
        def f(e):
            return e.activation(sc1[:, 0:16], sc1[:, 0:16], AF.Ln, bias=np.float32(1.0))
        self.op("scalar", f, reads=[sc1t], writes=[sc1t])
        def f(e):
            return e.activation(lf[:, 0:16], db[:, 0:16], AF.Relu, bias=np.float32(-1e4))
        self.op("scalar", f, reads=[dbt], writes=[lft])
        def f(e):
            return e.tensor_add(lf[:, 0:16], lf[:, 0:16], sc1[:, 0:16])
        self.op("vector", f, reads=[lft, sc1t], writes=[lft])
        def f(e):
            return e.tensor_mul(lf[:, 0:16], lf[:, 0:16], ab[:, 0:16])
        self.op("vector", f, reads=[lft, abt], writes=[lft])
        def f(e):
            return e.tensor_scalar(lf[:, 0:16], lf[:, 0:16], -2.0, None, OP.max)
        self.op("vector", f, reads=[lft], writes=[lft], tag="lnfT")
        rlA, rlAt = v("rlnA"), t("rlnA")
        for r in range(4):
            def f(e, r=r):
                return e.tensor_scalar(rlA[:, r:64:4], lf[:, 0:16], float(r),
                                       None, OP.mult)
            self.op("vector", f, reads=[lft], writes=[rlAt])
        fp, fpt = v("fpowT"), t("fpowT")
        def f(e):
            return e.memset(fp[:, 0:16], 1.0)
        self.op("vector", f, writes=[fpt])
        def f(e):
            return e.activation(fp[:, 16:32], lf[:, 0:16], AF.Exp)
        self.op("scalar", f, reads=[lft], writes=[fpt], tag="fillT")
        for r in (2, 3):
            def f(e, r=r):
                return e.tensor_mul(fp[:, 16 * r:16 * r + 16],
                                    fp[:, 16 * (r - 1):16 * r], fp[:, 16:32])
            self.op("vector", f, reads=[fpt], writes=[fpt])
        wf, wft = v("wfillT"), t("wfillT")
        pf, pft = v("PFT"), t("PFT")
        for r in range(4):
            def f(e, r=r):
                return e.matmul(pbc[:, 32 + 16 * r:48 + 16 * r], onesrowf[0:1, :],
                                m1[0:1, 64 + 16 * r:80 + 16 * r], start=True, stop=True)
            self.op("tensor", f, reads=[m1t, onesrowft], writes=[pbct])
        for r in range(4):
            def f(e, r=r):
                return e.tensor_mul(wf[:, 16 * r:16 * r + 16],
                                    fp[:, 16 * r:16 * r + 16],
                                    pbc[:, 32 + 16 * r:48 + 16 * r])
            self.op("vector", f, reads=[fpt, pbct], writes=[wft])
        def f(e):
            return e.tensor_scalar(wf[:, 0:64], wf[:, 0:64], float(DH ** -0.5),
                                   None, OP.mult)
        self.op("vector", f, reads=[wft], writes=[wft])
        def f(e):
            return e.tensor_add(pf[:, 0:16], wf[:, 0:16], wf[:, 16:32])
        self.op("vector", f, reads=[wft], writes=[pft])
        def f(e):
            return e.tensor_add(pf[:, 0:16], pf[:, 0:16], wf[:, 32:48])
        self.op("vector", f, reads=[wft, pft], writes=[pft])
        def f(e):
            return e.tensor_add(pf[:, 0:16], pf[:, 0:16], wf[:, 48:64])
        self.op("vector", f, reads=[wft, pft], writes=[pft], tag="PFT")

    def stage0_ln(self):
        nc, d = self.nc, self.dram
        TX = self.TX
        v = lambda n: self.tiles[n][0]
        t = lambda n: self.tiles[n][1]
        xch, xcht = v("xchunk"), t("xchunk")
        pst, pstt = self.tiles["pb5"]
        ps2, ps2t = self.tiles["pb6"]
        sr, srt = self.tiles["statrow"]
        pbc, pbct = self.tiles["pbc"]
        xn, xnt = v("xn"), t("xn")
        ft, ftt = v("ftmp"), t("ftmp")
        idsb, idsbt = self.tiles["idsb"]
        ntile = (TX + 511) // 512
        ones, onest = self.tiles["onescol"]
        onesrow, onesrowt = self.tiles["onesrow"]
        for ti in range(ntile):
            c0 = ti * 512
            cw = min(512, TX - c0)
            for ic in range(NIC):
                buf = (ic % 2) * 512
                self.dma("qsync", xch[:, buf:buf + cw],
                         d["x_t"][ic * 128:(ic + 1) * 128, c0:c0 + cw],
                         writes=[xcht], tag=f"ln.x.{ti}.{ic}")
                def f(e, buf=buf, cw=cw, ic=ic):
                    return e.matmul(pst[0:1, 0:cw], ones[:, 0:1], xch[:, buf:buf + cw],
                             start=(ic == 0), stop=(ic == NIC - 1))
                self.op("tensor", f, reads=[xcht, onest], writes=[pstt])
                def f(e, buf=buf, cw=cw):
                    return e.tensor_mul(ft[:, 0:cw], xch[:, buf:buf + cw], xch[:, buf:buf + cw])
                self.op("vector", f, reads=[xcht], writes=[ftt])
                def f(e, cw=cw, ic=ic):
                    return e.matmul(ps2[0:1, 0:cw], ones[:, 1:2], ft[:, 0:cw],
                             start=(ic == 0), stop=(ic == NIC - 1))
                self.op("tensor", f, reads=[ftt, onest], writes=[ps2t])
            def f(e, cw=cw):
                return e.tensor_scalar(sr[0:1, 0:cw], pst[0:1, 0:cw],
                                float(1.0 / HID), None, OP.mult)
            self.op("vector", f, reads=[pstt], writes=[srt])
            def f(e, cw=cw):
                return e.tensor_mul(sr[0:1, 1024:1024 + cw], sr[0:1, 0:cw], sr[0:1, 0:cw])
            self.op("vector", f, reads=[srt], writes=[srt])
            def f(e, cw=cw):
                return e.scalar_tensor_tensor(sr[0:1, 1024:1024 + cw], ps2[0:1, 0:cw],
                                       float(1.0 / HID), sr[0:1, 1024:1024 + cw],
                                       OP.mult, OP.subtract)
            self.op("vector", f, reads=[ps2t, srt], writes=[srt], tag="var")
            def f(e, cw=cw):
                return e.activation(sr[0:1, 1024:1024 + cw], sr[0:1, 1024:1024 + cw],
                             AF.Ln, bias=np.float32(EPS))
            self.op("scalar", f, reads=[srt], writes=[srt])
            def f(e, cw=cw):
                return e.activation(sr[0:1, 512:512 + cw], sr[0:1, 1024:1024 + cw],
                             AF.Exp, scale=np.float32(-0.5))
            self.op("scalar", f, reads=[srt], writes=[srt], tag="rstd")
            rb, rbt = self.tiles["rowbf"]
            def f(e, cw=cw):
                return e.tensor_copy(rb[0:1, 0:cw], sr[0:1, 0:cw])
            self.op("vector", f, reads=[srt], writes=[rbt])
            def f(e, cw=cw):
                return e.tensor_copy(rb[0:1, 512:512 + cw], sr[0:1, 512:512 + cw])
            self.op("vector", f, reads=[srt], writes=[rbt])
            def f(e, cw=cw):
                return e.matmul(pbc[:, 0:cw], onesrow[0:1, :], rb[0:1, 0:cw],
                         start=True, stop=True)
            self.op("tensor", f, reads=[rbt, onesrowt], writes=[pbct], tag="bc.mu")
            pb2, pb2t = self.tiles["pj1"]
            def f(e, cw=cw):
                return e.matmul(pb2[:, 0:cw], onesrow[0:1, :], rb[0:1, 512:512 + cw],
                         start=True, stop=True)
            self.op("tensor", f, reads=[rbt, onesrowt], writes=[pb2t], tag="bc.rs")
            for ic in range(NIC):
                buf = (ic % 2) * 512
                self.dma("qsync", xch[:, buf:buf + cw],
                         d["x_t"][ic * 128:(ic + 1) * 128, c0:c0 + cw],
                         writes=[xcht], tag=f"ln.x2.{ti}.{ic}")
                def f(e, buf=buf, cw=cw):
                    return e.tensor_sub(ft[:, 0:cw], xch[:, buf:buf + cw], pbc[:, 0:cw])
                self.op("vector", f, reads=[xcht, pbct], writes=[ftt])
                def f(e, ic=ic, c0=c0, cw=cw):
                    return e.tensor_tensor(xn[:, ic * TX + c0: ic * TX + c0 + cw],
                                    ft[:, 0:cw], pb2[:, 0:cw], OP.mult)
                self.op("vector", f, reads=[ftt, pb2t], writes=[xnt], tag="xn")

    def stage1(self):
        nc, d = self.nc, self.dram
        TX, TW, T_SL = self.TX, self.TW, self.T_SL
        v = lambda n: self.tiles[n][0]
        t = lambda n: self.tiles[n][1]
        xn, xnt = v("xn"), t("xn")
        wtf, wtft = v("wtf"), t("wtf")
        wtb, wtbt = v("wtb"), t("wtb")
        ybuf, ybuft = v("ybuf"), t("ybuf")
        convd, convdt = v("convd"), t("convd")
        idsb, idsbt = self.tiles["idsb"]
        onescolb, onescolbt = self.sbuf("onescolb", [128, 1], BF16)
        def f(e):
            return e.memset(onescolb[:, :], 1.0)
        self.op("vector", f, writes=[onescolbt])
        rtmp, rtmpt = self.sbuf("rtmp", [1, 1152], F32)
        g2bT, g2bTt = self.sbuf("g2bT", [128, 16], F32)
        self.dma("qsync", g2bT[:, 0:16],
                 d["g2b"].rearrange("(i p) o -> p (i o)", p=128),
                 writes=[g2bTt], slow=True, tag="g2bT")
        cwa, cwat = v("cwall"), t("cwall")
        for wn, base, noc in (("qcw", 0, 16), ("kcw", 64, 64), ("vcw", 320, 64)):
            self.dma("qsync", cwa[:, base:base + 4 * noc].rearrange(
                "p (o c) -> p o c", c=4),
                d[wn].rearrange("(o p) c -> p o c", p=128),
                writes=[cwat], tag=f"cw.{wn}")

        jobs = []
        for oc in range(16):
            jobs.append(("q", oc))
        for oc in range(64):
            jobs.append(("k", oc))
        for oc in range(64):
            jobs.append(("v", oc))
        jobs.append(("b", 0))
        jobs.append(("f1", 0))
        for oc in range(16):
            jobs.append(("f2", oc))
        jobs.append(("g1", 0))
        for oc in range(16):
            jobs.append(("g2", oc))

        W = {"q": ("wq", 128), "k": ("wk", 128), "v": ("wv", 128),
             "b": ("wb", 64), "f1": ("wf1", 128), "f2": ("wf2", 128),
             "g1": ("wg1", 128), "g2": ("wg2", 128)}

        WQRR = ("qsync", "qpool")
        def wload(jid, name, oc):
            wn, ww = W[name]
            buf = (jid % 2) * 2048
            nicj = 1 if name in ("f2", "g2") else NIC
            src = d[wn].rearrange("(i p) o -> p i o", p=128)[:, :, oc * ww:(oc + 1) * ww]
            self.dma(WQRR[jid % 2], wtf[:, buf:buf + nicj * ww].rearrange(
                "p (i o) -> p i o", i=nicj), src, writes=[wtft],
                tag=f"wl.{name}{oc}")
            def f(e, buf=buf, n=nicj * ww):
                return e.tensor_copy(wtb[:, buf:buf + n], wtf[:, buf:buf + n])
            self.op("gpsimd", f, reads=[wtft], writes=[wtbt], tag=f"wc.{name}{oc}")
            return buf, nicj, ww

        pend = {}
        pend[0] = wload(0, *jobs[0])
        for jid, (name, oc) in enumerate(jobs):
            if jid + 1 < len(jobs):
                pend[jid + 1] = wload(jid + 1, *jobs[jid + 1])
            buf, nicj, ww = pend.pop(jid)
            shift = 0 if name in ("q", "k", "v", "f1", "g1") else 3
            isconv = name in ("q", "k", "v")
            pj, pjt = self.tiles["pj0" if jid % 2 == 0 else "pj1"]
            if name in ("f2",):
                mov, movt = v("y1"), t("y1")
            elif name in ("g2",):
                mov, movt = v("yg1"), t("yg1")
            else:
                mov, movt = xn, xnt
            if name == "g2":
                tiles = [(self.warm + 3 + i, min(512, TX - (self.warm + 3 + i)))
                         for i in range(0, T_SL, 512)]
            else:
                tiles = [(shift + i, min(512, TX - shift - i))
                         for i in range(0, TX - shift, 512)]
            for ti, (c0, cw) in enumerate(tiles):
                for ic in range(nicj):
                    if name in ("f2", "g2"):
                        mv = mov[:, c0:c0 + cw]
                    else:
                        mv = mov[:, ic * TX + c0:ic * TX + c0 + cw]
                    def f(e, buf=buf, ic=ic, ww=ww, mv=mv, cw=cw, nicj=nicj, pj=pj):
                        return e.matmul(pj[0:ww, 0:cw],
                                        wtb[:, buf + ic * ww:buf + (ic + 1) * ww],
                                        mv, start=(ic == 0), stop=(ic == nicj - 1))
                    self.op("tensor", f, reads=[wtbt, movt], writes=[pjt],
                            tag=f"mm.{name}{oc}.{ti}")
                if isconv:
                    def f(e, c0=c0, cw=cw, pj=pj):
                        return e.tensor_copy(ybuf[:, c0:c0 + cw], pj[:, 0:cw])
                    self.op("vector", f, reads=[pjt], writes=[ybuft])
                elif name == "f1":
                    y1v, y1t = v("y1"), t("y1")
                    def f(e, c0=c0, cw=cw, pj=pj):
                        return e.tensor_copy(y1v[:, c0:c0 + cw], pj[:, 0:cw])
                    self.op("vector", f, reads=[pjt], writes=[y1t])
                elif name == "g1":
                    yg, ygt = v("yg1"), t("yg1")
                    def f(e, c0=c0, cw=cw, pj=pj):
                        return e.tensor_copy(yg[:, c0:c0 + cw], pj[:, 0:cw])
                    self.op("vector", f, reads=[pjt], writes=[ygt])
                elif name == "b":
                    self.ep_beta(pj, pjt, c0 - 3, cw)
                elif name == "f2":
                    self.ep_L(pj, pjt, oc, c0 - 3, cw)
                elif name == "g2":
                    self.ep_gate(pj, pjt, oc, c0 - (self.warm + 3), cw, g2bT, g2bTt)
            if isconv:
                self.ep_conv(name, oc)

    def ep_conv(self, name, oc):
        nc, d = self.nc, self.dram
        TW, TX = self.TW, self.TX
        v = lambda n: self.tiles[n][0]
        t = lambda n: self.tiles[n][1]
        ybuf, ybuft = v("ybuf"), t("ybuf")
        convd, convdt = v("convd"), t("convd")
        idsb, idsbt = self.tiles["idsb"]
        pcv, pcvt = self.tiles["pcv"]
        pss, psst = self.tiles["pss"]
        ft, ftt = v("ftmp"), t("ftmp")
        ft2, ft2t = v("ftmp2"), t("ftmp2")
        kt, ktt = v("ktile"), t("ktile")
        sq, sqt = v("sq"), t("sq")
        rt, rtt = v("rtmp"), t("rtmp")
        onescolb, onescolbt = self.tiles["onescolb"]
        cwa, cwat = v("cwall"), t("cwall")
        cbase = {"q": 0, "k": 64, "v": 320}[name] + 4 * oc
        for i in range(4):
            def f(e, i=i):
                return e.tensor_scalar(convd[:, i * 128:(i + 1) * 128],
                                       idsb[:, :], cwa[:, cbase + i:cbase + i + 1],
                                       None, OP.mult)
            self.op("vector", f, reads=[idsbt], writes=[convdt], hard=[cwat],
                    tag=f"diag.{name}{oc}.{i}")
        for tt0 in range(0, TW, 512):
            cwt = min(512, TW - tt0)
            for i in range(4):
                def f(e, tt0=tt0, i=i, cwt=cwt):
                    return e.matmul(pcv[:, 0:cwt], convd[:, i * 128:(i + 1) * 128],
                                    ybuf[:, tt0 + i:tt0 + i + cwt],
                                    start=(i == 0), stop=(i == 3))
                self.op("tensor", f, reads=[convdt, ybuft], writes=[pcvt],
                        tag=f"cv.{name}{oc}.{tt0}")
            def f(e, cwt=cwt):
                return e.activation(ft[:, 0:cwt], pcv[:, 0:cwt], AF.Silu)
            self.op("scalar", f, reads=[pcvt], writes=[ftt], tag="silu")
            def f(e, cwt=cwt):
                return e.tensor_copy(kt[:, 0:cwt], ft[:, 0:cwt])
            self.op("vector", f, reads=[ftt], writes=[ktt])
            if name == "v":
                vh, vht = v("vhi"), t("vhi")
                vl, vlt = v("vlo"), t("vlo")
                def f(e, cwt=cwt):
                    return e.tensor_copy(vh[:, 0:cwt], ft[:, 0:cwt])
                self.op("vector", f, reads=[ftt], writes=[vht])
                def f(e, cwt=cwt):
                    return e.scalar_tensor_tensor(vl[:, 0:cwt], vh[:, 0:cwt], -1.0,
                                                  ft[:, 0:cwt], OP.mult, OP.add)
                self.op("vector", f, reads=[vht, ftt], writes=[vlt])
                self.dma("qact", d["vsph"][oc * 128:(oc + 1) * 128, tt0:tt0 + cwt],
                         vh[:, 0:cwt], reads=[vht], tag=f"sp.vh{oc}")
                self.dma("qact", d["vspl"][oc * 128:(oc + 1) * 128, tt0:tt0 + cwt],
                         vl[:, 0:cwt], reads=[vlt], tag=f"sp.vl{oc}")
            else:
                dst = "qsp" if name == "q" else "ksp"
                self.dma("qact", d[dst][oc * 128:(oc + 1) * 128, tt0:tt0 + cwt],
                         kt[:, 0:cwt], reads=[ktt], tag=f"sp.{name}{oc}")
                def f(e, cwt=cwt):
                    return e.tensor_mul(sq[:, 0:cwt], kt[:, 0:cwt], kt[:, 0:cwt])
                self.op("vector", f, reads=[ktt], writes=[sqt])
                def f(e, cwt=cwt):
                    return e.matmul(pss[0:1, 0:cwt], onescolb[:, 0:1], sq[:, 0:cwt],
                                    start=True, stop=True)
                self.op("tensor", f, reads=[sqt, onescolbt], writes=[psst])
                def f(e, tt0=tt0, cwt=cwt):
                    return e.tensor_copy(rt[0:1, tt0:tt0 + cwt], pss[0:1, 0:cwt])
                self.op("vector", f, reads=[psst], writes=[rtt])
        if name != "v":
            def f(e):
                return e.activation(rt[0:1, 0:TW], rt[0:1, 0:TW], AF.Ln,
                                    bias=np.float32(1e-6))
            self.op("scalar", f, reads=[rtt], writes=[rtt])
            def f(e):
                return e.tensor_scalar(rt[0:1, 0:TW], rt[0:1, 0:TW],
                                       -0.5, None, OP.mult)
            self.op("vector", f, reads=[rtt], writes=[rtt])
            rowd = "rowq" if name == "q" else "rowk"
            self.dma("qact", d[rowd][oc:oc + 1, 0:TW],
                     rt[0:1, 0:TW], reads=[rtt], tag=f"row.{name}{oc}")

    def ep_beta(self, pj, pjt, tt0, cw):
        d = self.dram
        v = lambda n: self.tiles[n][0]
        t = lambda n: self.tiles[n][1]
        ft, ftt = v("ftmp"), t("ftmp")
        ft2, ft2t = v("ftmp2"), t("ftmp2")
        def f(e):
            return e.activation(ft[0:64, 0:cw], pj[0:64, 0:cw], AF.Relu,
                                scale=np.float32(-1.0))
        self.op("scalar", f, reads=[pjt], writes=[ftt])
        def f(e):
            return e.activation(ft2[0:64, 0:cw], pj[0:64, 0:cw], AF.Abs)
        self.op("scalar", f, reads=[pjt], writes=[ft2t])
        def f(e):
            return e.activation(ft2[0:64, 0:cw], ft2[0:64, 0:cw], AF.Exp,
                                scale=np.float32(-1.0))
        self.op("scalar", f, reads=[ft2t], writes=[ft2t])
        def f(e):
            return e.activation(ft2[0:64, 0:cw], ft2[0:64, 0:cw], AF.Ln,
                                bias=np.float32(1.0))
        self.op("scalar", f, reads=[ft2t], writes=[ft2t])
        def f(e):
            return e.scalar_tensor_tensor(ft[0:64, 0:cw], ft[0:64, 0:cw], -1.0,
                                          ft2[0:64, 0:cw], OP.mult, OP.subtract)
        self.op("vector", f, reads=[ftt, ft2t], writes=[ftt], tag="lnb")
        self.dma("qact", d["lnb"][0:64, tt0:tt0 + cw], ft[0:64, 0:cw],
                 reads=[ftt], tag="sp.lnb")

    def ep_L(self, pj, pjt, h, tt0, cw):
        d = self.dram
        v = lambda n: self.tiles[n][0]
        t = lambda n: self.tiles[n][1]
        ft, ftt = v("ftmp"), t("ftmp")
        ft2, ft2t = v("ftmp2"), t("ftmp2")
        db, dbt = v("dtbT"), t("dtbT")
        ab, abt = v("AbT"), t("AbT")
        def f(e):
            return e.activation(ft[:, 0:cw], pj[:, 0:cw], AF.Relu,
                                bias=db[:, h:h + 1])
        self.op("scalar", f, reads=[pjt], writes=[ftt], hard=[dbt])
        def f(e):
            return e.activation(ft2[:, 0:cw], pj[:, 0:cw], AF.Abs,
                                bias=db[:, h:h + 1])
        self.op("scalar", f, reads=[pjt], writes=[ft2t], hard=[dbt])
        def f(e):
            return e.activation(ft2[:, 0:cw], ft2[:, 0:cw], AF.Exp,
                                scale=np.float32(-1.0))
        self.op("scalar", f, reads=[ft2t], writes=[ft2t])
        def f(e):
            return e.activation(ft2[:, 0:cw], ft2[:, 0:cw], AF.Ln,
                                bias=np.float32(1.0))
        self.op("scalar", f, reads=[ft2t], writes=[ft2t])
        def f(e):
            return e.tensor_add(ft[:, 0:cw], ft[:, 0:cw], ft2[:, 0:cw])
        self.op("vector", f, reads=[ftt, ft2t], writes=[ftt])
        def f(e):
            return e.tensor_scalar(ft[:, 0:cw], ft[:, 0:cw], ab[:, h:h + 1],
                                   -17.0, OP.mult, OP.max)
        self.op("vector", f, reads=[ftt], writes=[ftt], hard=[abt], tag="L")
        self.dma("qact", d["lsp"][h * 128:(h + 1) * 128, tt0:tt0 + cw],
                 ft[:, 0:cw], reads=[ftt], tag=f"sp.L{h}")

    def ep_gate(self, pj, pjt, oc, c0out, cw, g2bT, g2bTt):
        d = self.dram
        v = lambda n: self.tiles[n][0]
        t = lambda n: self.tiles[n][1]
        vh, vht = v("vhi"), t("vhi")
        def f(e):
            return e.tensor_scalar(vh[:, 0:cw], pj[:, 0:cw],
                                   g2bT[:, oc:oc + 1], None, OP.add)
        self.op("vector", f, reads=[pjt], writes=[vht], hard=[g2bTt], tag="gate")
        self.dma("qact", d["gsp"][oc * 128:(oc + 1) * 128, c0out:c0out + cw],
                 vh[:, 0:cw], reads=[vht], tag=f"sp.g{oc}")

    def stage2(self):
        nc, d = self.nc, self.dram
        TW, T_SL = self.TW, self.T_SL
        nch = TW // CT
        v = lambda n: self.tiles[n][0]
        t = lambda n: self.tiles[n][1]
        s = self.sbuf
        NG = 4
        Sf, Sft = s("Sf", [128, H * 128], F32)
        Sb, Sbt = s("Sb", [128, H * 128], BF16)
        ob, obt = s("obuf", [128, H * T_SL], BF16)
        self.sfT = [Tile(f"Sf{g}") for g in range(4)]
        self.sbT = [Tile(f"Sb{g}") for g in range(4)]
        self.poT = [Tile("pO0"), Tile("pO1")]
        def f(e):
            return e.memset(Sf[:, :], 0.0)
        self.op("vector", f, writes=self.sfT)
        def f(e):
            return e.memset(Sb[:, :], 0.0)
        self.op("vector", f, writes=self.sbT)
        self.sl = {}
        for j in range(NG):
            for nm, wmul, dt_ in [("kh", 4, BF16), ("vh", 4, BF16), ("vl", 4, BF16),
                                  ("vhb", 4, BF16), ("vlb", 4, BF16),
                                  ("KQ", 8, BF16), ("WH", 4, BF16), ("KG", 4, BF16),
                                  ("WG", 4, BF16),
                                  ("qs", 1, BF16), ("Ls", 1, F32), ("Ct", 1, None),
                                  ("G0", 1, F32), ("Ga", 1, F32), ("G0r", 1, F32),
                                  ("QB", 1, BF16), ("dex", 1, None),
                                  ("rows", 1, None), ("LAM", 1, F32)]:
                if nm == "rows":
                    self.sl[(nm, j)] = s(f"sl_{nm}{j}", [1, 9 * SLABCH * CT], F32)
                elif nm == "Ct":
                    self.sl[(nm, j)] = s(f"sl_{nm}{j}", [128, SLABCH * CT + 1], F32)
                elif nm == "dex":
                    self.sl[(nm, j)] = s(f"sl_{nm}{j}", [128, 16 * SLABCH], F32)
                elif nm == "LAM":
                    for st in range(2):
                        self.sl[(nm, j, st)] = s(f"sl_{nm}{j}_{st}", [128, 16], F32)
                elif nm == "QB":
                    for st in range(2):
                        self.sl[(nm, j, st)] = s(f"sl_{nm}{j}_{st}",
                                                 [128, wmul * SLABCH * CT], dt_)
                else:
                    self.sl[(nm, j)] = s(f"sl_{nm}{j}", [128, wmul * SLABCH * CT], dt_)
        for nm, shp, dt_ in [("arg", [128, SLABCH * CMICRO], F32),
                             ("att", [128, SLABCH * CT + 64], F32),
                             ("att2", [128, SLABCH * CT + 64], F32),
                             ("anch", [128, 4], F32),
                             ("dexp", [128, 16], F32),
                             ("Wv", [128, 1024], BF16),
                             ("Nd", [128, 2048], BF16),
                             ("Yp", [128, 512], BF16),
                             ("Uhi", [128, 256], BF16), ("Ulo", [128, 256], BF16),
                             ("UhiU", [128, 256], BF16), ("UloU", [128, 256], BF16),
                             ("WgT", [128, 256], BF16), ("ApT", [128, 64], BF16),
                             ("Ngap", [128, 256], BF16), ("Agap", [128, 256], BF16),
                             ("mM", [128, CMICRO], BF16), ("mA", [128, CMICRO], BF16),
                             ("idg", [128, CMICRO], BF16), ("ppool", [128, CT], BF16)]:
            s(nm, shp, dt_)
        self.dma("qsync", v("mM")[:, :], d["c_maskM"], writes=[t("mM")])
        self.dma("qsync", v("mA")[:, :], d["c_maskA"], writes=[t("mA")])
        self.dma("qsync", v("idg")[:, :], d["c_identg"], writes=[t("idg")])
        self.dma("qsync", v("ppool")[:, :], d["c_ppool"], writes=[t("ppool")])
        wtfB = v("wtf").bitcast(BF16)
        self.s2buf = {}
        off = 0
        for nm, shp in [("Nd", [128, 2048]), ("Wv", [128, 1024]),
                        ("Ngap", [128, 256]), ("Agap", [128, 256]),
                        ("WgT", [128, 256]), ("ApT", [128, 64]),
                        ("Yp", [128, 512]), ("Uhi", [128, 256]),
                        ("Ulo", [128, 256]), ("UhiU", [128, 256]),
                        ("UloU", [128, 256])]:
            t1 = Tile(nm + "B1")
            ap1 = wtfB[0:shp[0], off:off + shp[1]]
            off += shp[1]
            self.s2buf[nm] = [(v(nm), t(nm)), (ap1, t1)]
        self.s2b1t = [self.s2buf[nm][1][1]
                      for nm in ("Nd", "Wv", "Ngap", "Agap", "WgT", "ApT",
                                 "Yp", "Uhi", "Ulo", "UhiU", "UloU")]
        ftm = v("ftmp")
        def f(e):
            return e.memset(ftm[0:1, 0:1], 0.0)
        self.op("vector", f, writes=[t("wtf")] + self.s2b1t + [t("ftmp")],
                tag="fenceA")
        pTz = self.tiles["pb2"][0]
        def f(e):
            return e.memset(pTz[:, 0:512], 0.0)
        self.op("vector", f, writes=[self.tiles["pb2"][1]])
        pAz = self.tiles["pb1"][0]
        def f(e):
            return e.memset(pAz[:, 256:320], 0.0)
        self.op("vector", f, writes=[self.tiles["pb1"][1]])
        for nmz in ("att", "att2"):
            vz = v(nmz)
            def f(e, vz=vz):
                return e.memset(vz[:, 0:64], 0.0)
            self.op("vector", f, writes=[t(nmz)])
        for nmz in ("UhiU", "UloU"):
            for bz in range(2):
                vz, tz = self.s2buf[nmz][bz]
                def f(e, vz=vz):
                    return e.memset(vz[0:64, 0:256], 0.0)
                self.op("vector", f, writes=[tz])
        for nmz in ("Uhi", "Ulo"):
            for bz in range(2):
                vz, tz = self.s2buf[nmz][bz]
                def f(e, vz=vz):
                    return e.memset(vz[64:128, 0:256], 0.0)
                self.op("vector", f, writes=[tz])
        blocks = []
        for slab in range((nch + SLABCH - 1) // SLABCH):
            m0 = slab * SLABCH
            nm_ = min(SLABCH, nch - m0)
            for hg in range(H // NG):
                blocks.append((m0, nm_, hg))

        def emit_rounds(rounds):
            for rnd in rounds:
                for th in rnd:
                    th()

        def loads_preps(bi):
            bm0, bnm, bhg = blocks[bi]
            for j in range(NG):
                h = bhg * NG + j
                self.s2_load_slab(h, j, bm0, bnm)
                self.s2_prep(h, j, bm0, bnm, bi % 2)

        loads_preps(0)
        for bi, (m0, nm_, hg) in enumerate(blocks):
            pend = []
            for mi in range(nm_):
                A, Bh, Bt = self.s2_chunk_phases(hg, m0 + mi, mi, nm_,
                                                 (m0 + mi) % 2, bi % 2)
                na, nb = len(A), len(pend)
                for i in range(max(na, nb)):
                    if i < na:
                        for th in A[i]:
                            th()
                    if i < nb:
                        for th in pend[i]:
                            th()
                if mi < nm_ - 1:
                    pend = Bh + Bt
                else:
                    emit_rounds(Bh)
                    if bi + 1 < len(blocks):
                        loads_preps(bi + 1)
                    emit_rounds(Bt)

    def s2_load_slab(self, h, j, m0, nm_):
        d = self.dram
        t0 = m0 * CT
        tw = nm_ * CT
        q2 = "qsync" if h % 2 == 0 else "qpool"
        def sv(nm):
            return self.sl[(nm, j)]
        for nm, src in [("kh", "ksp"), ("vh", "vsph"), ("vl", "vspl")]:
            vv, tt = sv(nm)
            ap = d[src][h * 512:(h + 1) * 512, t0:t0 + tw]
            ap = ap.rearrange("(r d) t -> d r t", d=128)
            out = vv[:, 0:4 * tw].rearrange("d (r t) -> d r t", r=4)
            self.dma(q2, out, ap, writes=[tt], tag=f"sl.{nm}{h}")
        vv, tt = sv("qs")
        self.dma(q2, vv[:, 0:tw], d["qsp"][h * 128:(h + 1) * 128, t0:t0 + tw],
                 writes=[tt], tag=f"sl.q{h}")
        vv, tt = sv("Ls")
        self.dma(q2, vv[:, 0:tw], d["lsp"][h * 128:(h + 1) * 128, t0:t0 + tw],
                 writes=[tt], tag=f"sl.L{h}")
        vv, tt = sv("rows")
        W8 = SLABCH * CT
        self.dma("qsync", vv[0:1, 0:4 * tw].rearrange("o (r t) -> o r t", r=4),
                 d["rowk"][4 * h:4 * h + 4, t0:t0 + tw].unsqueeze(0),
                 writes=[tt], tag=f"sl.rk{h}")
        self.dma("qsync", vv[0:1, 4 * W8:4 * W8 + 4 * tw].rearrange("o (r t) -> o r t", r=4),
                 d["lnb"][4 * h:4 * h + 4, t0:t0 + tw].unsqueeze(0),
                 writes=[tt], tag=f"sl.lb{h}")
        self.dma("qsync", vv[0:1, 8 * W8:8 * W8 + tw],
                 d["rowq"][h:h + 1, t0:t0 + tw], writes=[tt], tag=f"sl.rq{h}")

    def s2_prep(self, h, j, m0, nm_, set_):
        nc = self.nc
        v = lambda n: self.tiles[n][0]
        t = lambda n: self.tiles[n][1]
        tw = nm_ * CT
        mw = 4 * tw
        W8 = SLABCH * CT
        def sv(nm):
            return self.sl[(nm, j)]
        kh, kht = sv("kh"); vhh, vht = sv("vh"); vll, vlt = sv("vl")
        qs, qst = sv("qs"); Ls, Lst = sv("Ls"); Ct, Ctt = sv("Ct")
        G0, G0t = sv("G0"); Ga, Gat = sv("Ga"); rows, rowst = sv("rows")
        LAM, LAMt = self.sl[("LAM", j, set_)]
        KQ, KQt = sv("KQ"); WH, WHt = sv("WH"); KG, KGt = sv("KG")
        WG, WGt = sv("WG"); QB, QBt = self.sl[("QB", j, set_)]
        dx, dxt = sv("dex")
        art, artt = v("arg"), t("arg")
        att, attt = v("att"), t("att")
        at2, at2t = v("att2"), t("att2")
        lf, lft = v("lnfT"), t("lnfT")
        pfT, pfTt = v("PFT"), t("PFT")
        wfT, wfTt = v("wfillT"), t("wfillT")
        onesrowf, onesrowft = self.tiles["onesrowf"]
        pK, pKt = self.tiles["pb5"]
        pW, pWt = self.tiles["pb7"]
        pQ, pQt = self.tiles["pb7"][0][:, 256:512], self.tiles["pb7"][1]
        O = self.op
        rln, rlnt = v("rlnA")[:, 4 * h:4 * h + 4], t("rlnA")
        def f(e):
            return e.tensor_add(rows[0:1, 4 * W8:4 * W8 + 4 * tw],
                                rows[0:1, 4 * W8:4 * W8 + 4 * tw],
                                rows[0:1, 0:4 * tw])
        O("vector", f, reads=[rowst], writes=[rowst])
        def f(e):
            return e.matmul(pK[:, 0:mw], onesrowf[0:1, :], rows[0:1, 0:4 * tw],
                            start=True, stop=True)
        O("tensor", f, reads=[rowst, onesrowft], writes=[pKt])
        def f(e):
            return e.matmul(pW[:, 0:mw], onesrowf[0:1, :],
                            rows[0:1, 4 * W8:4 * W8 + 4 * tw], start=True, stop=True)
        O("tensor", f, reads=[rowst, onesrowft], writes=[pWt])
        def f(e):
            return e.matmul(pQ[:, 0:tw], onesrowf[0:1, :],
                            rows[0:1, 8 * W8:8 * W8 + tw], start=True, stop=True)
        O("tensor", f, reads=[rowst, onesrowft], writes=[pQt])
        def f(e):
            return e.memset(Ct[:, 0:1], 0.0)
        O("vector", f, writes=[Ctt])
        def f(e):
            return e.tensor_scalar(att[:, 64:64 + tw], Ls[:, 0:tw], rln[:, 3:4],
                                   None, OP.add)
        O("vector", f, reads=[Lst], writes=[attt], hard=[rlnt])
        src, srct, dst, dstt = att, attt, at2, at2t
        sh = 1
        while sh < tw:
            def f(e, sh=sh, src=src, dst=dst):
                return e.tensor_add(dst[:, 64:64 + tw], src[:, 64:64 + tw],
                                    src[:, 64 - sh:64 - sh + tw])
            O("vector", f, reads=[srct], writes=[dstt])
            src, srct, dst, dstt = dst, dstt, src, srct
            sh *= 2
        def f(e, src=src):
            return e.tensor_copy(Ct[:, 1:tw + 1], src[:, 64:64 + tw])
        O("vector", f, reads=[srct], writes=[Ctt])
        def f(e):
            return e.tensor_scalar(G0[:, 0:tw], Ct[:, 1:tw + 1], rln[:, 3:4], None,
                                   OP.subtract)
        O("vector", f, reads=[Ctt], writes=[G0t], hard=[rlnt])
        G0r, G0rt = sv("G0r")
        cecv = Ct[:, 0:tw:CT].unsqueeze(2).broadcast_to([128, nm_, CT])
        def f(e, cecv=cecv):
            return e.tensor_sub(G0r[:, 0:tw].rearrange("d (m t) -> d m t", t=CT),
                                G0[:, 0:tw].rearrange("d (m t) -> d m t", t=CT),
                                cecv)
        O("vector", f, reads=[G0t, Ctt], writes=[G0rt])
        ancv = Ct[:, 0:tw:4].unsqueeze(2).broadcast_to([128, tw // 4, 4])
        def f(e, ancv=ancv):
            return e.tensor_sub(Ga[:, 0:tw].rearrange("d (b p) -> d b p", p=4),
                                G0[:, 0:tw].rearrange("d (b p) -> d b p", p=4),
                                ancv)
        O("vector", f, reads=[G0t, Ctt], writes=[Gat])
        def f(e):
            return e.tensor_sub(LAM[:, 8:8 + nm_], Ct[:, CT:tw + 1:CT],
                                Ct[:, 0:tw:CT])
        O("vector", f, reads=[Ctt], writes=[LAMt])
        def f(e):
            return e.tensor_scalar(LAM[:, 0:nm_], LAM[:, 8:8 + nm_],
                                   float(-87.0), None, OP.max)
        O("vector", f, reads=[LAMt], writes=[LAMt])
        def f(e):
            return e.activation(LAM[:, 0:nm_], LAM[:, 0:nm_], AF.Exp)
        O("scalar", f, reads=[LAMt], writes=[LAMt])
        a4 = Ct[:, 0:tw:4].rearrange("d (m I) -> d m I", I=4)
        minu = a4.unsqueeze(3).broadcast_to([128, nm_, 4, 4])
        subt = a4.unsqueeze(2).broadcast_to([128, nm_, 4, 4])
        dxv = dx[:, 0:16 * nm_].rearrange("d (m I i) -> d m I i", I=4, i=4)
        def f(e, dxv=dxv, minu=minu, subt=subt):
            return e.tensor_sub(dxv, minu, subt)
        O("vector", f, reads=[Ctt], writes=[dxt], tag="dexp.sub")
        def f(e):
            return e.tensor_scalar(dx[:, 0:16 * nm_], dx[:, 0:16 * nm_],
                                   float(-87.0), float(0.0), OP.max, OP.min)
        O("vector", f, reads=[dxt], writes=[dxt])
        def f(e):
            return e.activation(dx[:, 0:16 * nm_], dx[:, 0:16 * nm_], AF.Exp)
        O("scalar", f, reads=[dxt], writes=[dxt], tag="dexp")
        def rmaj(x):
            return x[:, 0:mw].rearrange("d (r m I p) -> d m I r p",
                                        r=4, m=nm_, I=4)
        def bmaj(x):
            return x[:, 0:mw].rearrange("d (m I r p) -> d m I r p",
                                        m=nm_, I=4, r=4)
        artv = art[:, 0:mw].rearrange("d (r t) -> d r t", r=4)
        pKv = pK[:, 0:mw].rearrange("d (r t) -> d r t", r=4)
        pWv = pW[:, 0:mw].rearrange("d (r t) -> d r t", r=4)
        rlnb = rln[:, 0:4].unsqueeze(2).broadcast_to([128, 4, tw])
        Gab = Ga[:, 0:tw].unsqueeze(1).broadcast_to([128, 4, tw])
        G0rb = G0r[:, 0:tw].unsqueeze(1).broadcast_to([128, 4, tw])
        def f(e):
            return e.tensor_tensor(artv, pKv, rlnb, OP.add)
        O("vector", f, reads=[pKt, rlnt], writes=[artt])
        def f(e):
            return e.tensor_tensor(artv, artv, Gab, OP.add)
        O("vector", f, reads=[artt, Gat], writes=[artt])
        def f(e):
            return e.activation(art[:, 0:mw], art[:, 0:mw], AF.Exp)
        O("scalar", f, reads=[artt], writes=[artt])
        kqK = KQ[:, 0:2 * mw].rearrange("d (m I h r p) -> d h m I r p",
                                        I=4, h=2, r=4, p=4)[:, 0]
        def f(e, kqK=kqK):
            return e.tensor_tensor(kqK, rmaj(kh), rmaj(art), OP.mult)
        O("vector", f, reads=[kht, artt], writes=[KQt], tag="KH")
        def f(e):
            return e.tensor_tensor(artv, pKv, rlnb, OP.add)
        O("vector", f, reads=[pKt, rlnt], writes=[artt])
        def f(e):
            return e.tensor_tensor(artv, artv, G0rb, OP.add)
        O("vector", f, reads=[artt, G0rt], writes=[artt])
        def f(e):
            return e.tensor_scalar(art[:, 0:mw], art[:, 0:mw], float(-87.0),
                                   None, OP.max)
        O("vector", f, reads=[artt], writes=[artt])
        def f(e):
            return e.activation(art[:, 0:mw], art[:, 0:mw], AF.Exp)
        O("scalar", f, reads=[artt], writes=[artt])
        def f(e):
            return e.tensor_scalar(art[:, 0:mw], art[:, 0:mw], -1.0, None, OP.mult)
        O("vector", f, reads=[artt], writes=[artt])
        def f(e):
            return e.tensor_tensor(bmaj(KG), rmaj(kh), rmaj(art), OP.mult)
        O("vector", f, reads=[kht, artt], writes=[KGt], tag="KG")
        def f(e):
            return e.tensor_tensor(artv, pWv, rlnb, OP.subtract)
        O("vector", f, reads=[pWt, rlnt], writes=[artt])
        def f(e):
            return e.tensor_tensor(artv, artv, Gab, OP.subtract)
        O("vector", f, reads=[artt, Gat], writes=[artt])
        def f(e):
            return e.activation(art[:, 0:mw], art[:, 0:mw], AF.Exp)
        O("scalar", f, reads=[artt], writes=[artt])
        def f(e):
            return e.tensor_tensor(bmaj(WH), rmaj(kh), rmaj(art), OP.mult)
        O("vector", f, reads=[kht, artt], writes=[WHt], tag="WH")
        gev = LAM[:, 8:8 + nm_].unsqueeze(2).broadcast_to([128, nm_, CT])
        def f(e, gev=gev):
            return e.tensor_sub(att[:, 64:64 + tw].rearrange("d (m t) -> d m t", t=CT),
                                gev,
                                G0r[:, 0:tw].rearrange("d (m t) -> d m t", t=CT))
        O("vector", f, reads=[LAMt, G0rt], writes=[attt])
        attb = att[:, 64:64 + tw].unsqueeze(1).broadcast_to([128, 4, tw])
        def f(e):
            return e.tensor_tensor(artv, pWv, rlnb, OP.subtract)
        O("vector", f, reads=[pWt, rlnt], writes=[artt])
        def f(e):
            return e.tensor_tensor(artv, artv, attb, OP.add)
        O("vector", f, reads=[artt, attt], writes=[artt])
        def f(e):
            return e.tensor_scalar(art[:, 0:mw], art[:, 0:mw], float(-87.0),
                                   None, OP.max)
        O("vector", f, reads=[artt], writes=[artt])
        def f(e):
            return e.activation(art[:, 0:mw], art[:, 0:mw], AF.Exp)
        O("scalar", f, reads=[artt], writes=[artt])
        def f(e):
            return e.tensor_tensor(bmaj(WG), rmaj(kh), rmaj(art), OP.mult)
        O("vector", f, reads=[kht, artt], writes=[WGt], tag="WG")
        vhb, vhbt = sv("vhb")
        vlb, vlbt = sv("vlb")
        def f(e):
            return e.activation(bmaj(vhb), rmaj(vhh), AF.Copy)
        O("scalar", f, reads=[vht], writes=[vhbt])
        def f(e):
            return e.activation(bmaj(vlb), rmaj(vll), AF.Copy)
        O("scalar", f, reads=[vlt], writes=[vlbt])
        def f(e):
            return e.tensor_add(att[:, 64:64 + tw], Ga[:, 0:tw], pQ[:, 0:tw])
        O("vector", f, reads=[Gat, pQt], writes=[attt])
        def f(e):
            return e.activation(att[:, 64:64 + tw], att[:, 64:64 + tw], AF.Exp)
        O("scalar", f, reads=[attt], writes=[attt])
        def f(e):
            return e.tensor_mul(at2[:, 64:64 + tw], qs[:, 0:tw], att[:, 64:64 + tw])
        O("vector", f, reads=[qst, attt], writes=[at2t])
        qlv = KQ[:, 0:2 * mw].rearrange("d (m I h r p) -> d h r (m I) p",
                                        I=4, h=2, r=4, p=4)[:, 1]
        qbv = at2[:, 64:64 + tw].rearrange("d (mI p) -> d mI p", p=4)\
            .unsqueeze(1).broadcast_to([128, 4, tw // 4, 4])
        wfv = wfT[:, h:h + 49:16].unsqueeze(2).unsqueeze(3)\
            .broadcast_to([128, 4, tw // 4, 4])
        def f(e, qlv=qlv, qbv=qbv, wfv=wfv):
            return e.tensor_tensor(qlv, qbv, wfv, OP.mult)
        O("vector", f, reads=[at2t, wfTt], writes=[KQt])
        def f(e):
            return e.tensor_add(att[:, 64:64 + tw], G0r[:, 0:tw], pQ[:, 0:tw])
        O("vector", f, reads=[G0rt, pQt], writes=[attt])
        def f(e):
            return e.tensor_scalar(att[:, 64:64 + tw], att[:, 64:64 + tw],
                                   float(-87.0), None, OP.max)
        O("vector", f, reads=[attt], writes=[attt])
        def f(e):
            return e.activation(att[:, 64:64 + tw], att[:, 64:64 + tw], AF.Exp)
        O("scalar", f, reads=[attt], writes=[attt])
        def f(e):
            return e.tensor_mul(at2[:, 64:64 + tw], qs[:, 0:tw], att[:, 64:64 + tw])
        O("vector", f, reads=[qst, attt], writes=[at2t])
        def f(e):
            return e.tensor_scalar(QB[:, 0:tw], at2[:, 64:64 + tw], pfT[:, h:h + 1],
                                   None, OP.mult)
        O("vector", f, reads=[at2t], writes=[QBt], hard=[pfTt], tag="QB")

    def s2_chunk_phases(self, hg, m, mi, nm_, buf, set_):
        nc = self.nc
        v = lambda n: self.tiles[n][0]
        t = lambda n: self.tiles[n][1]
        warmch = self.warm // CT
        O = self.op
        idsb, idsbt = self.tiles["idsb"]
        Wv, Wvt = self.s2buf["Wv"][buf]
        Nd, Ndt = self.s2buf["Nd"][buf]
        Ngap, Ngapt = self.s2buf["Ngap"][buf]
        Agap, Agapt = self.s2buf["Agap"][buf]
        pNM, pNMt = self.tiles["pb0"]
        pA, pAt = self.tiles["pb1"]
        pT, pTt = self.tiles["pb2"]
        pR, pRt = self.tiles["pb3"]
        pH, pHt = self.tiles["pb4"]
        pS, pSt = self.tiles["pb5"]
        pO = self.tiles["pb6"][0][:, 64 * set_:64 * set_ + 64]
        pOt = self.poT[set_]
        Yp, Ypt = self.s2buf["Yp"][buf]
        Uhi, Uhit = self.s2buf["Uhi"][buf]
        Ulo, Ulot = self.s2buf["Ulo"][buf]
        WgT, WgTt = self.s2buf["WgT"][buf]
        ApT, ApTt = self.s2buf["ApT"][buf]
        mM, mMt = self.tiles["mM"]
        mA, mAt = self.tiles["mA"]
        idg, idgt = self.tiles["idg"]
        ppool, ppoolt = self.tiles["ppool"]
        Sf, Sft = self.tiles["Sf"][0], self.sfT[hg]
        Sb, Sbt = self.tiles["Sb"][0], self.sbT[hg]
        ob, obt = self.tiles["obuf"]
        c0 = mi * CMICRO
        JJ = range(4)
        TCOL = {1: 0, 2: 256, 4: 512, 8: 768, 16: 1024, 32: 1280}
        NCOL = (1536, 1792)
        def sj(nm, j):
            if nm in ("QB", "LAM"):
                return self.sl[(nm, j, set_)]
            return self.sl[(nm, j)]
        A = []
        B = []
        def ra():
            A.append([])
            return A[-1].append
        def rb():
            B.append([])
            return B[-1].append
        r_ = ra()
        for j in JJ:
            WH, WHt_ = sj("WH", j)
            dex, dext = sj("dex", j)
            dxv = dex[:, 16 * mi:16 * mi + 16].rearrange(
                "d (I i) -> d I i", i=4).unsqueeze(3).broadcast_to([128, 4, 4, 16])
            wvv = Wv[:, 256 * j:256 * (j + 1)].rearrange(
                "d (I i c) -> d I i c", I=4, i=4)
            whv2 = WH[:, c0:c0 + CMICRO].rearrange(
                "d (i c) -> d i c", i=4).unsqueeze(1).broadcast_to([128, 4, 4, 16])
            def f(e, wvv=wvv, whv2=whv2, dxv=dxv):
                return e.tensor_tensor(wvv, whv2, dxv, OP.mult)
            r_(lambda f=f, WHt_=WHt_, dext=dext: O("vector", f, reads=[WHt_, dext],
                                                   writes=[Wvt], tag="Wv"))
        r_ = ra()
        for j in JJ:
            KQ, KQt_ = sj("KQ", j)
            for I in range(4):
                def f(e, j=j, I=I, KQ=KQ):
                    return e.matmul(pNM[32 * I:32 * I + 32, 64 * j:64 * j + 64],
                                    KQ[:, 128 * mi + 32 * I:128 * mi + 32 * I + 32],
                                    Wv[:, 256 * j + 64 * I:256 * j + 64 * I + 64],
                                    start=True, stop=True, tile_position=(0, 32 * I))
                r_(lambda f=f, KQt_=KQt_: O("tensor", f, reads=[KQt_, Wvt],
                                            writes=[pNMt]))
        r_ = ra()
        mMv = mM[:, 0:CMICRO].unsqueeze(1).broadcast_to([128, 4, CMICRO])
        mAv = mA[:, 0:CMICRO].unsqueeze(1).broadcast_to([128, 4, CMICRO])
        def f(e):
            return e.tensor_tensor(Ngap[:, 0:256].rearrange("d (j c) -> d j c", j=4),
                                   pNM[:, 0:256].rearrange("d (j c) -> d j c", j=4),
                                   mMv, OP.mult)
        r_(lambda f=f: O("vector", f, reads=[pNMt, mMt], writes=[Ngapt]))
        def f(e):
            return e.tensor_tensor(Agap[:, 0:256].rearrange("d (j c) -> d j c", j=4),
                                   pNM[:, 0:256].rearrange("d (j c) -> d j c", j=4),
                                   mAv, OP.mult)
        r_(lambda f=f: O("vector", f, reads=[pNMt, mAt], writes=[Agapt]))
        r_ = ra()
        for h in JJ:
            p, q = h // 2, h % 2
            WG, WGt_ = sj("WG", h)
            def f(e, p=p, q=q, WG=WG):
                return e.matmul(pA[64 * q:64 * q + 64, 128 * p:128 * p + 128],
                                WG[:, c0:c0 + CMICRO], idsb[:, :],
                                start=True, stop=True, tile_position=(0, 64 * q))
            r_(lambda f=f, WGt_=WGt_: O("tensor", f, reads=[WGt_, idsbt],
                                        writes=[pAt]))
        if m >= warmch:
            for h in JJ:
                p, q = h // 2, h % 2
                def f(e, h=h, p=p, q=q):
                    return e.matmul(pA[64 * q:64 * q + 64,
                                       256 + 32 * p + 16 * q:256 + 32 * p + 16 * q + 16],
                                    Agap[:, 64 * h:64 * h + 64], ppool[:, :],
                                    start=True, stop=True, tile_position=(0, 64 * q))
                r_(lambda f=f: O("tensor", f, reads=[Agapt, ppoolt], writes=[pAt]))
        r_ = ra()
        def f(e):
            return e.activation(WgT[:, 0:256], pA[:, 0:256], AF.Copy)
        r_(lambda f=f: O("scalar", f, reads=[pAt], writes=[WgTt]))
        if m >= warmch:
            def f(e):
                return e.activation(ApT[:, 0:64], pA[:, 256:320], AF.Copy)
            r_(lambda f=f: O("scalar", f, reads=[pAt], writes=[ApTt]))
        r_ = ra()
        for h in JJ:
            p, q = h // 2, h % 2
            def f(e, h=h, p=p, q=q):
                return e.matmul(pT[64 * q:64 * q + 64,
                                   128 * p + 64 * q:128 * p + 64 * q + 64],
                                idg[:, 0:64], Ngap[:, 64 * h:64 * h + 64],
                                start=True, stop=True, tile_position=(0, 64 * q))
            r_(lambda f=f: O("tensor", f, reads=[idgt, Ngapt], writes=[pTt]))
            def f(e, h=h, p=p, q=q):
                return e.matmul(pT[64 * q:64 * q + 64,
                                   256 + 128 * p + 64 * q:256 + 128 * p + 64 * q + 64],
                                Ngap[:, 64 * h:64 * h + 64], idg[:, 0:64],
                                start=True, stop=True, tile_position=(0, 64 * q))
            r_(lambda f=f: O("tensor", f, reads=[idgt, Ngapt], writes=[pTt]))
        r_ = ra()
        def f(e):
            return e.activation(Nd[:, NCOL[0]:NCOL[0] + 256], pT[:, 0:256], AF.Copy)
        r_(lambda f=f: O("scalar", f, reads=[pTt], writes=[Ndt]))
        def f(e):
            return e.activation(Nd[:, 0:256], pT[:, 256:512], AF.Copy)
        r_(lambda f=f: O("scalar", f, reads=[pTt], writes=[Ndt]))
        cn = 0
        for lv in (1, 2, 4, 8):
            r_ = ra()
            tc, nc_ = TCOL[lv], NCOL[cn]
            tc2, nc2 = TCOL[2 * lv], NCOL[1 - cn]
            for p in range(2):
                def f(e, p=p, tc=tc, nc_=nc_):
                    return e.matmul(pT[:, 128 * p:128 * p + 128],
                                    Nd[:, tc + 128 * p:tc + 128 * p + 128],
                                    Nd[:, nc_ + 128 * p:nc_ + 128 * p + 128],
                                    start=True, stop=True)
                r_(lambda f=f: O("tensor", f, reads=[Ndt], writes=[pTt]))
                def f(e, p=p, tc=tc, nc_=nc_):
                    return e.matmul(pT[:, 256 + 128 * p:256 + 128 * p + 128],
                                    Nd[:, nc_ + 128 * p:nc_ + 128 * p + 128],
                                    Nd[:, tc + 128 * p:tc + 128 * p + 128],
                                    start=True, stop=True)
                r_(lambda f=f: O("tensor", f, reads=[Ndt], writes=[pTt]))
            r_ = ra()
            def f(e, nc2=nc2):
                return e.activation(Nd[:, nc2:nc2 + 256], pT[:, 0:256], AF.Copy)
            r_(lambda f=f: O("scalar", f, reads=[pTt], writes=[Ndt]))
            def f(e, tc2=tc2):
                return e.activation(Nd[:, tc2:tc2 + 256], pT[:, 256:512], AF.Copy)
            r_(lambda f=f: O("scalar", f, reads=[pTt], writes=[Ndt]))
            cn = 1 - cn
        r_ = ra()
        for p in range(2):
            def f(e, p=p, nc_=NCOL[cn], tc=TCOL[16]):
                return e.matmul(pT[:, 256 + 128 * p:256 + 128 * p + 128],
                                Nd[:, nc_ + 128 * p:nc_ + 128 * p + 128],
                                Nd[:, tc + 128 * p:tc + 128 * p + 128],
                                start=True, stop=True)
            r_(lambda f=f: O("tensor", f, reads=[Ndt], writes=[pTt]))
        r_ = ra()
        def f(e):
            return e.activation(Nd[:, TCOL[32]:TCOL[32] + 256], pT[:, 256:512], AF.Copy)
        r_(lambda f=f: O("scalar", f, reads=[pTt], writes=[Ndt], tag="N32T"))
        r_ = rb()
        for h in JJ:
            p, q = h // 2, h % 2
            KG, KGt_ = sj("KG", h)
            vhb, vhbt_ = sj("vhb", h)
            vlb, vlbt_ = sj("vlb", h)
            hh = 4 * hg + h
            def f(e, p=p, q=q, KG=KG, hh=hh):
                return e.matmul(pR[64 * q:64 * q + 64, 128 * p:128 * p + 128],
                                KG[:, c0:c0 + CMICRO],
                                Sb[:, hh * 128:(hh + 1) * 128],
                                start=True, stop=False, tile_position=(0, 64 * q))
            r_(lambda f=f, KGt_=KGt_: O("tensor", f, reads=[KGt_, Sbt], writes=[pRt]))
            def f(e, p=p, q=q, vhb=vhb):
                return e.matmul(pR[64 * q:64 * q + 64, 128 * p:128 * p + 128],
                                vhb[:, c0:c0 + CMICRO], idsb[:, :],
                                start=False, stop=False, tile_position=(0, 64 * q))
            r_(lambda f=f, vhbt_=vhbt_: O("tensor", f, reads=[vhbt_, idsbt],
                                          writes=[pRt]))
            def f(e, p=p, q=q, vlb=vlb):
                return e.matmul(pR[64 * q:64 * q + 64, 128 * p:128 * p + 128],
                                vlb[:, c0:c0 + CMICRO], idsb[:, :],
                                start=False, stop=True, tile_position=(0, 64 * q))
            r_(lambda f=f, vlbt_=vlbt_: O("tensor", f, reads=[vlbt_, idsbt],
                                          writes=[pRt]))
        if m >= warmch:
            for h in JJ:
                hh = 4 * hg + h
                QB, QBt_ = sj("QB", h)
                def f(e, h=h, hh=hh, QB=QB):
                    return e.matmul(pO[:, 16 * h:16 * h + 16],
                                    Sb[:, hh * 128:(hh + 1) * 128],
                                    QB[:, mi * CT:(mi + 1) * CT],
                                    start=(h == 0), stop=False)
                r_(lambda f=f, QBt_=QBt_: O("tensor", f, reads=[Sbt, QBt_],
                                            writes=[pOt]))
        r_ = rb()
        def f(e):
            return e.activation(Yp[:, 0:256], pR[:, 0:256], AF.Copy)
        r_(lambda f=f: O("scalar", f, reads=[pRt], writes=[Ypt], tag="Y0"))
        cur = 0
        for lv in (32, 16, 8, 4, 2):
            r_ = rb()
            tc = TCOL[lv]
            for p in range(2):
                def f(e, p=p, tc=tc, cur=cur):
                    return e.matmul(pH[:, 128 * p:128 * p + 128],
                                    Nd[:, tc + 128 * p:tc + 128 * p + 128],
                                    Yp[:, 256 * cur + 128 * p:256 * cur + 128 * p + 128],
                                    start=True, stop=False)
                r_(lambda f=f: O("tensor", f, reads=[Ndt, Ypt], writes=[pHt]))
                def f(e, p=p, cur=cur):
                    return e.matmul(pH[:, 128 * p:128 * p + 128], idsb[:, :],
                                    Yp[:, 256 * cur + 128 * p:256 * cur + 128 * p + 128],
                                    start=False, stop=True)
                r_(lambda f=f: O("tensor", f, reads=[idsbt, Ypt], writes=[pHt]))
            def f(e, cur=cur):
                return e.activation(Yp[:, 256 * (1 - cur):256 * (1 - cur) + 256],
                                    pH[:, 0:256], AF.Copy)
            r_(lambda f=f: O("scalar", f, reads=[pHt], writes=[Ypt]))
            cur = 1 - cur
        r_ = rb()
        for p in range(2):
            def f(e, p=p, cur=cur):
                return e.matmul(pH[:, 128 * p:128 * p + 128],
                                Nd[:, 128 * p:128 * p + 128],
                                Yp[:, 256 * cur + 128 * p:256 * cur + 128 * p + 128],
                                start=True, stop=False)
            r_(lambda f=f: O("tensor", f, reads=[Ndt, Ypt], writes=[pHt]))
            def f(e, p=p, cur=cur):
                return e.matmul(pH[:, 128 * p:128 * p + 128], idsb[:, :],
                                Yp[:, 256 * cur + 128 * p:256 * cur + 128 * p + 128],
                                start=False, stop=True)
            r_(lambda f=f: O("tensor", f, reads=[idsbt, Ypt], writes=[pHt]))
        r_ = rb()
        UhiU, UhiUt = self.s2buf["UhiU"][buf]
        UloU, UloUt = self.s2buf["UloU"][buf]
        def f(e):
            return e.activation(Uhi[0:64, 0:256], pH[0:64, 0:256], AF.Copy)
        r_(lambda f=f: O("scalar", f, reads=[pHt], writes=[Uhit]))
        def f(e):
            return e.activation(UhiU[64:128, 0:256], pH[64:128, 0:256], AF.Copy)
        r_(lambda f=f: O("scalar", f, reads=[pHt], writes=[UhiUt]))
        def f(e):
            return e.scalar_tensor_tensor(Ulo[0:64, 0:256], Uhi[0:64, 0:256], -1.0,
                                          pH[0:64, 0:256], OP.mult, OP.add)
        r_(lambda f=f: O("vector", f, reads=[Uhit, pHt], writes=[Ulot]))
        def f(e):
            return e.scalar_tensor_tensor(UloU[64:128, 0:256], UhiU[64:128, 0:256],
                                          -1.0, pH[64:128, 0:256], OP.mult, OP.add)
        r_(lambda f=f: O("vector", f, reads=[UhiUt, pHt], writes=[UloUt]))
        r_ = rb()
        if m >= warmch:
            for p in range(2):
                for ut, utt in ((Uhi, Uhit), (UhiU, UhiUt), (Ulo, Ulot)):
                    def f(e, p=p, ut=ut):
                        return e.matmul(pO[:, 32 * p:32 * p + 32],
                                        ut[:, 128 * p:128 * p + 128],
                                        ApT[:, 32 * p:32 * p + 32],
                                        start=False, stop=False)
                    r_(lambda f=f, utt=utt: O("tensor", f, reads=[utt, ApTt],
                                              writes=[pOt]))
                def f(e, p=p):
                    return e.matmul(pO[:, 32 * p:32 * p + 32],
                                    UloU[:, 128 * p:128 * p + 128],
                                    ApT[:, 32 * p:32 * p + 32],
                                    start=False, stop=(p == 1))
                r_(lambda f=f: O("tensor", f, reads=[UloUt, ApTt], writes=[pOt]))
        for h in JJ:
            p, q = h // 2, h % 2
            uh, uht = (Uhi, Uhit) if q == 0 else (UhiU, UhiUt)
            ul, ult = (Ulo, Ulot) if q == 0 else (UloU, UloUt)
            def f(e, h=h, p=p, uh=uh):
                return e.matmul(pS[:, 128 * h:128 * h + 128],
                                WgT[:, 128 * p:128 * p + 128],
                                uh[:, 128 * p:128 * p + 128],
                                start=True, stop=False)
            r_(lambda f=f, uht=uht: O("tensor", f, reads=[WgTt, uht], writes=[pSt]))
            def f(e, h=h, p=p, ul=ul):
                return e.matmul(pS[:, 128 * h:128 * h + 128],
                                WgT[:, 128 * p:128 * p + 128],
                                ul[:, 128 * p:128 * p + 128],
                                start=False, stop=True)
            r_(lambda f=f, ult=ult: O("tensor", f, reads=[WgTt, ult], writes=[pSt]))
        r_ = rb()
        if m >= warmch:
            toff = (m - warmch) * CT
            obv = ob[:, 4 * hg * self.T_SL:(4 * hg + 4) * self.T_SL].rearrange(
                "d (j t) -> d j t", t=self.T_SL)[:, :, toff:toff + CT]
            def f(e, obv=obv):
                return e.tensor_copy(obv,
                                     pO[:, 0:64].rearrange("d (j t) -> d j t", j=4))
            r_(lambda f=f: O("vector", f, reads=[pOt], writes=[obt], tag="oev"))
        h0 = 4 * hg
        for h in JJ:
            LAM, LAMt_ = sj("LAM", h)
            hh = h0 + h
            def f(e, h=h, hh=hh, LAM=LAM):
                return e.scalar_tensor_tensor(Sf[:, hh * 128:(hh + 1) * 128],
                                              Sf[:, hh * 128:(hh + 1) * 128],
                                              LAM[:, mi:mi + 1],
                                              pS[:, 128 * h:128 * h + 128],
                                              OP.mult, OP.add)
            r_(lambda f=f, LAMt_=LAMt_: O("vector", f, reads=[Sft, pSt],
                                          writes=[Sft], hard=[LAMt_], tag="Sup"))
        def f(e, h0=h0):
            return e.activation(Sb[:, h0 * 128:(h0 + 4) * 128],
                                Sf[:, h0 * 128:(h0 + 4) * 128], AF.Copy)
        r_(lambda f=f: O("scalar", f, reads=[Sft], writes=[Sbt]))
        return A, B[:1], B[1:]

    def stage3(self):
        nc, d = self.nc, self.dram
        T_SL, TX = self.T_SL, self.TX
        xn, xnt = self.tiles["xn"]
        v = lambda n: self.tiles[n][0]
        t = lambda n: self.tiles[n][1]
        ob, obt = self.tiles["obuf"]
        sq, sqt = self.tiles["sq"]
        rt, rtt = self.tiles["rtmp"]
        rb, rbt = self.tiles["rowbf"]
        ft, ftt = v("ftmp"), t("ftmp")
        ft2, ft2t = v("ftmp2"), t("ftmp2")
        kt, ktt = v("ktile"), t("ktile")
        xch, xcht = v("xchunk"), t("xchunk")
        wtf, wtft = v("wtf"), t("wtf")
        wtb, wtbt = v("wtb"), t("wtb")
        onesrow, onesrowt = self.tiles["onesrow"]
        onescolb, onescolbt = self.tiles["onescolb"]
        pss, psst = self.tiles["pss"]
        pbc, pbct = self.tiles["pbc"]
        O = self.op
        self.wtb3T = [Tile("wtb3a"), Tile("wtb3b")]
        def f(e):
            return e.memset(ft[0:1, 0:1], 0.0)
        O("vector", f, writes=[wtft, wtbt] + self.wtb3T + self.s2b1t + [ftt],
          tag="fenceB")
        onwc, onwct = self.sbuf("onwc", [128, 1], F32)
        self.dma("qsync", onwc[:, 0:1], d["onw"], writes=[onwct])
        ffscT, ffscTt = self.sbuf("ffscT", [128, 16], F32)
        ffbT, ffbTt = self.sbuf("ffbT", [128, 16], F32)
        self.dma("qsync", ffscT[:, 0:16],
                 d["ffsc"].rearrange("(i p) o -> p (i o)", p=128),
                 writes=[ffscTt], slow=True)
        self.dma("qsync", ffbT[:, 0:16],
                 d["ffb"].rearrange("(i p) o -> p (i o)", p=128),
                 writes=[ffbTt], slow=True)
        for h in range(H):
            for c0 in range(0, T_SL, 512):
                cw = min(512, T_SL - c0)
                oc0 = h * T_SL + c0
                def f(e, oc0=oc0, cw=cw):
                    return e.tensor_mul(sq[:, 0:cw], ob[:, oc0:oc0 + cw],
                                        ob[:, oc0:oc0 + cw])
                O("vector", f, reads=[obt], writes=[sqt])
                def f(e, cw=cw):
                    return e.matmul(pss[0:1, 0:cw], onescolb[:, 0:1], sq[:, 0:cw],
                                    start=True, stop=True)
                O("tensor", f, reads=[sqt, onescolbt], writes=[psst])
                def f(e, cw=cw):
                    return e.activation(rt[0:1, 0:cw], pss[0:1, 0:cw], AF.Ln,
                                        scale=np.float32(1.0 / DV),
                                        bias=np.float32(EPS))
                O("scalar", f, reads=[psst], writes=[rtt])
                def f(e, cw=cw):
                    return e.activation(rt[0:1, 512:512 + cw], rt[0:1, 0:cw],
                                        AF.Exp, scale=np.float32(-0.5))
                O("scalar", f, reads=[rtt], writes=[rtt])
                def f(e, cw=cw):
                    return e.tensor_copy(rb[0:1, 0:cw], rt[0:1, 512:512 + cw])
                O("vector", f, reads=[rtt], writes=[rbt])
                def f(e, cw=cw):
                    return e.matmul(pbc[:, 0:cw], onesrow[0:1, :], rb[0:1, 0:cw],
                                    start=True, stop=True)
                O("tensor", f, reads=[rbt, onesrowt], writes=[pbct])
                self.dma("qact", kt[:, 0:cw],
                         d["gsp"][h * 128:(h + 1) * 128, c0:c0 + cw],
                         writes=[ktt], tag="g.ld")
                def f(e, cw=cw):
                    return e.activation(ft[:, 0:cw], kt[:, 0:cw], AF.Sigmoid)
                O("scalar", f, reads=[ktt], writes=[ftt])
                def f(e, oc0=oc0, cw=cw):
                    return e.scalar_tensor_tensor(ft2[:, 0:cw], ob[:, oc0:oc0 + cw],
                                                  onwc[:, 0:1], pbc[:, 0:cw],
                                                  OP.mult, OP.mult)
                O("vector", f, reads=[obt, pbct], writes=[ft2t], hard=[onwct])
                def f(e, oc0=oc0, cw=cw):
                    return e.tensor_tensor(ob[:, oc0:oc0 + cw], ft2[:, 0:cw],
                                           ft[:, 0:cw], OP.mult)
                O("vector", f, reads=[ft2t, ftt], writes=[obt], tag="onrm")
        pj0, pj0t = self.tiles["pj0"]
        pj1, pj1t = self.tiles["pj1"]
        W3RR = ("qsync", "qpool", "qact")
        for oc in range(16):
            for wi, wn in enumerate(("wop", "wff")):
                src = d[wn].rearrange("(i p) o -> p i o", p=128)[:, :, oc * 128:(oc + 1) * 128]
                self.dma(W3RR[(2 * oc + wi) % 3],
                         wtf[:, wi * 2048:wi * 2048 + 2048].rearrange(
                             "p (i o) -> p i o", i=16),
                         src, writes=[wtft], tag=f"w3.{wn}{oc}")
                def f(e, wi=wi, oc=oc):
                    return e.tensor_copy(
                        wtb[:, (oc % 2) * 4096 + wi * 2048:
                            (oc % 2) * 4096 + wi * 2048 + 2048],
                        wtf[:, wi * 2048:wi * 2048 + 2048])
                O("gpsimd", f, reads=[wtft], writes=[self.wtb3T[oc % 2]])
            for c0 in range(0, T_SL, 512):
                cw = min(512, T_SL - c0)
                for ic in range(NIC):
                    def f(e, ic=ic, c0=c0, cw=cw, oc=oc):
                        return e.matmul(pj0[:, 0:cw],
                                        wtb[:, (oc % 2) * 4096 + ic * 128:
                                            (oc % 2) * 4096 + (ic + 1) * 128],
                                        ob[:, ic * T_SL + c0:ic * T_SL + c0 + cw],
                                        start=(ic == 0), stop=(ic == NIC - 1))
                    O("tensor", f, reads=[self.wtb3T[oc % 2], obt],
                      writes=[pj0t], tag="mmop")
                xc0 = (self.TX - self.T_SL) + c0
                for ic in range(NIC):
                    def f(e, ic=ic, c0=c0, cw=cw, xc0=xc0, oc=oc):
                        return e.matmul(pj1[:, 0:cw],
                                        wtb[:, (oc % 2) * 4096 + 2048 + ic * 128:
                                            (oc % 2) * 4096 + 2048 + (ic + 1) * 128],
                                        xn[:, ic * self.TX + xc0:ic * self.TX + xc0 + cw],
                                        start=(ic == 0), stop=(ic == NIC - 1))
                    O("tensor", f, reads=[self.wtb3T[oc % 2], xnt],
                      writes=[pj1t], tag="mmff")
                def f(e, cw=cw, oc=oc):
                    return e.activation(ft[:, 0:cw], pj1[:, 0:cw], AF.Silu,
                                        scale=ffscT[:, oc:oc + 1],
                                        bias=ffbT[:, oc:oc + 1])
                O("scalar", f, reads=[pj1t], writes=[ftt], hard=[ffscTt, ffbTt],
                  tag="ffsilu")
                self.dma("qsync", xch[:, 0:cw],
                         d["x_t"][oc * 128:(oc + 1) * 128, xc0:xc0 + cw],
                         writes=[xcht], tag="x3.ld")
                def f(e, cw=cw):
                    return e.tensor_tensor(ft2[:, 0:cw], pj0[:, 0:cw], ft[:, 0:cw],
                                           OP.mult)
                O("vector", f, reads=[pj0t, ftt], writes=[ft2t])
                def f(e, cw=cw):
                    return e.tensor_add(ft2[:, 0:cw], ft2[:, 0:cw], xch[:, 0:cw])
                O("vector", f, reads=[ft2t, xcht], writes=[ft2t], tag="res")
                self.dma("qact", d["out"][oc * 128:(oc + 1) * 128, c0:c0 + cw],
                         ft2[:, 0:cw], reads=[ft2t], tag=f"out.{oc}.{c0}")

    def emit_debug_dumps(self):
        for ent in self.dbg_dump:
            if isinstance(ent, tuple):
                name, pmax, cmax = ent
            else:
                name, pmax, cmax = ent, None, None
            v, t = self.tiles[name]
            shp = list(v.shape)
            if pmax:
                shp = [pmax, cmax]
            dt_ = v.dtype if hasattr(v, 'dtype') else F32
            d = self.nc.dram_tensor("dbg_" + name, shp, dt_,
                                    kind="ExternalOutput").ap()
            src = v[:, :] if not pmax else v[0:pmax, 0:cmax]
            self.dma("qpool", d, src, reads=[t], tag="dbg." + name)

    def emit_blocks(self):
        nc, pl = self.nc, self.pl
        esem = {e: nc.semaphore(f"es_{e}").__enter__() for e in ENGINES}
        qsem = {(q, l): nc.semaphore(f"qs_{q}_{l}").__enter__()
                for q in DMAQ for l in range(NLANES)}
        blkcm = nc.Block()
        blk = blkcm.__enter__()
        self.cms.append(blkcm)

        def run_stream(eng_obj, ename):
            qfinal = {}
            for op in pl.streams[ename]:
                if op.dma_queue:
                    key = (op.dma_queue, op.qpos % NLANES)
                    qfinal[key] = 16 * (op.qpos // NLANES + 1)
            seen = {}
            for op in pl.streams[ename]:
                for key, val in sorted(op.waits.items(), key=str):
                    if seen.get(key, -1) >= val:
                        continue
                    seen[key] = val
                    sem = esem[key[1]] if key[0] == "e" else qsem[(key[1], key[2])]
                    eng_obj.wait_ge(sem, val)
                inst = op.emit(eng_obj)
                if op.dma_queue:
                    inst.then_inc(qsem[(op.dma_queue, op.qpos % NLANES)], 16)
                else:
                    if inst is not None:
                        inst.then_inc(esem[ename], 1)
                    else:
                        raise RuntimeError(f"op {op.idx} {op.tag} returned None")
            for key, val in sorted(qfinal.items()):
                eng_obj.wait_ge(qsem[key], val)

        @blk.tensor
        def _(e):
            run_stream(e, "tensor")

        @blk.vector
        def _(e):
            run_stream(e, "vector")

        @blk.scalar
        def _(e):
            run_stream(e, "scalar")

        @blk.gpsimd
        def _(e):
            run_stream(e, "gpsimd")

        @blk.sync
        def _(e):
            run_stream(e, "sync")


def kernel(**inputs):
    n_cores = 8
    T_SL = 1024
    maps = host_prepare(inputs, n_cores, T_SL)
    nc, _ = build_program(T_SL)
    res = run_bass_kernel_spmd(nc, maps, core_ids=list(range(n_cores)))
    parts = [np.asarray(res.results[ci]["out"]) for ci in range(n_cores)]
    full = np.concatenate(parts, axis=1)
    return np.ascontiguousarray(full.T)[None].astype(np.float32)
